# version 04
# baseline (speedup 1.0000x reference)
import contextlib

import numpy as np

import concourse.bass as bass
import concourse.mybir as mybir
from concourse.bass_utils import run_bass_kernel_spmd

F32 = mybir.dt.float32
BF16 = mybir.dt.bfloat16
AF = mybir.ActivationFunctionType
ALU = mybir.AluOpType
AX = mybir.AxisListType

NCORES = 8
D = 1024
S = 8192
SEQ = 4096
NL = 6
HEADS = 16
DH = 64
WIN = 256
NWIN = SEQ // WIN
DI = 2816
NJ = DI // 128
ERR_CLIP = 5.0
GRAD_CLIP = 1.0
TTT_SCALE = 2.0 / 4096.0
LN_EPS = 1e-5
LN8 = float(np.log(8.0))
NEGM = -1.0e30
ENGINES = ("sync", "gpsimd", "vector", "scalar", "tensor")



def _host_consts():
    c = {}
    c["identF"] = np.eye(128, dtype=np.float32)
    c["onesCol"] = np.ones((128, 1), dtype=np.float32)
    c["epsCol"] = np.full((128, 1), LN_EPS, dtype=np.float32)
    c["ln8Col"] = np.full((128, 1), LN8, dtype=np.float32)
    c["onesRow"] = np.ones((1, 128), dtype=np.float32)
    hs = np.zeros((128, 8, HEADS), dtype=np.float32)
    for ch in range(8):
        hs[:64, ch, 2 * ch] = 1.0
        hs[64:, ch, 2 * ch + 1] = 1.0
    c["Hsel"] = hs
    p = np.arange(128)[:, None]
    j = np.arange(384)[None, :]
    c["maskW"] = np.where((j >= p) & (j <= p + 256), 0.0, NEGM).astype(np.float32)
    j2 = np.arange(256)[None, :]
    c["mask0a"] = np.where(j2 <= p, 0.0, NEGM).astype(np.float32)
    c["mask0b"] = np.where(j2 <= p + 128, 0.0, NEGM).astype(np.float32)
    BIG = 3.0e38
    qq = np.arange(256)[None, :]
    va = [qq <= p, (qq >= p - 128) & (qq <= p + 128), qq >= p, qq >= p + 128]
    c["maskTA"] = np.concatenate(
        [np.where(va[0], BIG, 0.0), np.where(va[1], BIG, 0.0)],
        axis=1).astype(np.float32)
    c["maskTB"] = np.concatenate(
        [np.where(va[2], BIG, 0.0), np.where(va[3], BIG, 0.0)],
        axis=1).astype(np.float32)
    hst = np.zeros((HEADS, 8, 128), dtype=np.float32)
    for ch in range(8):
        hst[2 * ch, ch, :64] = 1.0
        hst[2 * ch + 1, ch, 64:] = 1.0
    c["HselT"] = hst
    half = np.arange(0, DH, 2, dtype=np.float64)
    inv_freq = 1.0 / 10000.0 ** (half / DH)
    t = np.arange(SEQ, dtype=np.float64)
    freqs = np.concatenate([t[:, None] * inv_freq[None, :]] * 2, axis=1)
    base_scale = (half + 0.4 * DH) / (1.4 * DH)
    power = (t - SEQ // 2) / (WIN // 2)
    scale = np.concatenate([base_scale[None, :] ** power[:, None]] * 2, axis=1)
    cos, sin = np.cos(freqs), np.sin(freqs)
    sgn = np.ones(DH)
    sgn[: DH // 2] = -1.0
    for nm, tb in (("CQ", cos * scale), ("SQ", sin * scale * sgn[None, :]),
                   ("CK", cos / scale), ("SK", sin / scale * sgn[None, :])):
        tbT = tb.T
        c["rot" + nm] = np.ascontiguousarray(
            np.concatenate([tbT, tbT], axis=0).astype(np.float32))
    return c


def _chunk_cols(v, n):
    return np.ascontiguousarray(v.reshape(n, 128).T.astype(np.float32))


def _host_prep(inp):
    f = lambda a: np.ascontiguousarray(np.asarray(a, dtype=np.float32))
    tr = lambda a: np.ascontiguousarray(np.asarray(a, dtype=np.float32).T)
    sh = dict(_host_consts())
    sh["WkT"] = tr(inp["L0_attn_Wk"])
    sh["WvT"] = tr(inp["L0_attn_Wv"])
    sh["WqT"] = tr(inp["L0_attn_Wq"])
    sh["W0T"] = tr(inp["L0_attn_W0"])
    sh["W1T"] = tr(inp["L0_attn_W1"])
    sh["W1N"] = f(inp["L0_attn_W1"])
    sh["WoutT"] = tr(inp["L0_attn_Wout"])
    sh["Wg3"] = np.ascontiguousarray(np.concatenate(
        [f(inp["L0_attn_Wg_decay"]), f(inp["L0_attn_Wg_lr"]),
         f(inp["L0_attn_Wg_mom"])], axis=0).T)
    sh["bg3"] = np.ascontiguousarray(np.stack(
        [f(inp["L0_attn_bg_decay"]).ravel(), f(inp["L0_attn_bg_lr"]).ravel(),
         f(inp["L0_attn_bg_mom"]).ravel()]).reshape(3, 1))
    for i in range(NL):
        sh[f"Wqkv{i}"] = f(inp["L0_ffn_Wqkv"][i])
        sh[f"Wo{i}"] = f(inp["L0_ffn_Wo"][i])
        w1 = f(inp["L0_ffn_W1"][i])
        sh[f"W1a{i}"] = np.ascontiguousarray(np.pad(w1[:, :2730], ((0, 0), (0, 86))))
        sh[f"W1g{i}"] = np.ascontiguousarray(np.pad(w1[:, 2730:], ((0, 0), (0, 86))))
        sh[f"W2p{i}"] = np.ascontiguousarray(np.pad(f(inp["L0_ffn_W2"][i]),
                                                    ((0, 86), (0, 0))))
        b1 = f(inp["L0_ffn_b1"][i])
        sh[f"b1a{i}"] = _chunk_cols(np.pad(b1[:2730], (0, 86)), NJ)
        sh[f"b1g{i}"] = _chunk_cols(np.pad(b1[2730:], (0, 86)), NJ)
        sh[f"b2c{i}"] = _chunk_cols(f(inp["L0_ffn_b2"][i]), 8)
        sh[f"lng{i}"] = _chunk_cols(f(inp["L0_ffn_ln_g"][i]), 8)
        sh[f"lnb{i}"] = _chunk_cols(f(inp["L0_ffn_ln_b"][i]), 8)
        mq = np.zeros((HEADS, D), dtype=np.float32)
        mk = np.zeros((HEADS, D), dtype=np.float32)
        for h in range(HEADS):
            mq[h, DH * h: DH * h + DH] = f(inp["L0_ffn_q_scale"][i])
            mk[h, DH * h: DH * h + DH] = f(inp["L0_ffn_k_scale"][i])
        sh[f"Mq{i}"] = mq
        sh[f"Mk{i}"] = mk
    x = np.asarray(inp["L0_attn_x"], dtype=np.float32)
    shards = [np.ascontiguousarray(x[2 * c: 2 * c + 2].reshape(S, D))
              for c in range(NCORES)]
    return sh, shards



class Sem:
    __slots__ = ("h", "n")

    def __init__(self, h):
        self.h = h
        self.n = 0


class Prog:
    def __init__(self):
        self.ops = {e: [] for e in ENGINES}
        self.anchor = {e: None for e in ENGINES}
        self.kind = {e: None for e in ENGINES}

    def em(self, e, fn):
        self.ops[e].append(fn)
        self.kind[e] = "plain"


class Ctx:
    pass


def w8(pg, e, sem, val):
    if val > 0:
        def fn(eng, s=sem.h, v=val):
            eng.wait_ge(s, v)
        pg.em(e, fn)
        pg.anchor[e] = ("wait",)
        pg.kind[e] = "wait"


def dma(pg, q, out, in_, sem, **kw):
    def fn(eng, o=out, i=in_, s=sem.h, kw=kw):
        eng.dma_start(out=o, in_=i, **kw).then_inc(s, 16)
    pg.em(q, fn)
    sem.n += 16
    pg.anchor[q] = (sem.h, sem.n)
    pg.kind[q] = "inc"
    return sem.n


def tinc(ctx, pg, e, key=None):
    sem = ctx.tile[key or e]

    def fn(eng, s=sem.h):
        return eng.drain().then_inc(s, 1)
    pg.em(e, fn)
    pg.anchor[e] = (sem.h, sem.n + 1)
    pg.kind[e] = "inc"
    sem.n += 1
    return sem.n


def tincL(ctx, pg, e, key=None):
    sem = ctx.tile[key or e]
    assert pg.kind[e] == "plain", pg.kind[e]
    f = pg.ops[e][-1]

    def wrap(eng, f=f, s=sem.h):
        return f(eng).then_inc(s, 1)
    pg.ops[e][-1] = wrap
    sem.n += 1
    pg.anchor[e] = (sem.h, sem.n)
    pg.kind[e] = "inc"
    return sem.n


def pe_group(ctx, pg, b, fns):
    w8(pg, "tensor", ctx.evS[b], ctx.frees[b])
    for f in fns[:-1]:
        pg.em("tensor", f)
    pg.em("tensor", lambda eng, f=fns[-1], s=ctx.mmS[b].h: f(eng).then_inc(s, 1))
    ctx.mmS[b].n += 1
    ctx.fills[b] += 1
    pg.anchor["tensor"] = (ctx.mmS[b].h, ctx.mmS[b].n)
    pg.kind["tensor"] = "inc"


def consume(ctx, pg, e, b, fns):
    w8(pg, e, ctx.mmS[b], ctx.fills[b])
    for f in fns[:-1]:
        pg.em(e, f)
    pg.em(e, lambda eng, f=fns[-1], s=ctx.evS[b].h: f(eng).then_inc(s, 1))
    ctx.evS[b].n += 1
    ctx.frees[b] += 1
    pg.anchor[e] = (ctx.evS[b].h, ctx.evS[b].n)
    pg.kind[e] = "inc"


def run_phase(ctx, pg):
    p = ctx.phase_idx
    ctx.phase_idx += 1
    blk = ctx.blk
    for name in ENGINES:
        sec = getattr(blk, name)

        anchor = pg.anchor[name]

        @sec
        def _(eng, name=name, p=p, ops=pg.ops[name], anchor=anchor):
            if p > 0:
                for s in ctx.ph.values():
                    eng.wait_ge(s.h, p)
            for fn in ops:
                fn(eng)
            if anchor is not None and anchor[0] != "wait":
                eng.wait_ge(anchor[0], anchor[1])
            eng.sem_inc(ctx.ph[name].h, 1)
    for name in ENGINES:
        ctx.ph[name].n += 1


def spacer(pg, ctx, e="vector"):
    pg.em(e, lambda eng: eng.tensor_scalar(
        ctx.junk[:, 0:256], ctx.junk[:, 0:256], 1.0, None, ALU.mult))


def rsqrt_nr(pg, y, ve, tmp, n=2):
    for _ in range(n):
        pg.em("vector", lambda eng, y=y, t=tmp: eng.tensor_mul(t, y, y))
        pg.em("vector", lambda eng, v=ve, t=tmp: eng.tensor_mul(t, t, v))
        pg.em("vector", lambda eng, t=tmp: eng.tensor_scalar(
            t, t, -0.5, 1.5, ALU.mult, ALU.add))
        pg.em("vector", lambda eng, y=y, t=tmp: eng.tensor_mul(y, y, t))


def fm_slice(t, s0, n):
    return t.ap()[:, :, s0:s0 + n].transpose([1, 0, 2])


def wchunks(dr_name, ctx):
    return ctx.dr[dr_name].ap().rearrange("(c p) n -> p c n", p=128)


def bank_cycler(start=0, n=8):
    i = [start]

    def nb():
        b = i[0] % n
        i[0] += 1
        return b
    return nb



def ph_P0(ctx, pg):
    dr, cst = ctx.dr, ctx.cst
    sG = ctx.dsemf("p0g")
    sS = ctx.dsemf("p0s")
    dma(pg, "gpsimd", cst.identB[:, :], dr["identF"].ap(), sG)
    dma(pg, "gpsimd", cst.onesColB[:, :], dr["onesCol"].ap(), sG)
    dma(pg, "gpsimd", cst.onesRowB[:, :], dr["onesRow"].ap(), sG)
    dma(pg, "gpsimd", cst.HselB[:, :, :], dr["Hsel"].ap(), sG)
    dma(pg, "gpsimd", cst.maskWb[:, :], dr["maskW"].ap(), sG)
    dma(pg, "gpsimd", cst.mask0ab[:, :], dr["mask0a"].ap(), sG)
    dma(pg, "gpsimd", cst.mask0bb[:, :], dr["mask0b"].ap(), sG)
    dma(pg, "gpsimd", cst.maskTAb[:, :], dr["maskTA"].ap(), sG)
    dma(pg, "gpsimd", cst.maskTBb[:, :], dr["maskTB"].ap(), sG)
    dma(pg, "gpsimd", cst.HselT[:, :, :], dr["HselT"].ap(), sG)
    dma(pg, "sync", cst.identF[:, :], dr["identF"].ap(), sS)
    dma(pg, "sync", cst.bg3[:, :], dr["bg3"].ap(), sS)
    dma(pg, "sync", cst.epsCol[:, :], dr["epsCol"].ap(), sS)
    dma(pg, "sync", cst.ln8Col[:, :], dr["ln8Col"].ap(), sS)
    w8(pg, "gpsimd", sG, sG.n)
    w8(pg, "sync", sS, sS.n)


def ph_A1(ctx, pg):
    import os
    nc, dr, cst, sb = ctx.nc, ctx.dr, ctx.cst, ctx.sb
    TT, NT = 512, 16
    if os.environ.get("DBG_NT1"):
        NT = 1
    Wk = sb("a1_Wk", (128, 8, D), BF16)
    Wv = sb("a1_Wv", (128, 8, D), BF16)
    Wq = sb("a1_Wq", (128, 8, D), BF16)
    xt = [sb(f"a1_xt{b}", (128, 4, D)) for b in range(2)]
    xnt = [sb(f"a1_xnt{b}", (128, 4, D), BF16) for b in range(2)]
    xnf = [sb(f"a1_xnf{b}", (128, 8, TT), BF16) for b in range(2)]
    rstage = sb("a1_rst", (128, 8, TT))
    qf = sb("a1_qf", (128, 8, TT), BF16)
    kf = sb("a1_kf", (128, 8, TT), BF16)
    vf = sb("a1_vf", (128, 8, TT), BF16)
    stS = sb("a1_sts", (128, 64))
    stQ = sb("a1_stq", (128, 64))
    stM = sb("a1_stm", (128, 64))
    stV = sb("a1_stv", (128, 64))
    stE = sb("a1_ste", (128, 64))
    stY = sb("a1_sty", (128, 64))
    stT = sb("a1_stt", (128, 64))
    gsum = sb("a1_gsum", (128, 64))
    sW = ctx.dsemf("a1W")
    sX = [ctx.dsemf(f"a1x{b}") for b in range(2)]
    sRst = ctx.dsemf("a1rs")
    sQKV = ctx.dsemf("a1qkv")
    tPE, tDV, tAC = ctx.tile["tensor"], ctx.tile["vector"], ctx.tile["scalar"]

    dma(pg, "gpsimd", Wk[:, :, :], wchunks("WkT", ctx), sW)
    dma(pg, "gpsimd", Wv[:, :, :], wchunks("WvT", ctx), sW)
    dma(pg, "gpsimd", Wq[:, :, :], wchunks("WqT", ctx), sW)
    pg.em("vector", lambda eng: eng.memset(ctx.sxn[:, :], 0.0))
    x0 = dr["x0"].ap()
    junk = ctx.junk
    w8(pg, "tensor", sW, 48)
    nb = bank_cycler()
    pe_base, dv_base, ac_base = tPE.n, tDV.n, tAC.n

    def load(i):
        if i >= NT:
            return
        b = i % 2
        w8(pg, "sync", tDV, dv_base + max(0, i - 1) * 4)
        w8(pg, "sync", tPE, pe_base + max(0, i - 1))
        w8(pg, "sync", tAC, ac_base + max(0, i - 1) * 4)
        dma(pg, "sync", xt[b][:, :, :],
            x0[TT * i: TT * (i + 1), :].rearrange("(a p) e -> p a e", p=128),
            sX[b])

    load(0)
    for i in range(NT):
        b = i % 2
        load(i + 1)
        X, XN = xt[b], xnt[b]
        nld = 16 * (i // 2 + 1)
        w8(pg, "scalar", sX[b], nld)
        for a in range(4):
            pg.em("scalar", lambda eng, X=X, a=a: eng.activation(
                junk[:, :D], X[:, a, :], AF.Copy, accum_out=stS[:, a:a + 1]))
            pg.em("scalar", lambda eng, X=X, a=a: eng.activation(
                junk[:, :D], X[:, a, :], AF.Square, accum_out=stQ[:, a:a + 1]))
        ac1 = tinc(ctx, pg, "scalar")
        w8(pg, "vector", tAC, ac1)

        def dve_stats(eng):
            eng.tensor_scalar(stM[:, :], stS[:, :], 1.0 / D, None, ALU.mult)
            eng.tensor_scalar(stV[:, :], stQ[:, :], 1.0 / D, None, ALU.mult)
            eng.tensor_mul(stT[:, :], stM[:, :], stM[:, :])
            eng.tensor_sub(stV[:, :], stV[:, :], stT[:, :])
            return eng.tensor_scalar(stE[:, :], stV[:, :], LN_EPS, None, ALU.add)
        pg.em("vector", dve_stats)
        dv1 = tinc(ctx, pg, "vector")
        w8(pg, "scalar", tDV, dv1)
        pg.em("scalar", lambda eng: eng.activation(
            stY[:, :], stV[:, :], AF.Ln, bias=cst.epsCol[:, :]))
        pg.em("scalar", lambda eng: eng.activation(
            stY[:, :], stY[:, :], AF.Exp, scale=-0.5))
        ac2 = tinc(ctx, pg, "scalar")
        w8(pg, "vector", tAC, ac2)
        rsqrt_nr(pg, stY[:, :], stE[:, :], stT[:, :])
        w8(pg, "vector", tPE, pe_base + max(0, i - 1))

        def dve_norm(eng, X=X, XN=XN):
            ins = None
            for a in range(4):
                ins = eng.tensor_scalar(XN[:, a, :], X[:, a, :], stM[:, a:a + 1],
                                        stY[:, a:a + 1], ALU.subtract, ALU.mult)
            return ins
        pg.em("vector", dve_norm)
        w8(pg, "tensor", sX[b], nld)
        w8(pg, "scalar", sRst, 16 * i)
        for c in range(8):
            bk = nb()
            pe_group(ctx, pg, bk, [
                (lambda eng, X=X, a=a, c=c, bk=bk: nc.tensor.matmul(
                    ctx.ps[bk][:, 128 * a:128 * (a + 1)],
                    X[:, a, 128 * c:128 * (c + 1)], cst.identF[:, :],
                    start=(a == 0), stop=(a == 3))) for a in range(4)])
            consume(ctx, pg, "scalar", bk, [
                lambda eng, c=c, bk=bk: eng.copy(rstage[:, c, :], ctx.ps[bk][:, :])])
        acR = tinc(ctx, pg, "scalar")
        w8(pg, "gpsimd", tAC, acR)
        dma(pg, "gpsimd", fm_slice(ctx.dr["R"], TT * i, TT), rstage[:, :, :], sRst)
        dv2 = tinc(ctx, pg, "vector")
        w8(pg, "tensor", tDV, dv2)
        for c in range(8):
            bk = nb()
            pe_group(ctx, pg, bk, [
                (lambda eng, XN=XN, a=a, c=c, bk=bk: nc.tensor.matmul(
                    ctx.ps[bk][:, 128 * a:128 * (a + 1)],
                    XN[:, a, 128 * c:128 * (c + 1)], cst.identB[:, :],
                    start=(a == 0), stop=(a == 3))) for a in range(4)])
            consume(ctx, pg, "scalar", bk, [
                lambda eng, c=c, bk=bk, b=b: eng.copy(
                    xnf[b][:, c, :], ctx.ps[bk][:, :])])
        acX = tinc(ctx, pg, "scalar")
        w8(pg, "vector", tAC, acX)
        pg.em("vector", lambda eng, b=b: eng.tensor_reduce(
            gsum[:, 0:8], xnf[b][:, :, :], AX.X, ALU.add))
        spacer(pg, ctx)
        pg.em("vector", lambda eng: eng.tensor_add(
            ctx.sxn[:, :], ctx.sxn[:, :], gsum[:, 0:8]))
        dv3 = tinc(ctx, pg, "vector")
        w8(pg, "tensor", tAC, acX)
        w8(pg, "tensor", tDV, dv3)
        first_ev = {"vector": True, "scalar": True}
        for t_i, (Wt, dst) in enumerate([(Wk, kf), (Wv, vf), (Wq, qf)]):
            for m in range(8):
                bk = nb()
                pe_group(ctx, pg, bk, [
                    (lambda eng, Wt=Wt, kc=kc, m=m, bk=bk, b=b: nc.tensor.matmul(
                        ctx.ps[bk][:, :], Wt[:, kc, 128 * m:128 * (m + 1)],
                        xnf[b][:, kc, :], start=(kc == 0), stop=(kc == 7)))
                    for kc in range(8)])
                e = "vector" if (m + t_i) % 2 == 0 else "scalar"
                if first_ev[e]:
                    w8(pg, e, sQKV, 16 * 3 * i)
                    first_ev[e] = False
                if e == "vector":
                    fn = (lambda eng, dst=dst, m=m, bk=bk: eng.tensor_copy(
                        dst[:, m, :], ctx.ps[bk][:, :]))
                else:
                    fn = (lambda eng, dst=dst, m=m, bk=bk: eng.copy(
                        dst[:, m, :], ctx.ps[bk][:, :]))
                consume(ctx, pg, e, bk, fn if isinstance(fn, list) else [fn])
        pe1 = tinc(ctx, pg, "tensor")
        dv4 = tinc(ctx, pg, "vector")
        ac3 = tinc(ctx, pg, "scalar")
        w8(pg, "gpsimd", tDV, dv4)
        w8(pg, "gpsimd", tAC, ac3)
        w8(pg, "gpsimd", ctx.tile["tensor"], pe1)

        dma(pg, "gpsimd", fm_slice(dr["kS"], TT * i, TT), kf[:, :, :], sQKV)
        dma(pg, "gpsimd", fm_slice(dr["vS"], TT * i, TT), vf[:, :, :], sQKV)
        dma(pg, "gpsimd", fm_slice(dr["qnS"], TT * i, TT), qf[:, :, :], sQKV)

    w8(pg, "gpsimd", sQKV, sQKV.n)
    w8(pg, "gpsimd", sRst, sRst.n)
    if os.environ.get("DBG_NT1"):
        sdbg = ctx.dsemf("a1dbg2")
        sdbg2 = ctx.dsemf("a1dbg3")
        w8(pg, "sync", ctx.tile["tensor"], ctx.tile["tensor"].n)
        w8(pg, "gpsimd", ctx.tile["tensor"], ctx.tile["tensor"].n)
        dma(pg, "gpsimd", dr["dbgA"].ap()[0:128, 0:4096],
            xnt[0].ap().rearrange("p a e -> p (a e)"), sdbg2)
        dma(pg, "gpsimd", dr["dbgA"].ap()[128:256, 0:4096],
            xnf[0].ap().rearrange("p c s -> p (c s)"), sdbg2)
        dma(pg, "sync", dr["dbgA"].ap()[256:384, 0:24],
            stat.ap().rearrange("p a k -> p (a k)"), sdbg)
        w8(pg, "sync", sdbg, sdbg.n)
        w8(pg, "gpsimd", sdbg2, sdbg2.n)



def build_program(upto=None, taps=()):
    nc = bass.Bass("TRN2", target_bir_lowering=False, debug=False)
    ctx = Ctx()
    ctx.nc = nc
    ctx.phase_idx = 0
    dr = {}

    def din(name, shape, dtype=F32):
        dr[name] = nc.dram_tensor(name, list(shape), dtype, kind="ExternalInput")

    def dint(name, shape, dtype=F32):
        dr[name] = nc.dram_tensor(name, list(shape), dtype, kind="Internal")

    din("x0", (S, D))
    for nm in ("WkT", "WvT", "WqT", "W0T", "W1T", "W1N", "WoutT"):
        din(nm, (D, D))
    din("Wg3", (D, 3))
    din("bg3", (3, 1))
    din("identF", (128, 128))
    din("onesCol", (128, 1))
    din("epsCol", (128, 1))
    din("ln8Col", (128, 1))
    din("onesRow", (1, 128))
    din("Hsel", (128, 8, HEADS))
    din("maskW", (128, 384))
    din("mask0a", (128, 256))
    din("mask0b", (128, 256))
    din("maskTA", (128, 512))
    din("maskTB", (128, 512))
    din("HselT", (HEADS, 8, 128))
    for nm in ("rotCQ", "rotSQ", "rotCK", "rotSK"):
        din(nm, (128, SEQ))
    for i in range(NL):
        din(f"Wqkv{i}", (D, 3 * D))
        din(f"Wo{i}", (D, D))
        din(f"W1a{i}", (D, DI))
        din(f"W1g{i}", (D, DI))
        din(f"W2p{i}", (DI, D))
        din(f"b1a{i}", (128, NJ))
        din(f"b1g{i}", (128, NJ))
        din(f"b2c{i}", (128, 8))
        din(f"lng{i}", (128, 8))
        din(f"lnb{i}", (128, 8))
        din(f"Mq{i}", (HEADS, D))
        din(f"Mk{i}", (HEADS, D))
    y = nc.dram_tensor("y", [S, D], F32, kind="ExternalOutput")
    dint("R", (8, 128, S))
    dint("R1", (8, 128, S))
    dint("kS", (8, 128, S), BF16)
    dint("vS", (8, 128, S), BF16)
    dint("qnS", (8, 128, S), BF16)
    dint("aqnS", (8, 128, S), BF16)
    dint("aknS", (8, 128, S), BF16)
    dint("avtS", (64, 128, HEADS, DH + 1), BF16)
    dint("dbgA", (384, 4096))
    dint("rows", (2, S), BF16)
    dint("rrow", (HEADS * WIN,), BF16)
    dint("ccI", (2051, D))
    dint("ccO", (2051, D))
    ctx.dr = dr

    tap_outs = {}
    for t in taps:
        h = dr[t]
        tap_outs[t] = nc.dram_tensor("tap_" + t, list(h.shape), h.dtype,
                                     kind="ExternalOutput")

    with contextlib.ExitStack() as stk:
        ctx.stk = stk
        nsem = [0]

        def new_sem(tag):
            nsem[0] += 1
            return Sem(stk.enter_context(nc.semaphore(f"s{nsem[0]}_{tag}")))

        ctx.new_sem = new_sem
        ctx.ps = [stk.enter_context(nc.psum_tensor(f"ps{i}", [128, 512], F32))
                  for i in range(8)]
        ctx.mmS = [new_sem(f"mm{i}") for i in range(8)]
        ctx.evS = [new_sem(f"ev{i}") for i in range(8)]
        ctx.fills = [0] * 8
        ctx.frees = [0] * 8
        ctx.ph = {e: new_sem("ph_" + e) for e in ENGINES}
        ctx.dsem = {}

        def dsemf(tag):
            if tag not in ctx.dsem:
                ctx.dsem[tag] = new_sem("d_" + tag)
            return ctx.dsem[tag]

        ctx.dsemf = dsemf
        ctx.tile = {e: new_sem("t_" + e) for e in
                    ("vector", "scalar", "tensor", "gpsimd", "sync")}
        ctx.ccS = new_sem("cc")
        ctx.scope = [stk]

        nsb = [0]

        def sb(name, shape, dtype=F32):
            nsb[0] += 1
            return ctx.scope[-1].enter_context(
                nc.sbuf_tensor(f"{name}_{nsb[0]}", list(shape), dtype))

        ctx.sb = sb

        with nc.Block() as blk:
            ctx.blk = blk
            cst = Ctx()
            cst.identF = stk.enter_context(nc.sbuf_tensor("c_idF", [128, 128], F32))
            cst.identB = stk.enter_context(nc.sbuf_tensor("c_idB", [128, 128], BF16))
            cst.onesColB = stk.enter_context(nc.sbuf_tensor("c_o1", [128, 1], BF16))
            cst.onesRowB = stk.enter_context(nc.sbuf_tensor("c_or", [1, 128], BF16))
            cst.HselB = stk.enter_context(nc.sbuf_tensor("c_hs", [128, 8, HEADS], BF16))
            cst.maskWb = stk.enter_context(nc.sbuf_tensor("c_mw", [128, 384], BF16))
            cst.mask0ab = stk.enter_context(nc.sbuf_tensor("c_m0a", [128, 256], BF16))
            cst.mask0bb = stk.enter_context(nc.sbuf_tensor("c_m0b", [128, 256], BF16))
            cst.maskTAb = stk.enter_context(nc.sbuf_tensor("c_mta", [128, 512], BF16))
            cst.maskTBb = stk.enter_context(nc.sbuf_tensor("c_mtb", [128, 512], BF16))
            cst.HselT = stk.enter_context(
                nc.sbuf_tensor("c_hst", [HEADS, 8, 128], BF16))
            cst.bg3 = stk.enter_context(nc.sbuf_tensor("c_bg3", [3, 1], F32))
            cst.epsCol = stk.enter_context(nc.sbuf_tensor("c_eps", [128, 1], F32))
            cst.ln8Col = stk.enter_context(nc.sbuf_tensor("c_ln8", [128, 1], F32))
            ctx.cst = cst
            ctx.junk = stk.enter_context(nc.sbuf_tensor("junk", [128, D], BF16))
            ctx.sxn = stk.enter_context(nc.sbuf_tensor("sxn", [128, 8], F32))

            done = [False]

            def phase(fn, **kw):
                if done[0]:
                    return
                with contextlib.ExitStack() as pscope:
                    ctx.scope.append(pscope)
                    pg = Prog()
                    fn(ctx, pg, **kw)
                    run_phase(ctx, pg)
                    ctx.scope.pop()
                name = fn.__name__ + str(kw.get("layer", ""))
                if upto is not None and name == upto:
                    done[0] = True

            phase(ph_P0)
            phase(ph_A1)
            phase(ph_A2)
            with contextlib.ExitStack() as wscope:
                ctx.W0nT = wscope.enter_context(
                    nc.sbuf_tensor("W0nT", [128, 8, D], BF16))
                ctx.W1nT = wscope.enter_context(
                    nc.sbuf_tensor("W1nT", [128, 8, D], BF16))
                ctx.WoutTb = wscope.enter_context(
                    nc.sbuf_tensor("WoutTb", [128, 8, D], BF16))
                phase(ph_AR)
                phase(ph_E)
            phase(ph_E1)
            for i in range(NL):
                with contextlib.ExitStack() as lscope:
                    ctx.Wo_l = lscope.enter_context(
                        nc.sbuf_tensor(f"Wo_l{i}", [128, 8, D], BF16))
                    phase(ph_ATTN1, layer=i)
                    phase(ph_ATTN2, layer=i)
                phase(ph_FFN0, layer=i)
                phase(ph_FFN1, layer=i)
            phase(ph_FIN, y=y)

            if upto is not None:
                pg = Prog()
                sst = dsemf("dump")
                dma(pg, "sync",
                    y.ap().rearrange("(c p a) e -> c p (a e)", c=8, p=128),
                    dr["R"].ap(), sst)
                for t, o in tap_outs.items():
                    dma(pg, "sync", o.ap(), dr[t].ap(), sst)
                w8(pg, "sync", sst, sst.n)
                run_phase(ctx, pg)
    return nc


def ph_A2(ctx, pg):
    nc, dr, cst, sb = ctx.nc, ctx.dr, ctx.cst, ctx.sb
    TT2, NT2 = 256, 32
    W0T = sb("a2_W0T", (128, 8, D), BF16)
    W1T = sb("a2_W1T", (128, 8, D), BF16)
    W1N = sb("a2_W1N", (128, 8, D), BF16)
    ctx.G1T = sb("a2_G1T", (128, 8, D))
    ctx.G0T = sb("a2_G0T", (128, 8, D))
    kf = [sb(f"a2_kf{b}", (128, 8, TT2), BF16) for b in range(2)]
    vf = [sb(f"a2_vf{b}", (128, 8, TT2), BF16) for b in range(2)]
    sg = sb("a2_sg", (128, 8, TT2), BF16)
    a1f = sb("a2_a1", (128, 8, TT2), BF16)
    sdf = sb("a2_sd", (128, 8, TT2), BF16)
    erf = sb("a2_er", (128, 8, TT2), BF16)
    d1f = sb("a2_d1", (128, 8, TT2), BF16)
    e1t = sb("a2_e1t", (128, TT2))
    t1t = sb("a2_t1t", (128, TT2), BF16)
    t2t = sb("a2_t2t", (128, TT2))
    kT = sb("a2_kT", (128, 2, 8, 128), BF16)
    a1T = sb("a2_a1T", (128, 2, 8, 128), BF16)
    erT = sb("a2_erT", (128, 2, 8, 128), BF16)
    d1T = sb("a2_d1T", (128, 2, 8, 128), BF16)
    sW = ctx.dsemf("a2W")
    sK = [ctx.dsemf(f"a2k{b}") for b in range(2)]
    sG = ctx.dsemf("a2G")
    tPE, tDV, tAC = ctx.tile["tensor"], ctx.tile["vector"], ctx.tile["scalar"]

    dma(pg, "gpsimd", W0T[:, :, :], wchunks("W0T", ctx), sW)
    dma(pg, "gpsimd", W1T[:, :, :], wchunks("W1T", ctx), sW)
    dma(pg, "gpsimd", W1N[:, :, :], wchunks("W1N", ctx), sW)
    pg.em("vector", lambda eng: eng.memset(ctx.G1T[:, :, :], 0.0))
    pg.em("vector", lambda eng: eng.memset(ctx.G0T[:, :, :], 0.0))
    w8(pg, "tensor", sW, sW.n)
    nb = bank_cycler()
    pe_base = tPE.n

    def load(i):
        if i >= NT2:
            return
        b = i % 2
        w8(pg, "sync", tPE, pe_base + max(0, i - 1))
        dma(pg, "sync", kf[b][:, :, :], fm_slice(dr["kS"], TT2 * i, TT2), sK[b])
        dma(pg, "sync", vf[b][:, :, :], fm_slice(dr["vS"], TT2 * i, TT2), sK[b])

    load(0)
    for i in range(NT2):
        b = i % 2
        load(i + 1)
        nld = 32 * (i // 2 + 1)
        w8(pg, "tensor", sK[b], nld)
        for m in range(8):
            bk = nb()
            pe_group(ctx, pg, bk, [
                (lambda eng, kc=kc, m=m, bk=bk, b=b: nc.tensor.matmul(
                    ctx.ps[bk][:, 0:TT2], W0T[:, kc, 128 * m:128 * (m + 1)],
                    kf[b][:, kc, :], start=(kc == 0), stop=(kc == 7)))
                for kc in range(8)])
            consume(ctx, pg, "scalar", bk, [
                lambda eng, m=m, bk=bk: eng.activation(
                    sg[:, m, :], ctx.ps[bk][:, 0:TT2], AF.Sigmoid)])
            acm = tinc(ctx, pg, "scalar")
            w8(pg, "vector", tAC, acm)

            def dve_sd(eng, m=m, bk=bk):
                ps = ctx.ps[bk][:, 0:TT2]
                eng.tensor_tensor(a1f[:, m, :], ps, sg[:, m, :], ALU.mult)
                eng.tensor_scalar(t1t[:, :], sg[:, m, :], -1.0, 1.0,
                                  ALU.mult, ALU.add)
                eng.tensor_tensor(t2t[:, :], ps, t1t[:, :], ALU.mult)
                return eng.scalar_tensor_tensor(
                    sdf[:, m, :], t2t[:, :], 1.0, sg[:, m, :], ALU.add, ALU.mult)
            consume(ctx, pg, "vector", bk, [dve_sd])
        dv_a = tinc(ctx, pg, "vector")
        w8(pg, "tensor", tDV, dv_a)
        for m in range(8):
            bk = nb()
            pe_group(ctx, pg, bk, [
                (lambda eng, kc=kc, m=m, bk=bk: nc.tensor.matmul(
                    ctx.ps[bk][:, 0:TT2], W1T[:, kc, 128 * m:128 * (m + 1)],
                    a1f[:, kc, :], start=(kc == 0), stop=(kc == 7)))
                for kc in range(8)])

            def dve_err(eng, m=m, bk=bk, b=b):
                eng.tensor_sub(e1t[:, :], ctx.ps[bk][:, 0:TT2], vf[b][:, m, :])
                return eng.tensor_scalar(erf[:, m, :], e1t[:, :], ERR_CLIP,
                                         -ERR_CLIP, ALU.min, ALU.max)
            consume(ctx, pg, "vector", bk, [dve_err])
        dv_b = tinc(ctx, pg, "vector")
        w8(pg, "tensor", tDV, dv_b)
        for m in range(8):
            bk = nb()
            pe_group(ctx, pg, bk, [
                (lambda eng, kc=kc, m=m, bk=bk: nc.tensor.matmul(
                    ctx.ps[bk][:, 0:TT2], W1N[:, kc, 128 * m:128 * (m + 1)],
                    erf[:, kc, :], start=(kc == 0), stop=(kc == 7)))
                for kc in range(8)])
            consume(ctx, pg, "vector", bk, [
                lambda eng, m=m, bk=bk: eng.tensor_tensor(
                    d1f[:, m, :], ctx.ps[bk][:, 0:TT2], sdf[:, m, :], ALU.mult)])
        dv_c = tinc(ctx, pg, "vector")
        w8(pg, "tensor", tDV, dv_c)
        for (srcT, dstT) in ((kf[b], kT), (a1f, a1T), (erf, erT), (d1f, d1T)):
            for c in range(8):
                bk = nb()
                pe_group(ctx, pg, bk, [
                    (lambda eng, srcT=srcT, c=c, u=u, bk=bk: nc.tensor.matmul(
                        ctx.ps[bk][:, 128 * u:128 * (u + 1)],
                        srcT[:, c, 128 * u:128 * (u + 1)], cst.identB[:, :],
                        start=(u == 0), stop=(u == 1))) for u in range(2)])
                consume(ctx, pg, "vector", bk, [
                    lambda eng, dstT=dstT, c=c, bk=bk: eng.tensor_copy(
                        dstT[:, :, c, :], ctx.ps[bk][:, 0:TT2])])
        dv_d = tinc(ctx, pg, "vector")
        w8(pg, "tensor", tDV, dv_d)
        for (lT, rT, acc) in ((a1T, erT, ctx.G1T), (kT, d1T, ctx.G0T)):
            for m in range(8):
                for h in range(2):
                    bk = nb()
                    pe_group(ctx, pg, bk, [
                        (lambda eng, lT=lT, rT=rT, m=m, h=h, u=u, bk=bk:
                         nc.tensor.matmul(
                             ctx.ps[bk][:, :],
                             lT[:, u, m, :],
                             rT[:, u, 4 * h:4 * (h + 1), :], start=(u == 0),
                             stop=(u == 1))) for u in range(2)])
                    consume(ctx, pg, "vector", bk, [
                        lambda eng, acc=acc, m=m, h=h, bk=bk: eng.tensor_tensor(
                            acc[:, m, 512 * h:512 * (h + 1)],
                            acc[:, m, 512 * h:512 * (h + 1)],
                            ctx.ps[bk][:, :], ALU.add)])
        tinc(ctx, pg, "tensor")
    dv_e = tinc(ctx, pg, "vector")
    w8(pg, "sync", tDV, dv_e)
    dma(pg, "sync", dr["ccI"].ap()[0:1024, :].rearrange("(c p) d -> p c d", p=128),
        ctx.G1T[:, :, :], sG)
    dma(pg, "sync", dr["ccI"].ap()[1024:2048, :].rearrange("(c p) d -> p c d", p=128),
        ctx.G0T[:, :, :], sG)
    w8(pg, "sync", sG, sG.n)


def ph_AR(ctx, pg):
    nc, dr, cst, sb = ctx.nc, ctx.dr, ctx.cst, ctx.sb
    Wg3b = sb("ar_wg", (128, 8, 3), BF16)
    sxnb = sb("ar_sxnb", (128, 8), BF16)
    g3s = sb("ar_g3", (3, 1))
    a3r = sb("ar_a3r", (1, 3))
    a3rb = sb("ar_a3rb", (1, 3), BF16)
    gbf = sb("ar_gb", (128, 3))
    alc = sb("ar_alc", (128, 1))
    lrs = sb("ar_lrs", (128, 1))
    Gt = [sb(f"ar_gt{b}", (128, 1024)) for b in range(2)]
    Wf = [sb(f"ar_wf{b}", (128, 1024)) for b in range(2)]
    t2 = sb("ar_t2", (128, 1024))
    sL = ctx.dsemf("arL")
    sO = ctx.dsemf("arO")
    sW = ctx.dsemf("arW")
    tPE, tDV, tAC = ctx.tile["tensor"], ctx.tile["vector"], ctx.tile["scalar"]

    dma(pg, "gpsimd", Wg3b[:, :, :], dr["Wg3"].ap().rearrange(
        "(c p) n -> p c n", p=128), sL)
    dma(pg, "gpsimd", ctx.WoutTb[:, :, :], wchunks("WoutT", ctx), sL)
    pg.em("vector", lambda eng: eng.tensor_copy(sxnb[:, :], ctx.sxn[:, :]))
    dv1 = tinc(ctx, pg, "vector")
    w8(pg, "tensor", sL, 32)
    w8(pg, "tensor", tDV, dv1)
    bk = 0
    pe_group(ctx, pg, bk, [
        (lambda eng, kc=kc: nc.tensor.matmul(
            ctx.ps[0][0:3, 0:1], Wg3b[:, kc, :], sxnb[:, kc:kc + 1],
            start=(kc == 0), stop=(kc == 7))) for kc in range(8)])
    consume(ctx, pg, "vector", bk, [
        lambda eng: eng.tensor_copy(g3s[:, :], ctx.ps[0][0:3, 0:1])])
    dv2 = tinc(ctx, pg, "vector")
    w8(pg, "sync", tDV, dv2)
    dma(pg, "sync", dr["ccI"].ap()[2048:2051, 0:1], g3s[:, :],
        ctx.dsemf("arG"), allow_slow_non_contiguous=True)
    w8(pg, "sync", ctx.dsemf("arG"), ctx.dsemf("arG").n)
    sy1 = tinc(ctx, pg, "sync")
    w8(pg, "gpsimd", ctx.tile["sync"], sy1)
    w8(pg, "gpsimd", ctx.dsemf("a2G"), ctx.dsemf("a2G").n)

    def cc(eng):
        return eng.collective_compute(
            "AllReduce", ALU.add, replica_groups=[list(range(NCORES))],
            ins=[dr["ccI"].ap()], outs=[dr["ccO"].ap()])
    pg.em("gpsimd", lambda eng: cc(eng).then_inc(ctx.ccS.h))
    ctx.ccS.n += 1
    pg.anchor["gpsimd"] = (ctx.ccS.h, 1)
    pg.kind["gpsimd"] = "inc"
    w8(pg, "sync", ctx.ccS, 1)
    dma(pg, "sync", g3s[:, :], dr["ccO"].ap()[2048:2051, 0:1],
        ctx.dsemf("arG2"), allow_slow_non_contiguous=True)
    w8(pg, "scalar", ctx.dsemf("arG2"), 16)
    pg.em("scalar", lambda eng: eng.activation(
        g3s[:, :], g3s[:, :], AF.Sigmoid, bias=cst.bg3[:, :],
        scale=1.0 / (16 * SEQ)))
    ac1 = tinc(ctx, pg, "scalar")
    w8(pg, "tensor", tAC, ac1)
    pe_group(ctx, pg, 1, [lambda eng: nc.tensor.matmul(
        ctx.ps[1][0:1, 0:3], g3s[:, :], cst.identF[0:3, 0:3],
        start=True, stop=True)])
    consume(ctx, pg, "vector", 1, [
        lambda eng: eng.tensor_copy(a3r[:, :], ctx.ps[1][0:1, 0:3])])
    spacer(pg, ctx)
    pg.em("vector", lambda eng: eng.tensor_copy(a3rb[:, :], a3r[:, :]))
    dv3 = tinc(ctx, pg, "vector")
    w8(pg, "tensor", tDV, dv3)
    pe_group(ctx, pg, 2, [lambda eng: nc.tensor.matmul(
        ctx.ps[2][:, 0:3], cst.onesRowB[:, :], a3rb[:, :],
        start=True, stop=True)])
    consume(ctx, pg, "vector", 2, [
        lambda eng: eng.tensor_copy(gbf[:, :], ctx.ps[2][:, 0:3])])
    spacer(pg, ctx)
    pg.em("vector", lambda eng: eng.tensor_scalar(
        alc[:, :], gbf[:, 0:1], -1.0, 1.0, ALU.mult, ALU.add))
    spacer(pg, ctx)
    pg.em("vector", lambda eng: eng.tensor_copy(lrs[:, :], gbf[:, 1:2]))
    spacer(pg, ctx)
    dv4 = tinc(ctx, pg, "vector")
    ccO = dr["ccO"].ap()
    for wi, (rows, WTname, dst) in enumerate(
            ((1024, "W0T", ctx.W0nT), (0, "W1T", ctx.W1nT))):
        for c in range(8):
            b = (wi * 8 + c) % 2
            sGt = ctx.dsemf(f"arGt{b}")
            sWf = ctx.dsemf(f"arWf{b}")
            if wi * 8 + c >= 2:
                w8(pg, "sync", tDV, dv4 + (wi * 8 + c) - 1)
            dma(pg, "sync", Gt[b][:, :],
                ccO[rows:rows + 1024, :].rearrange("(c p) d -> p c d", p=128)[:, c, :],
                sGt)
            dma(pg, "sync", Wf[b][:, :],
                dr[WTname].ap().rearrange("(c p) d -> p c d", p=128)[:, c, :],
                sWf)
            w8(pg, "vector", sGt, sGt.n)
            w8(pg, "vector", sWf, sWf.n)

            def build(eng, b=b, c=c, dst=dst):
                eng.tensor_scalar(Gt[b][:, :], Gt[b][:, :], TTT_SCALE,
                                  GRAD_CLIP, ALU.mult, ALU.min)
                eng.tensor_scalar(t2[:, :], Gt[b][:, :], -GRAD_CLIP,
                                  lrs[:, :], ALU.max, ALU.mult)
                return eng.scalar_tensor_tensor(
                    dst[:, c, :], Wf[b][:, :], alc[:, :], t2[:, :],
                    ALU.mult, ALU.subtract)
            pg.em("vector", build)
            tinc(ctx, pg, "vector")
    w8(pg, "gpsimd", sL, sL.n)


def ph_E(ctx, pg):
    nc, dr, cst, sb = ctx.nc, ctx.dr, ctx.cst, ctx.sb
    TT, NT = 512, 16
    qf = [sb(f"e_qf{b}", (128, 8, TT), BF16) for b in range(2)]
    sg1 = sb("e_sg", (128, 8, TT), BF16)
    s1f = sb("e_s1", (128, 8, TT), BF16)
    h2f = sb("e_h2", (128, 8, TT), BF16)
    ost = sb("e_ost", (128, 8, TT))
    sQ = [ctx.dsemf(f"eq{b}") for b in range(2)]
    sE = ctx.dsemf("eR")
    tPE, tDV, tAC = ctx.tile["tensor"], ctx.tile["vector"], ctx.tile["scalar"]
    nb = bank_cycler()
    pe_base = tPE.n

    def load(i):
        if i >= NT:
            return
        b = i % 2
        w8(pg, "sync", tPE, pe_base + max(0, i - 1))
        dma(pg, "sync", qf[b][:, :, :], fm_slice(dr["qnS"], TT * i, TT), sQ[b])

    load(0)
    for i in range(NT):
        b = i % 2
        load(i + 1)
        w8(pg, "tensor", sQ[b], 16 * (i // 2 + 1))
        for m in range(8):
            bk = nb()
            pe_group(ctx, pg, bk, [
                (lambda eng, kc=kc, m=m, bk=bk, b=b: nc.tensor.matmul(
                    ctx.ps[bk][:, :], ctx.W0nT[:, kc, 128 * m:128 * (m + 1)],
                    qf[b][:, kc, :], start=(kc == 0), stop=(kc == 7)))
                for kc in range(8)])
            consume(ctx, pg, "scalar", bk, [
                lambda eng, m=m, bk=bk: eng.activation(
                    sg1[:, m, :], ctx.ps[bk][:, :], AF.Sigmoid)])
            acm = tinc(ctx, pg, "scalar")
            w8(pg, "vector", tAC, acm)
            consume(ctx, pg, "vector", bk, [
                lambda eng, m=m, bk=bk: eng.tensor_tensor(
                    s1f[:, m, :], ctx.ps[bk][:, :], sg1[:, m, :], ALU.mult)])
        dv1 = tinc(ctx, pg, "vector")
        w8(pg, "tensor", tDV, dv1)
        for m in range(8):
            bk = nb()
            pe_group(ctx, pg, bk, [
                (lambda eng, kc=kc, m=m, bk=bk: nc.tensor.matmul(
                    ctx.ps[bk][:, :], ctx.W1nT[:, kc, 128 * m:128 * (m + 1)],
                    s1f[:, kc, :], start=(kc == 0), stop=(kc == 7)))
                for kc in range(8)])
            consume(ctx, pg, "vector", bk, [
                lambda eng, m=m, bk=bk: eng.tensor_copy(
                    h2f[:, m, :], ctx.ps[bk][:, :])])
        dv2 = tinc(ctx, pg, "vector")
        w8(pg, "tensor", tDV, dv2)
        w8(pg, "scalar", sE, 16 * i)
        for m in range(8):
            bk = nb()
            pe_group(ctx, pg, bk, [
                (lambda eng, kc=kc, m=m, bk=bk: nc.tensor.matmul(
                    ctx.ps[bk][:, :], ctx.WoutTb[:, kc, 128 * m:128 * (m + 1)],
                    h2f[:, kc, :], start=(kc == 0), stop=(kc == 7)))
                for kc in range(8)])
            consume(ctx, pg, "scalar", bk, [
                lambda eng, m=m, bk=bk: eng.copy(ost[:, m, :], ctx.ps[bk][:, :])])
        ac2 = tinc(ctx, pg, "scalar")
        tinc(ctx, pg, "tensor")
        w8(pg, "gpsimd", tAC, ac2)
        dma(pg, "gpsimd", fm_slice(dr["R"], TT * i, TT), ost[:, :, :], sE,
            accum_op=ALU.add)
    w8(pg, "gpsimd", sE, sE.n)


def ph_E1(ctx, pg):
    nc, dr, cst, sb = ctx.nc, ctx.dr, ctx.cst, ctx.sb
    TT, NT = 512, 16
    Rt = [sb(f"e1_r{b}", (128, 8, TT)) for b in range(2)]
    x1b = sb("e1_x1b", (128, 8, TT), BF16)
    sqb = sb("e1_sqb", (128, 8, TT), BF16)
    xLs = sb("e1_xls", (128, 8, TT))
    muR = sb("e1_mu", (1, TT))
    muR2 = sb("e1_mu2", (1, TT))
    veR = sb("e1_ve", (1, TT))
    veE = sb("e1_vee", (1, TT))
    yR = sb("e1_y", (1, TT))
    tR = sb("e1_t", (1, TT))
    rowYb = sb("e1_yb", (1, TT), BF16)
    nmRb = sb("e1_nmb", (1, TT), BF16)
    sR = [ctx.dsemf(f"e1r{b}") for b in range(2)]
    sS1 = ctx.dsemf("e1s1")
    sSL = ctx.dsemf("e1sl")
    tPE, tDV, tAC = ctx.tile["tensor"], ctx.tile["vector"], ctx.tile["scalar"]
    pe_base = tPE.n

    dv_base = tDV.n
    ac_base = tAC.n

    def load(i):
        if i >= NT:
            return
        b = i % 2
        w8(pg, "sync", tDV, dv_base + max(0, i - 1) * 3)
        w8(pg, "sync", tAC, ac_base + max(0, i - 1) * 2)
        dma(pg, "sync", Rt[b][:, :, :], fm_slice(dr["R"], TT * i, TT), sR[b])

    load(0)
    for i in range(NT):
        b = i % 2
        load(i + 1)
        nld = 16 * (i // 2 + 1)
        w8(pg, "scalar", sR[b], nld)
        w8(pg, "scalar", tPE, pe_base + max(0, i))
        pg.em("scalar", lambda eng, b=b: eng.activation(
            x1b.ap().rearrange("p c s -> p (c s)"),
            Rt[b].ap().rearrange("p c s -> p (c s)"), AF.Copy))
        pg.em("scalar", lambda eng, b=b: eng.activation(
            sqb.ap().rearrange("p c s -> p (c s)"),
            Rt[b].ap().rearrange("p c s -> p (c s)"), AF.Square))
        ac1 = tinc(ctx, pg, "scalar")
        dma(pg, "scalar", fm_slice(dr["R1"], TT * i, TT), Rt[b][:, :, :], sS1)
        w8(pg, "tensor", tAC, ac1)
        pe_group(ctx, pg, 0, [
            (lambda eng, c=c: nc.tensor.matmul(
                ctx.ps[0][0:1, :], cst.onesColB[:, :], x1b[:, c, :],
                start=(c == 0), stop=(c == 7))) for c in range(8)])
        pe_group(ctx, pg, 1, [
            (lambda eng, c=c: nc.tensor.matmul(
                ctx.ps[1][0:1, :], cst.onesColB[:, :], sqb[:, c, :],
                start=(c == 0), stop=(c == 7))) for c in range(8)])

        def rows0(eng):
            eng.tensor_scalar(muR[:, :], ctx.ps[0][0:1, :], 1.0 / D, None, ALU.mult)
            return eng.tensor_scalar(muR2[:, :], ctx.ps[0][0:1, :], 1.0 / D,
                                     None, ALU.mult)

        def rows1(eng):
            eng.tensor_scalar(veR[:, :], ctx.ps[1][0:1, :], 1.0 / D, None, ALU.mult)
            eng.tensor_mul(tR[:, :], muR[:, :], muR2[:, :])
            eng.tensor_sub(veR[:, :], veR[:, :], tR[:, :])
            return eng.tensor_scalar(veE[:, :], veR[:, :], LN_EPS, None, ALU.add)
        consume(ctx, pg, "vector", 0, [rows0])
        consume(ctx, pg, "vector", 1, [rows1])
        dv1 = tinc(ctx, pg, "vector")
        w8(pg, "scalar", tDV, dv1)
        pg.em("scalar", lambda eng: eng.activation(
            yR[:, :], veR[:, :], AF.Ln, bias=cst.epsCol[0:1, :]))
        pg.em("scalar", lambda eng: eng.activation(
            yR[:, :], yR[:, :], AF.Exp, scale=-0.5))
        ac2 = tinc(ctx, pg, "scalar")
        w8(pg, "vector", tAC, ac2)
        rsqrt_nr(pg, yR[:, :], veE[:, :], tR[:, :])

        def rows2(eng):
            eng.tensor_mul(tR[:, :], muR[:, :], yR[:, :])
            eng.tensor_scalar(tR[:, :], tR[:, :], -1.0, None, ALU.mult)
            eng.tensor_copy(rowYb[:, :], yR[:, :])
            return eng.tensor_copy(nmRb[:, :], tR[:, :])
        pg.em("vector", rows2)
        dv2 = tinc(ctx, pg, "vector")
        w8(pg, "tensor", tDV, dv2)
        pe_group(ctx, pg, 2, [lambda eng: nc.tensor.matmul(
            ctx.ps[2][:, :], cst.onesRowB[:, :], rowYb[:, :],
            start=True, stop=True)])
        pe_group(ctx, pg, 3, [lambda eng: nc.tensor.matmul(
            ctx.ps[3][:, :], cst.onesRowB[:, :], nmRb[:, :],
            start=True, stop=True)])
        w8(pg, "vector", sSL, 16 * i)
        fns = []
        for c in range(8):
            fns.append(lambda eng, c=c, b=b: eng.tensor_tensor(
                xLs[:, c, :], Rt[b][:, c, :], ctx.ps[2][:, :], ALU.mult))
        consume(ctx, pg, "vector", 2, fns)
        fns2 = []
        for c in range(8):
            fns2.append(lambda eng, c=c: eng.tensor_tensor(
                xLs[:, c, :], xLs[:, c, :], ctx.ps[3][:, :], ALU.add))
        consume(ctx, pg, "vector", 3, fns2)
        dv3 = tinc(ctx, pg, "vector")
        tinc(ctx, pg, "tensor")
        w8(pg, "sync", tDV, dv3)
        w8(pg, "sync", sS1, sS1.n)
        dma(pg, "sync", fm_slice(dr["R"], TT * i, TT), xLs[:, :, :], sSL)
    w8(pg, "sync", sSL, sSL.n)
    w8(pg, "scalar", sS1, sS1.n)


def ph_ATTN1(ctx, pg, layer=0):
    nc, dr, cst, sb = ctx.nc, ctx.dr, ctx.cst, ctx.sb
    TT, NT = 512, 16
    Wq3 = sb("t1_wq3", (128, 8, 3 * D), BF16)
    Mqb = sb("t1_mq", (16, D), BF16)
    Mkb = sb("t1_mk", (16, D), BF16)
    xsb = [sb(f"t1_xs{b}", (128, 8, TT + 1), BF16) for b in range(2)]
    tabs = [[sb(f"t1_tb{b}_{j}", (128, TT), BF16) for j in range(4)]
            for b in range(2)]
    qraw = sb("t1_qr", (128, 8, TT), BF16)
    kraw = sb("t1_kr", (128, 8, TT), BF16)
    vraw = sb("t1_vr", (128, 8, TT), BF16)
    sq2 = sb("t1_sq2", (128, 8, TT), BF16)
    ssm = sb("t1_ssm", (16, 2 * TT))
    yss = sb("t1_yss", (16, 2 * TT))
    tss = sb("t1_tss", (16, 2 * TT))
    rq8 = sb("t1_rq8", (16, TT), BF16)
    rk1 = sb("t1_rk1", (16, TT), BF16)
    tq = sb("t1_tq", (128, 8, TT), BF16)
    tk = sb("t1_tk", (128, 8, TT), BF16)
    tqp = sb("t1_tqp", (128, 8, TT), BF16)
    tkp = sb("t1_tkp", (128, 8, TT), BF16)
    avs = sb("t1_avs", (128, 4, HEADS, DH + 1), BF16)
    sW = ctx.dsemf("t1W")
    sX = [ctx.dsemf(f"t1x{b}") for b in range(2)]
    sTb = [ctx.dsemf(f"t1t{b}") for b in range(2)]
    sRot = ctx.dsemf("t1rot")
    sSt = ctx.dsemf("t1st")
    sX0 = [s.n for s in sX]
    sTb0 = [s.n for s in sTb]
    sSt0 = sSt.n
    tPE, tDV, tAC = ctx.tile["tensor"], ctx.tile["vector"], ctx.tile["scalar"]
    nb = bank_cycler()
    dma(pg, "gpsimd", Wq3[:, :, :], wchunks(f"Wqkv{layer}", ctx), sW)
    dma(pg, "gpsimd", Mqb[:, :], dr[f"Mq{layer}"].ap(), sW)
    dma(pg, "gpsimd", Mkb[:, :], dr[f"Mk{layer}"].ap(), sW)
    dma(pg, "gpsimd", ctx.Wo_l[:, :, :], wchunks(f"Wo{layer}", ctx), sW)
    w8(pg, "tensor", sW, sW.n)
    dvends = []
    pe_mark = {}

    def load(i):
        if i >= NT:
            return
        b = i % 2
        if i >= 2:
            w8(pg, "gpsimd", *pe_mark[i - 2])
            w8(pg, "gpsimd", tDV, dvends[i - 2])
        s0 = TT * i
        if i % 8 == 0:
            dma(pg, "gpsimd", xsb[b][:, :, 1:TT + 1],
                fm_slice(dr["R"], s0, TT), sX[b])
        else:
            dma(pg, "gpsimd", xsb[b][:, :, :],
                fm_slice(dr["R"], s0 - 1, TT + 1), sX[b])
        c0 = s0 % SEQ
        for j, tn in enumerate(("rotCQ", "rotSQ", "rotCK", "rotSK")):
            dma(pg, "gpsimd", tabs[b][j][:, :],
                dr[tn].ap()[:, c0:c0 + TT], sTb[b])

    pg.em("vector", lambda eng: eng.memset(avs[:, :, :, DH:DH + 1], 1.0))
    load(0)
    for i in range(NT):
        b = i % 2
        load(i + 1)
        if i % 8 == 0:
            pg.em("vector", lambda eng, b=b: eng.memset(xsb[b][:, :, 0:1], 0.0))
            dvm = tincL(ctx, pg, "vector")
            w8(pg, "tensor", tDV, dvm)
        w8(pg, "tensor", sX[b], sX0[b] + 16 * (i // 2 + 1))
        for m in range(24):
            bk = nb()
            pe_group(ctx, pg, bk, [
                (lambda eng, kc=kc, m=m, bk=bk, b=b: nc.tensor.matmul(
                    ctx.ps[bk][:, :],
                    Wq3[:, kc, 128 * m:128 * (m + 1)],
                    xsb[b][:, kc, 1:TT + 1] if kc < 4 else xsb[b][:, kc, 0:TT],
                    start=(kc == 0), stop=(kc == 7)))
                for kc in range(8)])
            tgt, mm = (qraw, m) if m < 8 else ((kraw, m - 8) if m < 16
                                               else (vraw, m - 16))
            consume(ctx, pg, "vector", bk, [
                lambda eng, tgt=tgt, mm=mm, bk=bk: eng.tensor_copy(
                    tgt[:, mm, :], ctx.ps[bk][:, :])])
            qkv_done = (ctx.evS[bk], ctx.evS[bk].n)
        w8(pg, "scalar", *qkv_done)
        pg.em("scalar", lambda eng: eng.activation(
            sq2.ap().rearrange("p c s -> p (c s)"),
            qraw.ap().rearrange("p c s -> p (c s)"), AF.Square))
        ac1 = tinc(ctx, pg, "scalar")
        w8(pg, "tensor", tAC, ac1)
        bq = nb()
        pe_group(ctx, pg, bq, [
            (lambda eng, c=c, bq=bq: nc.tensor.matmul(
                ctx.ps[bq][0:16, :], cst.HselB[:, c, :], sq2[:, c, :],
                start=(c == 0), stop=(c == 7))) for c in range(8)])
        w8(pg, "scalar", ctx.mmS[bq], ctx.fills[bq])
        pg.em("scalar", lambda eng: eng.activation(
            sq2.ap().rearrange("p c s -> p (c s)"),
            kraw.ap().rearrange("p c s -> p (c s)"), AF.Square))
        ac2 = tinc(ctx, pg, "scalar")
        w8(pg, "tensor", tAC, ac2)
        bK = nb()
        pe_group(ctx, pg, bK, [
            (lambda eng, c=c, bK=bK: nc.tensor.matmul(
                ctx.ps[bK][0:16, :], cst.HselB[:, c, :], sq2[:, c, :],
                start=(c == 0), stop=(c == 7))) for c in range(8)])
        consume(ctx, pg, "vector", bq, [
            lambda eng, bq=bq: eng.tensor_scalar(
                ssm[:, 0:TT], ctx.ps[bq][0:16, :], 1e-24, None, ALU.max)])
        consume(ctx, pg, "vector", bK, [
            lambda eng, bK=bK: eng.tensor_scalar(
                ssm[:, TT:], ctx.ps[bK][0:16, :], 1e-24, None, ALU.max)])
        ssm_done = (ctx.evS[bK], ctx.evS[bK].n)
        w8(pg, "scalar", *ssm_done)
        pg.em("scalar", lambda eng: eng.activation(yss[:, :], ssm[:, :], AF.Ln))
        pg.em("scalar", lambda eng: eng.activation(
            yss[:, :], yss[:, :], AF.Exp, scale=-0.5))
        ac3 = tinc(ctx, pg, "scalar")
        w8(pg, "vector", tAC, ac3)
        rsqrt_nr(pg, yss[:, :], ssm[:, :], tss[:, :])

        def facs(eng):
            eng.tensor_scalar(rq8[:, :], yss[:, 0:TT], 8.0, None, ALU.mult)
            return eng.tensor_copy(rk1[:, :], yss[:, TT:])
        pg.em("vector", facs)
        dv3 = tincL(ctx, pg, "vector")
        w8(pg, "vector", sSt, sSt0 + 48 * i)
        w8(pg, "tensor", tDV, dv3)
        for (Mb, rb, raw, tdst) in ((Mqb, rq8, qraw, tq), (Mkb, rk1, kraw, tk)):
            for c in range(8):
                bk = nb()
                pe_group(ctx, pg, bk, [
                    lambda eng, Mb=Mb, rb=rb, c=c, bk=bk: nc.tensor.matmul(
                        ctx.ps[bk][:, :], Mb[:, 128 * c:128 * (c + 1)], rb[:, :],
                        start=True, stop=True)])
                consume(ctx, pg, "vector", bk, [
                    lambda eng, raw=raw, tdst=tdst, c=c, bk=bk: eng.tensor_tensor(
                        tdst[:, c, :], raw[:, c, :], ctx.ps[bk][:, :], ALU.mult)])
                bc_done = (ctx.evS[bk], ctx.evS[bk].n)
        w8(pg, "gpsimd", *bc_done)
        for (src_t, dst_t) in ((tq, tqp), (tk, tkp)):
            for (d0, s0_) in ((0, 32), (32, 0), (64, 96), (96, 64)):
                dma(pg, "gpsimd", dst_t[d0:d0 + 32, :, :],
                    src_t[s0_:s0_ + 32, :, :], sRot)
        w8(pg, "vector", sRot, sRot.n)
        w8(pg, "vector", sTb[b], sTb0[b] + 64 * (i // 2 + 1))

        def rot(eng, b=b):
            CQ = tabs[b][0].ap().unsqueeze(1).broadcast_to([128, 8, TT])
            SQ = tabs[b][1].ap().unsqueeze(1).broadcast_to([128, 8, TT])
            CK = tabs[b][2].ap().unsqueeze(1).broadcast_to([128, 8, TT])
            SK = tabs[b][3].ap().unsqueeze(1).broadcast_to([128, 8, TT])
            eng.tensor_tensor(tq[:, :, :], tq[:, :, :], CQ, ALU.mult)
            eng.tensor_tensor(tqp[:, :, :], tqp[:, :, :], SQ, ALU.mult)
            eng.tensor_tensor(tq[:, :, :], tq[:, :, :], tqp[:, :, :], ALU.add)
            eng.tensor_tensor(tk[:, :, :], tk[:, :, :], CK, ALU.mult)
            eng.tensor_tensor(tkp[:, :, :], tkp[:, :, :], SK, ALU.mult)
            return eng.tensor_tensor(tk[:, :, :], tk[:, :, :], tkp[:, :, :],
                                     ALU.add)
        pg.em("vector", rot)
        dv5 = tincL(ctx, pg, "vector")
        for u in range(4):
            for half in range(2):
                bk = nb()
                pe_group(ctx, pg, bk, [
                    (lambda eng, u=u, half=half, cc2=cc2, bk=bk: nc.tensor.matmul(
                        ctx.ps[bk][:, 128 * cc2:128 * (cc2 + 1)],
                        vraw[:, 4 * half + cc2, 128 * u:128 * (u + 1)],
                        cst.identB[:, :], start=(cc2 == 0), stop=(cc2 == 3)))
                    for cc2 in range(4)])
                consume(ctx, pg, "vector", bk, [
                    lambda eng, u=u, half=half, bk=bk: eng.tensor_copy(
                        avs[:, u, 8 * half:8 * (half + 1), 0:DH],
                        ctx.ps[bk].ap().rearrange("p (h e) -> p h e", h=8))])
                avs_done = (ctx.evS[bk], ctx.evS[bk].n)
                pe_mark[i] = (ctx.mmS[bk], ctx.mmS[bk].n)
        w8(pg, "sync", *avs_done)
        w8(pg, "sync", tDV, dv5)
        s0 = TT * i
        dma(pg, "sync", fm_slice(dr["aqnS"], s0, TT), tq[:, :, :], sSt)
        dma(pg, "sync", fm_slice(dr["aknS"], s0, TT), tk[:, :, :], sSt)
        dma(pg, "sync", dr["avtS"].ap()[4 * i:4 * (i + 1), :, :, :].transpose(
            [1, 0, 2, 3]), avs[:, :, :, :], sSt)
        dvends.append(tDV.n)
    w8(pg, "sync", sSt, sSt.n)
    w8(pg, "gpsimd", sRot, sRot.n)
    tinc(ctx, pg, "vector")
    tinc(ctx, pg, "scalar")
    tinc(ctx, pg, "tensor")


def ph_ATTN2(ctx, pg, layer=0):
    nc, dr, cst, sb = ctx.nc, ctx.dr, ctx.cst, ctx.sb
    aqn = [sb(f"t2_q{b}", (128, 8, WIN), BF16) for b in range(2)]
    akn = [sb(f"t2_k{b}", (128, 8, 2 * WIN), BF16) for b in range(2)]
    avt = [sb(f"t2_v{b}", (128, 4, HEADS, DH + 1), BF16) for b in range(2)]
    Pb2 = [sb(f"t2_p{b}", (128, 2, 512), BF16) for b in range(2)]
    AOr = sb("t2_aor", (128, 8, WIN), BF16)
    AOn = sb("t2_aon", (128, 8, WIN), BF16)
    LmatR = sb("t2_l", (1, HEADS * WIN))
    rcpR = sb("t2_rcf", (1, HEADS * WIN))
    rcpRb = sb("t2_rcb", (1, HEADS * WIN), BF16)
    rcp16 = sb("t2_rcp", (HEADS, WIN), BF16)
    ost = sb("t2_ost", (128, 8, WIN))
    sQ = [ctx.dsemf(f"t2q{b}") for b in range(2)]
    sK = [ctx.dsemf(f"t2k{b}") for b in range(2)]
    sV = [ctx.dsemf(f"t2v{b}") for b in range(2)]
    sO = ctx.dsemf("t2o")
    sLr = ctx.dsemf("t2lr")
    sQ0 = [s.n for s in sQ]
    sK0 = [s.n for s in sK]
    sV0 = [s.n for s in sV]
    sO0 = sO.n
    tPE, tDV, tAC = ctx.tile["tensor"], ctx.tile["vector"], ctx.tile["scalar"]
    nb = bank_cycler()
    NWT = 2 * NWIN
    pe_mark = {}

    def load(i):
        if i >= NWT:
            return
        b = i % 2
        seq, w = divmod(i, NWIN)
        s0 = seq * SEQ + w * WIN
        if i >= 2:
            w8(pg, "sync", *pe_mark[i - 2])
        dma(pg, "sync", aqn[b][:, :, :], fm_slice(dr["aqnS"], s0, WIN), sQ[b])
        if w == 0:
            dma(pg, "sync", akn[b][:, :, 0:WIN],
                fm_slice(dr["aknS"], s0, WIN), sK[b])
            dma(pg, "sync", avt[b][:, 0:2, :, :],
                dr["avtS"].ap()[2 * i:2 * i + 2].transpose([1, 0, 2, 3]), sV[b])
        else:
            dma(pg, "sync", akn[b][:, :, :],
                fm_slice(dr["aknS"], s0 - WIN, 2 * WIN), sK[b])
            dma(pg, "sync", avt[b][:, :, :, :],
                dr["avtS"].ap()[2 * i - 2:2 * i + 2].transpose([1, 0, 2, 3]),
                sV[b])

    load(0)
    for i in range(NWT):
        b = i % 2
        seq, w = divmod(i, NWIN)
        w0 = (w == 0)
        load(i + 1)
        nq = 16 * (i // 2 + 1)
        w8(pg, "tensor", sQ[b], sQ0[b] + nq)
        w8(pg, "tensor", sK[b], sK0[b] + nq)
        w8(pg, "tensor", sV[b], sV0[b] + nq)
        scs = (0, 1) if w0 else (0, 1, 2, 3)
        gs = (0,) if w0 else (0, 1)
        lm_mark = None
        for h in range(HEADS):
            hp = h % 2
            pb = 64 * (h % 2)
            hc = h // 2
            exp_mark = None
            for g in gs:
                bk = nb()
                pe_group(ctx, pg, bk, [
                    (lambda eng, sc=2 * g + d, d=d, bk=bk, hc=hc, pb=pb, b=b:
                     nc.tensor.matmul(
                        ctx.ps[bk][:, 256 * d:256 * (d + 1)],
                        akn[b][pb:pb + 64, hc, 128 * sc:128 * (sc + 1)],
                        aqn[b][pb:pb + 64, hc, :],
                        start=(d == 0), stop=(d == 1)))
                    for d in range(2)])
                consume(ctx, pg, "scalar", bk, [
                    lambda eng, g=g, bk=bk, hp=hp: eng.activation(
                        Pb2[hp][:, g, :], ctx.ps[bk][:, :], AF.Exp)])
                exp_mark = (ctx.evS[bk], ctx.evS[bk].n)
            w8(pg, "vector", *exp_mark)
            for g in gs:
                mask = cst.maskTBb if (w0 or g == 1) else cst.maskTAb
                pg.em("vector", lambda eng, g=g, hp=hp, mask=mask:
                      eng.tensor_tensor(Pb2[hp][:, g, :], Pb2[hp][:, g, :],
                                        mask[:, :], ALU.min))
            pmin = tincL(ctx, pg, "vector")
            w8(pg, "tensor", tDV, pmin)
            bkv = nb()
            pe_group(ctx, pg, bkv, [
                (lambda eng, sc=sc, z=z, bkv=bkv, b=b, h=h, hp=hp:
                 nc.tensor.matmul(
                    ctx.ps[bkv][0:DH + 1, 0:WIN],
                    avt[b][:, sc, h, :],
                    Pb2[hp][:, sc // 2, 256 * (sc % 2):256 * (sc % 2) + 256],
                    start=(z == 0), stop=(z == len(scs) - 1)))
                for z, sc in enumerate(scs)])
            consume(ctx, pg, "scalar", bkv, [
                (lambda eng, pb=pb, hc=hc, bkv=bkv: eng.copy(
                    AOr[pb:pb + 64, hc, :], ctx.ps[bkv][0:DH, 0:WIN])),
                (lambda eng, h=h, bkv=bkv: eng.copy(
                    LmatR[0:1, WIN * h:WIN * (h + 1)],
                    ctx.ps[bkv][DH:DH + 1, 0:WIN])),
            ])
            lm_mark = (ctx.evS[bkv], ctx.evS[bkv].n)
        w8(pg, "vector", sLr, sLr.n)
        w8(pg, "vector", *lm_mark)
        pg.em("vector", lambda eng: eng.reciprocal(rcpR[:, :], LmatR[:, :]))
        pg.em("vector", lambda eng: eng.tensor_copy(rcpRb[:, :], rcpR[:, :]))
        rc_mark = tincL(ctx, pg, "vector")
        w8(pg, "gpsimd", tDV, rc_mark)
        dma(pg, "gpsimd", dr["rrow"].ap().rearrange("(o k) -> o k", o=1),
            rcpRb[:, :], sLr)
        w8(pg, "gpsimd", sLr, sLr.n)
        dma(pg, "gpsimd", rcp16[:, :],
            dr["rrow"].ap().rearrange("(h q) -> h q", h=HEADS), sLr)
        w8(pg, "tensor", sLr, sLr.n)
        aon_mark = None
        for c in range(8):
            bk = nb()
            pe_group(ctx, pg, bk, [
                lambda eng, c=c, bk=bk: nc.tensor.matmul(
                    ctx.ps[bk][:, 0:WIN], cst.HselT[:, c, :], rcp16[:, :],
                    start=True, stop=True)])
            consume(ctx, pg, "vector", bk, [
                lambda eng, c=c, bk=bk: eng.tensor_tensor(
                    AOn[:, c, :], AOr[:, c, :], ctx.ps[bk][:, 0:WIN],
                    ALU.mult)])
            aon_mark = (ctx.evS[bk], ctx.evS[bk].n)
        w8(pg, "tensor", *aon_mark)
        w8(pg, "vector", sO, sO0 + 16 * i)
        ost_done = None
        for m in range(8):
            bk = nb()
            pe_group(ctx, pg, bk, [
                (lambda eng, kc=kc, m=m, bk=bk: nc.tensor.matmul(
                    ctx.ps[bk][:, 0:WIN], ctx.Wo_l[:, kc, 128 * m:128 * (m + 1)],
                    AOn[:, kc, :], start=(kc == 0), stop=(kc == 7)))
                for kc in range(8)])
            consume(ctx, pg, "vector", bk, [
                lambda eng, m=m, bk=bk: eng.tensor_copy(ost[:, m, :],
                                                        ctx.ps[bk][:, 0:WIN])])
            ost_done = (ctx.evS[bk], ctx.evS[bk].n)
            if m == 7:
                pe_mark[i] = (ctx.mmS[bk], ctx.mmS[bk].n)
        w8(pg, "gpsimd", *ost_done)
        s0 = seq * SEQ + w * WIN
        dma(pg, "gpsimd", fm_slice(dr["R"], s0, WIN), ost[:, :, :], sO,
            accum_op=ALU.add)
    w8(pg, "gpsimd", sO, sO.n)
    tinc(ctx, pg, "vector")
    tinc(ctx, pg, "scalar")
    tinc(ctx, pg, "tensor")


def ph_FFN0(ctx, pg, layer=0):
    nc, dr, cst, sb = ctx.nc, ctx.dr, ctx.cst, ctx.sb
    TT, NT = 512, 16
    xsb = [sb(f"f0_xs{b}", (128, 8, TT + 1), BF16) for b in range(2)]
    sqb = sb("f0_sq", (128, 8, TT + 1), BF16)
    muR = sb("f0_mu", (1, TT))
    muR2 = sb("f0_mu2", (1, TT))
    veR = sb("f0_ve", (1, TT))
    veE = sb("f0_vee", (1, TT))
    yR = sb("f0_y", (1, TT))
    tR = sb("f0_t", (1, TT))
    arL = sb("f0_ar", (1, TT), BF16)
    amL = sb("f0_am", (1, TT), BF16)
    sX = [ctx.dsemf(f"f0x{b}") for b in range(2)]
    sRw = ctx.dsemf("f0rw")
    sX0 = [s.n for s in sX]
    sRw0 = sRw.n
    tPE, tDV, tAC = ctx.tile["tensor"], ctx.tile["vector"], ctx.tile["scalar"]
    pe_base = tPE.n

    def load(i):
        if i >= NT:
            return
        b = i % 2
        w8(pg, "gpsimd", tPE, pe_base + max(0, i - 1))
        s0 = TT * i
        if i % 8 == 0:
            dma(pg, "gpsimd", xsb[b][:, :, 1:TT + 1],
                fm_slice(dr["R"], s0, TT), sX[b])
        else:
            dma(pg, "gpsimd", xsb[b][:, :, :],
                fm_slice(dr["R"], s0 - 1, TT + 1), sX[b])

    load(0)
    for i in range(NT):
        b = i % 2
        load(i + 1)
        if i % 8 == 0:
            pg.em("vector", lambda eng, b=b: eng.memset(xsb[b][:, :, 0:1], 0.0))
            dvm = tinc(ctx, pg, "vector")
            w8(pg, "scalar", tDV, dvm)
        w8(pg, "scalar", sX[b], sX0[b] + 16 * (i // 2 + 1))
        w8(pg, "scalar", tPE, pe_base + max(0, i))
        pg.em("scalar", lambda eng, b=b: eng.activation(
            sqb.ap().rearrange("p c s -> p (c s)"),
            xsb[b].ap().rearrange("p c s -> p (c s)"), AF.Square))
        ac1 = tinc(ctx, pg, "scalar")
        w8(pg, "tensor", tAC, ac1)
        pe_group(ctx, pg, 0, [
            (lambda eng, c=c, b=b: nc.tensor.matmul(
                ctx.ps[0][0:1, :], cst.onesColB[:, :],
                xsb[b][:, c, 1:TT + 1] if c < 4 else xsb[b][:, c, 0:TT],
                start=(c == 0), stop=(c == 7))) for c in range(8)])
        pe_group(ctx, pg, 1, [
            (lambda eng, c=c: nc.tensor.matmul(
                ctx.ps[1][0:1, :], cst.onesColB[:, :],
                sqb[:, c, 1:TT + 1] if c < 4 else sqb[:, c, 0:TT],
                start=(c == 0), stop=(c == 7))) for c in range(8)])

        def rows0(eng):
            eng.tensor_scalar(muR[:, :], ctx.ps[0][0:1, :], 1.0 / D, None,
                              ALU.mult)
            return eng.tensor_scalar(muR2[:, :], ctx.ps[0][0:1, :], 1.0 / D,
                                     None, ALU.mult)

        def rows1(eng):
            eng.tensor_scalar(veR[:, :], ctx.ps[1][0:1, :], 1.0 / D, None,
                              ALU.mult)
            eng.tensor_mul(tR[:, :], muR[:, :], muR2[:, :])
            eng.tensor_sub(veR[:, :], veR[:, :], tR[:, :])
            return eng.tensor_scalar(veE[:, :], veR[:, :], LN_EPS, None, ALU.add)
        consume(ctx, pg, "vector", 0, [rows0])
        consume(ctx, pg, "vector", 1, [rows1])
        dv1 = tinc(ctx, pg, "vector")
        w8(pg, "scalar", tDV, dv1)
        pg.em("scalar", lambda eng: eng.activation(
            yR[:, :], veR[:, :], AF.Ln, bias=cst.epsCol[0:1, :]))
        pg.em("scalar", lambda eng: eng.activation(
            yR[:, :], yR[:, :], AF.Exp, scale=-0.5))
        ac2 = tinc(ctx, pg, "scalar")
        w8(pg, "vector", tAC, ac2)
        rsqrt_nr(pg, yR[:, :], veE[:, :], tR[:, :])
        w8(pg, "vector", sRw, sRw0 + 32 * i)

        def rows2(eng):
            eng.tensor_mul(tR[:, :], muR[:, :], yR[:, :])
            eng.tensor_scalar(amL[:, :], tR[:, :], -1.0, None, ALU.mult)
            return eng.tensor_copy(arL[:, :], yR[:, :])
        pg.em("vector", rows2)
        dv2 = tinc(ctx, pg, "vector")
        tinc(ctx, pg, "tensor")
        w8(pg, "sync", tDV, dv2)
        s0 = TT * i
        dma(pg, "sync", dr["rows"].ap()[0:1, s0:s0 + TT], arL[:, :], sRw)
        dma(pg, "sync", dr["rows"].ap()[1:2, s0:s0 + TT], amL[:, :], sRw)
    w8(pg, "sync", sRw, sRw.n)


def ph_FFN1(ctx, pg, layer=0):
    nc, dr, cst, sb = ctx.nc, ctx.dr, ctx.cst, ctx.sb
    TT, NT = 512, 16
    W1a = sb("f1_w1a", (128, 8, DI), BF16)
    W1g = sb("f1_w1g", (128, 8, DI), BF16)
    W2p = sb("f1_w2", (128, NJ, D), BF16)
    b1a = sb("f1_b1a", (128, NJ))
    b1g = sb("f1_b1g", (128, NJ))
    b2c = sb("f1_b2", (128, 8))
    lng = sb("f1_g", (128, 8))
    lnb = sb("f1_b", (128, 8))
    rwr = [sb(f"f1_rr{b}", (1, TT), BF16) for b in range(2)]
    rwm = [sb(f"f1_rm{b}", (1, TT), BF16) for b in range(2)]
    xsb = sb("f1_xs", (128, 8, TT + 1), BF16)
    xn = sb("f1_xn", (128, 8, TT), BF16)
    gelG = sb("f1_gg", (128, TT), BF16)
    u = sb("f1_u", (128, NJ, TT), BF16)
    ost = sb("f1_ost", (128, 8, TT // 2))
    sW = ctx.dsemf("f1W")
    sWs = ctx.dsemf("f1Ws")
    sX = ctx.dsemf("f1x")
    sRw = [ctx.dsemf(f"f1rw{b}") for b in range(2)]
    sO = ctx.dsemf("f1o")
    sX0 = sX.n
    sRw0 = [s.n for s in sRw]
    sO0 = sO.n
    tPE, tDV, tAC = ctx.tile["tensor"], ctx.tile["vector"], ctx.tile["scalar"]
    nb = bank_cycler()
    dma(pg, "gpsimd", W1a[:, :, :], dr[f"W1a{layer}"].ap().rearrange(
        "(c p) n -> p c n", p=128), sW)
    dma(pg, "gpsimd", W1g[:, :, :], dr[f"W1g{layer}"].ap().rearrange(
        "(c p) n -> p c n", p=128), sW)
    dma(pg, "gpsimd", W2p[:, :, :], dr[f"W2p{layer}"].ap().rearrange(
        "(j p) n -> p j n", p=128), sW)
    dma(pg, "sync", b1a[:, :], dr[f"b1a{layer}"].ap(), sWs)
    dma(pg, "sync", b1g[:, :], dr[f"b1g{layer}"].ap(), sWs)
    dma(pg, "sync", b2c[:, :], dr[f"b2c{layer}"].ap(), sWs)
    dma(pg, "sync", lng[:, :], dr[f"lng{layer}"].ap(), sWs)
    dma(pg, "sync", lnb[:, :], dr[f"lnb{layer}"].ap(), sWs)
    w8(pg, "tensor", sW, sW.n)
    w8(pg, "tensor", sWs, sWs.n)
    pe_base = tPE.n
    dv2s = []

    def loadx(i):
        if i >= NT:
            return
        s0 = TT * i
        if i >= 1:
            w8(pg, "gpsimd", tDV, dv2s[i - 1])
        if i % 8 == 0:
            dma(pg, "gpsimd", xsb[:, :, 1:TT + 1],
                fm_slice(dr["R"], s0, TT), sX)
        else:
            dma(pg, "gpsimd", xsb[:, :, :],
                fm_slice(dr["R"], s0 - 1, TT + 1), sX)

    bc_mark = {}

    def loadrows(i):
        if i >= NT:
            return
        b = i % 2
        s0 = TT * i
        if i >= 2:
            w8(pg, "sync", *bc_mark[i - 2])
        dma(pg, "sync", rwr[b][:, :], dr["rows"].ap()[0:1, s0:s0 + TT], sRw[b])
        dma(pg, "sync", rwm[b][:, :], dr["rows"].ap()[1:2, s0:s0 + TT], sRw[b])

    loadx(0)
    loadrows(0)
    for i in range(NT):
        b = i % 2
        s0 = TT * i
        loadrows(i + 1)
        if i % 8 == 0:
            pg.em("vector", lambda eng: eng.memset(xsb[:, :, 0:1], 0.0))
        w8(pg, "tensor", sRw[b], sRw0[b] + 32 * (i // 2 + 1))
        bR, bM = nb(), nb()
        pe_group(ctx, pg, bR, [lambda eng, bR=bR, b=b: nc.tensor.matmul(
            ctx.ps[bR][:, :], cst.onesRowB[:, :], rwr[b][:, :],
            start=True, stop=True)])
        pe_group(ctx, pg, bM, [lambda eng, bM=bM, b=b: nc.tensor.matmul(
            ctx.ps[bM][:, :], cst.onesRowB[:, :], rwm[b][:, :],
            start=True, stop=True)])
        bc_mark[i] = (ctx.mmS[bM], ctx.mmS[bM].n)
        w8(pg, "vector", sX, sX0 + 16 * (i + 1))
        fnsR = []
        for c in range(8):
            off = 1 if c < 4 else 0
            fnsR.append(lambda eng, c=c, off=off, bR=bR: eng.tensor_tensor(
                xn[:, c, :], xsb[:, c, off:off + TT], ctx.ps[bR][:, :],
                ALU.mult))
        consume(ctx, pg, "vector", bR, fnsR)
        fnsM = []
        for c in range(8):
            fnsM.append(lambda eng, c=c, bM=bM: eng.tensor_tensor(
                xn[:, c, :], xn[:, c, :], ctx.ps[bM][:, :], ALU.add))
        consume(ctx, pg, "vector", bM, fnsM)
        for c in range(8):
            pg.em("vector", lambda eng, c=c: eng.tensor_scalar(
                xn[:, c, :], xn[:, c, :], lng[:, c:c + 1], lnb[:, c:c + 1],
                ALU.mult, ALU.add))
        dv2 = tincL(ctx, pg, "vector")
        dv2s.append(dv2)
        loadx(i + 1)
        w8(pg, "tensor", tDV, dv2)
        gel_free = None
        for j in range(NJ):
            bA, bG = nb(), nb()
            pe_group(ctx, pg, bA, [
                (lambda eng, kc=kc, j=j, bA=bA: nc.tensor.matmul(
                    ctx.ps[bA][:, :], W1a[:, kc, 128 * j:128 * (j + 1)],
                    xn[:, kc, :], start=(kc == 0), stop=(kc == 7)))
                for kc in range(8)])
            pe_group(ctx, pg, bG, [
                (lambda eng, kc=kc, j=j, bG=bG: nc.tensor.matmul(
                    ctx.ps[bG][:, :], W1g[:, kc, 128 * j:128 * (j + 1)],
                    xn[:, kc, :], start=(kc == 0), stop=(kc == 7)))
                for kc in range(8)])
            if gel_free is not None:
                w8(pg, "scalar", gel_free[0], gel_free[1])
            consume(ctx, pg, "scalar", bG, [
                lambda eng, j=j, bG=bG: eng.activation(
                    gelG[:, :], ctx.ps[bG][:, :], AF.Gelu,
                    bias=b1g[:, j:j + 1])])
            w8(pg, "vector", ctx.evS[bG], ctx.evS[bG].n)
            consume(ctx, pg, "vector", bA, [
                lambda eng, j=j, bA=bA: eng.scalar_tensor_tensor(
                    u[:, j, :], ctx.ps[bA][:, :], b1a[:, j:j + 1], gelG[:, :],
                    ALU.add, ALU.mult)])
            gel_free = (ctx.evS[bA], ctx.evS[bA].n)
        u_done = gel_free
        w8(pg, "tensor", *u_done)
        bks = [nb() for _ in range(8)]
        for m in range(8):
            pe_group(ctx, pg, bks[m], [
                (lambda eng, j=j, m=m, bk=bks[m]: nc.tensor.matmul(
                    ctx.ps[bk][:, :], W2p[:, j, 128 * m:128 * (m + 1)],
                    u[:, j, :], start=(j == 0), stop=(j == NJ - 1)))
                for j in range(NJ)])
        w8(pg, "vector", sO, sO0 + 32 * i)
        for m in range(8):
            w8(pg, "vector", ctx.mmS[bks[m]], ctx.fills[bks[m]])
            pg.em("vector", lambda eng, m=m, bk=bks[m]: eng.tensor_scalar(
                ost[:, m, :], ctx.ps[bk][:, 0:TT // 2], b2c[:, m:m + 1],
                None, ALU.add))
        dvA = tincL(ctx, pg, "vector")
        w8(pg, "gpsimd", tDV, dvA)
        if i + 1 < NT:
            w8(pg, "gpsimd", sX, sX0 + 16 * (i + 2))
        dma(pg, "gpsimd", fm_slice(dr["R"], s0, TT // 2), ost[:, :, :], sO,
            accum_op=ALU.add)
        w8(pg, "vector", sO, sO0 + 32 * i + 16)
        ostB_done = None
        for m in range(8):
            consume(ctx, pg, "vector", bks[m], [
                lambda eng, m=m, bk=bks[m]: eng.tensor_scalar(
                    ost[:, m, :], ctx.ps[bk][:, TT // 2:TT], b2c[:, m:m + 1],
                    None, ALU.add)])
            ostB_done = (ctx.evS[bks[m]], ctx.evS[bks[m]].n)
        w8(pg, "gpsimd", *ostB_done)
        dma(pg, "gpsimd", fm_slice(dr["R"], s0 + TT // 2, TT // 2),
            ost[:, :, :], sO, accum_op=ALU.add)
    w8(pg, "gpsimd", sO, sO.n)
    tinc(ctx, pg, "vector")
    tinc(ctx, pg, "scalar")
    tinc(ctx, pg, "tensor")


def ph_FIN(ctx, pg, y=None):
    nc, dr, cst, sb = ctx.nc, ctx.dr, ctx.cst, ctx.sb
    TT, NT = 512, 16
    rt = [sb(f"fn_r{b}", (128, 8, TT)) for b in range(2)]
    r1t = [sb(f"fn_r1{b}", (128, 8, TT)) for b in range(2)]
    sm = sb("fn_sm", (128, 8, TT))
    yt = sb("fn_yt", (128, 4, D))
    sR = [ctx.dsemf(f"fnr{b}") for b in range(2)]
    sY = ctx.dsemf("fny")
    tPE, tDV, tAC = ctx.tile["tensor"], ctx.tile["vector"], ctx.tile["scalar"]
    pe_base, dv_base = tPE.n, tDV.n
    nb = bank_cycler()

    def load(i):
        if i >= NT:
            return
        b = i % 2
        w8(pg, "sync", tDV, dv_base + max(0, i - 1))
        dma(pg, "sync", rt[b][:, :, :], fm_slice(dr["R"], TT * i, TT), sR[b])
        dma(pg, "sync", r1t[b][:, :, :], fm_slice(dr["R1"], TT * i, TT), sR[b])

    load(0)
    for i in range(NT):
        b = i % 2
        load(i + 1)
        w8(pg, "vector", sR[b], 32 * (i // 2 + 1))
        w8(pg, "vector", tPE, pe_base + max(0, i))
        pg.em("vector", lambda eng, b=b: eng.tensor_tensor(
            sm.ap().rearrange("p c s -> p (c s)"),
            rt[b].ap().rearrange("p c s -> p (c s)"),
            r1t[b].ap().rearrange("p c s -> p (c s)"), ALU.add))
        dv1 = tinc(ctx, pg, "vector")
        w8(pg, "tensor", tDV, dv1)
        w8(pg, "scalar", sY, 16 * i)
        for a in range(4):
            for half in range(2):
                bk = nb()
                pe_group(ctx, pg, bk, [
                    (lambda eng, a=a, c=4 * half + cc, cc=cc, bk=bk:
                     nc.tensor.matmul(
                        ctx.ps[bk][:, 128 * cc:128 * (cc + 1)],
                        sm[:, c, 128 * a:128 * (a + 1)],
                        cst.identF[:, :], start=(cc == 0), stop=(cc == 3)))
                    for cc in range(4)])
                consume(ctx, pg, "scalar", bk, [
                    lambda eng, a=a, half=half, bk=bk: eng.copy(
                        yt[:, a, 512 * half:512 * (half + 1)],
                        ctx.ps[bk][:, :])])
        tinc(ctx, pg, "tensor")
        ac1 = tinc(ctx, pg, "scalar")
        w8(pg, "sync", tAC, ac1)
        dma(pg, "sync",
            y.ap()[TT * i:TT * (i + 1), :].rearrange("(a p) e -> p a e", p=128),
            yt[:, :, :], sY)
    w8(pg, "sync", sY, sY.n)


def kernel(**inputs):
    shared, shards = _host_prep(inputs)
    nc = build_program()
    in_maps = []
    for c in range(NCORES):
        m = dict(shared)
        m["x0"] = shards[c]
        in_maps.append(m)
    res = run_bass_kernel_spmd(nc, in_maps, core_ids=list(range(NCORES)))
    out = np.concatenate([np.asarray(res.results[c]["y"]).reshape(2, SEQ, D)
                          for c in range(NCORES)], axis=0)
    return out.reshape(16, SEQ, D)


# version 05
# speedup vs baseline: 1.2424x; 1.2424x over previous
import contextlib

import numpy as np

import concourse.bass as bass
import concourse.mybir as mybir
from concourse.bass_utils import run_bass_kernel_spmd

F32 = mybir.dt.float32
BF16 = mybir.dt.bfloat16
AF = mybir.ActivationFunctionType
ALU = mybir.AluOpType
AX = mybir.AxisListType

NCORES = 8
D = 1024
S = 8192
SEQ = 4096
NL = 6
HEADS = 16
DH = 64
WIN = 256
NWIN = SEQ // WIN
DI = 2816
NJ = DI // 128
ERR_CLIP = 5.0
GRAD_CLIP = 1.0
TTT_SCALE = 2.0 / 4096.0
LN_EPS = 1e-5
LN8 = float(np.log(8.0))
NEGM = -1.0e30
ENGINES = ("sync", "gpsimd", "vector", "scalar", "tensor")



def _host_consts():
    c = {}
    c["identF"] = np.eye(128, dtype=np.float32)
    c["onesCol"] = np.ones((128, 1), dtype=np.float32)
    c["epsCol"] = np.full((128, 1), LN_EPS, dtype=np.float32)
    c["ln8Col"] = np.full((128, 1), LN8, dtype=np.float32)
    c["onesRow"] = np.ones((1, 128), dtype=np.float32)
    hs = np.zeros((128, 8, HEADS), dtype=np.float32)
    for ch in range(8):
        hs[:64, ch, 2 * ch] = 1.0
        hs[64:, ch, 2 * ch + 1] = 1.0
    c["Hsel"] = hs
    p = np.arange(128)[:, None]
    j = np.arange(384)[None, :]
    c["maskW"] = np.where((j >= p) & (j <= p + 256), 0.0, NEGM).astype(np.float32)
    j2 = np.arange(256)[None, :]
    c["mask0a"] = np.where(j2 <= p, 0.0, NEGM).astype(np.float32)
    c["mask0b"] = np.where(j2 <= p + 128, 0.0, NEGM).astype(np.float32)
    BIG = 3.0e38
    qq = np.arange(256)[None, :]
    va = [qq <= p, (qq >= p - 128) & (qq <= p + 128), qq >= p, qq >= p + 128]
    c["maskTA"] = np.concatenate(
        [np.where(va[0], BIG, 0.0), np.where(va[1], BIG, 0.0)],
        axis=1).astype(np.float32)
    c["maskTB"] = np.concatenate(
        [np.where(va[2], BIG, 0.0), np.where(va[3], BIG, 0.0)],
        axis=1).astype(np.float32)
    hst = np.zeros((HEADS, 8, 128), dtype=np.float32)
    for ch in range(8):
        hst[2 * ch, ch, :64] = 1.0
        hst[2 * ch + 1, ch, 64:] = 1.0
    c["HselT"] = hst
    half = np.arange(0, DH, 2, dtype=np.float64)
    inv_freq = 1.0 / 10000.0 ** (half / DH)
    t = np.arange(SEQ, dtype=np.float64)
    freqs = np.concatenate([t[:, None] * inv_freq[None, :]] * 2, axis=1)
    base_scale = (half + 0.4 * DH) / (1.4 * DH)
    power = (t - SEQ // 2) / (WIN // 2)
    scale = np.concatenate([base_scale[None, :] ** power[:, None]] * 2, axis=1)
    cos, sin = np.cos(freqs), np.sin(freqs)
    sgn = np.ones(DH)
    sgn[: DH // 2] = -1.0
    for nm, tb in (("CQ", cos * scale), ("SQ", sin * scale * sgn[None, :]),
                   ("CK", cos / scale), ("SK", sin / scale * sgn[None, :])):
        tbT = tb.T
        c["rot" + nm] = np.ascontiguousarray(
            np.concatenate([tbT, tbT], axis=0).astype(np.float32))
    return c


def _chunk_cols(v, n):
    return np.ascontiguousarray(v.reshape(n, 128).T.astype(np.float32))


def _host_prep(inp):
    f = lambda a: np.ascontiguousarray(np.asarray(a, dtype=np.float32))
    tr = lambda a: np.ascontiguousarray(np.asarray(a, dtype=np.float32).T)
    sh = dict(_host_consts())
    sh["WkT"] = tr(inp["L0_attn_Wk"])
    sh["WvT"] = tr(inp["L0_attn_Wv"])
    sh["WqT"] = tr(inp["L0_attn_Wq"])
    sh["W0T"] = tr(inp["L0_attn_W0"])
    sh["W1T"] = tr(inp["L0_attn_W1"])
    sh["W1N"] = f(inp["L0_attn_W1"])
    sh["WoutT"] = tr(inp["L0_attn_Wout"])
    sh["Wg3"] = np.ascontiguousarray(np.concatenate(
        [f(inp["L0_attn_Wg_decay"]), f(inp["L0_attn_Wg_lr"]),
         f(inp["L0_attn_Wg_mom"])], axis=0).T)
    sh["bg3"] = np.ascontiguousarray(np.stack(
        [f(inp["L0_attn_bg_decay"]).ravel(), f(inp["L0_attn_bg_lr"]).ravel(),
         f(inp["L0_attn_bg_mom"]).ravel()]).reshape(3, 1))
    for i in range(NL):
        sh[f"Wqkv{i}"] = f(inp["L0_ffn_Wqkv"][i])
        sh[f"Wo{i}"] = f(inp["L0_ffn_Wo"][i])
        w1 = f(inp["L0_ffn_W1"][i])
        sh[f"W1a{i}"] = np.ascontiguousarray(np.pad(w1[:, :2730], ((0, 0), (0, 86))))
        sh[f"W1g{i}"] = np.ascontiguousarray(np.pad(w1[:, 2730:], ((0, 0), (0, 86))))
        sh[f"W2p{i}"] = np.ascontiguousarray(np.pad(f(inp["L0_ffn_W2"][i]),
                                                    ((0, 86), (0, 0))))
        b1 = f(inp["L0_ffn_b1"][i])
        sh[f"b1a{i}"] = _chunk_cols(np.pad(b1[:2730], (0, 86)), NJ)
        sh[f"b1g{i}"] = _chunk_cols(np.pad(b1[2730:], (0, 86)), NJ)
        sh[f"b2c{i}"] = _chunk_cols(f(inp["L0_ffn_b2"][i]), 8)
        sh[f"lng{i}"] = _chunk_cols(f(inp["L0_ffn_ln_g"][i]), 8)
        sh[f"lnb{i}"] = _chunk_cols(f(inp["L0_ffn_ln_b"][i]), 8)
        mq = np.zeros((HEADS, D), dtype=np.float32)
        mk = np.zeros((HEADS, D), dtype=np.float32)
        for h in range(HEADS):
            mq[h, DH * h: DH * h + DH] = f(inp["L0_ffn_q_scale"][i])
            mk[h, DH * h: DH * h + DH] = f(inp["L0_ffn_k_scale"][i])
        sh[f"Mq{i}"] = mq
        sh[f"Mk{i}"] = mk
    x = np.asarray(inp["L0_attn_x"], dtype=np.float32)
    shards = [np.ascontiguousarray(x[2 * c: 2 * c + 2].reshape(S, D))
              for c in range(NCORES)]
    return sh, shards



class Sem:
    __slots__ = ("h", "n")

    def __init__(self, h):
        self.h = h
        self.n = 0


class Prog:
    def __init__(self):
        self.ops = {e: [] for e in ENGINES}
        self.anchor = {e: None for e in ENGINES}
        self.kind = {e: None for e in ENGINES}

    def em(self, e, fn):
        self.ops[e].append(fn)
        self.kind[e] = "plain"


class Ctx:
    pass


def w8(pg, e, sem, val):
    if val > 0:
        def fn(eng, s=sem.h, v=val):
            eng.wait_ge(s, v)
        pg.em(e, fn)
        pg.anchor[e] = ("wait",)
        pg.kind[e] = "wait"


def dma(pg, q, out, in_, sem, **kw):
    def fn(eng, o=out, i=in_, s=sem.h, kw=kw):
        eng.dma_start(out=o, in_=i, **kw).then_inc(s, 16)
    pg.em(q, fn)
    sem.n += 16
    pg.anchor[q] = (sem.h, sem.n)
    pg.kind[q] = "inc"
    return sem.n


def tinc(ctx, pg, e, key=None):
    sem = ctx.tile[key or e]

    def fn(eng, s=sem.h):
        return eng.drain().then_inc(s, 1)
    pg.em(e, fn)
    pg.anchor[e] = (sem.h, sem.n + 1)
    pg.kind[e] = "inc"
    sem.n += 1
    return sem.n


def tincL(ctx, pg, e, key=None):
    sem = ctx.tile[key or e]
    assert pg.kind[e] == "plain", pg.kind[e]
    f = pg.ops[e][-1]

    def wrap(eng, f=f, s=sem.h):
        return f(eng).then_inc(s, 1)
    pg.ops[e][-1] = wrap
    sem.n += 1
    pg.anchor[e] = (sem.h, sem.n)
    pg.kind[e] = "inc"
    return sem.n


def pe_group(ctx, pg, b, fns):
    w8(pg, "tensor", ctx.evS[b], ctx.frees[b])
    for f in fns[:-1]:
        pg.em("tensor", f)
    pg.em("tensor", lambda eng, f=fns[-1], s=ctx.mmS[b].h: f(eng).then_inc(s, 1))
    ctx.mmS[b].n += 1
    ctx.fills[b] += 1
    pg.anchor["tensor"] = (ctx.mmS[b].h, ctx.mmS[b].n)
    pg.kind["tensor"] = "inc"


def consume(ctx, pg, e, b, fns):
    w8(pg, e, ctx.mmS[b], ctx.fills[b])
    for f in fns[:-1]:
        pg.em(e, f)
    pg.em(e, lambda eng, f=fns[-1], s=ctx.evS[b].h: f(eng).then_inc(s, 1))
    ctx.evS[b].n += 1
    ctx.frees[b] += 1
    pg.anchor[e] = (ctx.evS[b].h, ctx.evS[b].n)
    pg.kind[e] = "inc"


def run_phase(ctx, pg):
    p = ctx.phase_idx
    ctx.phase_idx += 1
    blk = ctx.blk
    for name in ENGINES:
        sec = getattr(blk, name)

        anchor = pg.anchor[name]

        @sec
        def _(eng, name=name, p=p, ops=pg.ops[name], anchor=anchor):
            if p > 0:
                for s in ctx.ph.values():
                    eng.wait_ge(s.h, p)
            for fn in ops:
                fn(eng)
            if anchor is not None and anchor[0] != "wait":
                eng.wait_ge(anchor[0], anchor[1])
            eng.sem_inc(ctx.ph[name].h, 1)
    for name in ENGINES:
        ctx.ph[name].n += 1


def spacer(pg, ctx, e="vector"):
    pg.em(e, lambda eng: eng.tensor_scalar(
        ctx.junk[:, 0:256], ctx.junk[:, 0:256], 1.0, None, ALU.mult))


def rsqrt_nr(pg, y, ve, tmp, n=2):
    for _ in range(n):
        pg.em("vector", lambda eng, y=y, t=tmp: eng.tensor_mul(t, y, y))
        pg.em("vector", lambda eng, v=ve, t=tmp: eng.tensor_mul(t, t, v))
        pg.em("vector", lambda eng, t=tmp: eng.tensor_scalar(
            t, t, -0.5, 1.5, ALU.mult, ALU.add))
        pg.em("vector", lambda eng, y=y, t=tmp: eng.tensor_mul(y, y, t))


def fm_slice(t, s0, n):
    return t.ap()[:, :, s0:s0 + n].transpose([1, 0, 2])


def wchunks(dr_name, ctx):
    return ctx.dr[dr_name].ap().rearrange("(c p) n -> p c n", p=128)


def bank_cycler(start=0, n=8):
    i = [start]

    def nb():
        b = i[0] % n
        i[0] += 1
        return b
    return nb



def ph_P0(ctx, pg):
    dr, cst = ctx.dr, ctx.cst
    sG = ctx.dsemf("p0g")
    sS = ctx.dsemf("p0s")
    dma(pg, "gpsimd", cst.identB[:, :], dr["identF"].ap(), sG)
    dma(pg, "gpsimd", cst.onesColB[:, :], dr["onesCol"].ap(), sG)
    dma(pg, "gpsimd", cst.onesRowB[:, :], dr["onesRow"].ap(), sG)
    dma(pg, "gpsimd", cst.HselB[:, :, :], dr["Hsel"].ap(), sG)
    dma(pg, "gpsimd", cst.maskWb[:, :], dr["maskW"].ap(), sG)
    dma(pg, "gpsimd", cst.mask0ab[:, :], dr["mask0a"].ap(), sG)
    dma(pg, "gpsimd", cst.mask0bb[:, :], dr["mask0b"].ap(), sG)
    dma(pg, "gpsimd", cst.maskTAb[:, :], dr["maskTA"].ap(), sG)
    dma(pg, "gpsimd", cst.maskTBb[:, :], dr["maskTB"].ap(), sG)
    dma(pg, "gpsimd", cst.HselT[:, :, :], dr["HselT"].ap(), sG)
    dma(pg, "sync", cst.identF[:, :], dr["identF"].ap(), sS)
    dma(pg, "sync", cst.bg3[:, :], dr["bg3"].ap(), sS)
    dma(pg, "sync", cst.epsCol[:, :], dr["epsCol"].ap(), sS)
    dma(pg, "sync", cst.ln8Col[:, :], dr["ln8Col"].ap(), sS)
    w8(pg, "gpsimd", sG, sG.n)
    w8(pg, "sync", sS, sS.n)


def ph_A1(ctx, pg):
    import os
    nc, dr, cst, sb = ctx.nc, ctx.dr, ctx.cst, ctx.sb
    TT, NT = 512, 16
    if os.environ.get("DBG_NT1"):
        NT = 1
    Wk = sb("a1_Wk", (128, 8, D), BF16)
    Wv = sb("a1_Wv", (128, 8, D), BF16)
    Wq = sb("a1_Wq", (128, 8, D), BF16)
    xt = [sb(f"a1_xt{b}", (128, 4, D)) for b in range(2)]
    xnt = [sb(f"a1_xnt{b}", (128, 4, D), BF16) for b in range(2)]
    xnf = [sb(f"a1_xnf{b}", (128, 8, TT), BF16) for b in range(2)]
    rstage = sb("a1_rst", (128, 8, TT))
    qf = sb("a1_qf", (128, 8, TT), BF16)
    kf = sb("a1_kf", (128, 8, TT), BF16)
    vf = sb("a1_vf", (128, 8, TT), BF16)
    stS = sb("a1_sts", (128, 64))
    stQ = sb("a1_stq", (128, 64))
    stM = sb("a1_stm", (128, 64))
    stV = sb("a1_stv", (128, 64))
    stE = sb("a1_ste", (128, 64))
    stY = sb("a1_sty", (128, 64))
    stT = sb("a1_stt", (128, 64))
    gsum = sb("a1_gsum", (128, 64))
    sW = ctx.dsemf("a1W")
    sX = [ctx.dsemf(f"a1x{b}") for b in range(2)]
    sRst = ctx.dsemf("a1rs")
    sQKV = ctx.dsemf("a1qkv")
    tPE, tDV, tAC = ctx.tile["tensor"], ctx.tile["vector"], ctx.tile["scalar"]

    dma(pg, "gpsimd", Wk[:, :, :], wchunks("WkT", ctx), sW)
    dma(pg, "gpsimd", Wv[:, :, :], wchunks("WvT", ctx), sW)
    dma(pg, "gpsimd", Wq[:, :, :], wchunks("WqT", ctx), sW)
    pg.em("vector", lambda eng: eng.memset(ctx.sxn[:, :], 0.0))
    x0 = dr["x0"].ap()
    junk = ctx.junk
    w8(pg, "tensor", sW, 48)
    nb = bank_cycler()
    pe_base, dv_base, ac_base = tPE.n, tDV.n, tAC.n

    def load(i):
        if i >= NT:
            return
        b = i % 2
        w8(pg, "sync", tDV, dv_base + max(0, i - 1) * 4)
        w8(pg, "sync", tPE, pe_base + max(0, i - 1))
        w8(pg, "sync", tAC, ac_base + max(0, i - 1) * 4)
        dma(pg, "sync", xt[b][:, :, :],
            x0[TT * i: TT * (i + 1), :].rearrange("(a p) e -> p a e", p=128),
            sX[b])

    load(0)
    for i in range(NT):
        b = i % 2
        load(i + 1)
        X, XN = xt[b], xnt[b]
        nld = 16 * (i // 2 + 1)
        w8(pg, "scalar", sX[b], nld)
        for a in range(4):
            pg.em("scalar", lambda eng, X=X, a=a: eng.activation(
                junk[:, :D], X[:, a, :], AF.Copy, accum_out=stS[:, a:a + 1]))
            pg.em("scalar", lambda eng, X=X, a=a: eng.activation(
                junk[:, :D], X[:, a, :], AF.Square, accum_out=stQ[:, a:a + 1]))
        ac1 = tinc(ctx, pg, "scalar")
        w8(pg, "vector", tAC, ac1)

        def dve_stats(eng):
            eng.tensor_scalar(stM[:, :], stS[:, :], 1.0 / D, None, ALU.mult)
            eng.tensor_scalar(stV[:, :], stQ[:, :], 1.0 / D, None, ALU.mult)
            eng.tensor_mul(stT[:, :], stM[:, :], stM[:, :])
            eng.tensor_sub(stV[:, :], stV[:, :], stT[:, :])
            return eng.tensor_scalar(stE[:, :], stV[:, :], LN_EPS, None, ALU.add)
        pg.em("vector", dve_stats)
        dv1 = tinc(ctx, pg, "vector")
        w8(pg, "scalar", tDV, dv1)
        pg.em("scalar", lambda eng: eng.activation(
            stY[:, :], stV[:, :], AF.Ln, bias=cst.epsCol[:, :]))
        pg.em("scalar", lambda eng: eng.activation(
            stY[:, :], stY[:, :], AF.Exp, scale=-0.5))
        ac2 = tinc(ctx, pg, "scalar")
        w8(pg, "vector", tAC, ac2)
        rsqrt_nr(pg, stY[:, :], stE[:, :], stT[:, :])
        w8(pg, "vector", tPE, pe_base + max(0, i - 1))

        def dve_norm(eng, X=X, XN=XN):
            ins = None
            for a in range(4):
                ins = eng.tensor_scalar(XN[:, a, :], X[:, a, :], stM[:, a:a + 1],
                                        stY[:, a:a + 1], ALU.subtract, ALU.mult)
            return ins
        pg.em("vector", dve_norm)
        w8(pg, "tensor", sX[b], nld)
        w8(pg, "scalar", sRst, 16 * i)
        for c in range(8):
            bk = nb()
            pe_group(ctx, pg, bk, [
                (lambda eng, X=X, a=a, c=c, bk=bk: nc.tensor.matmul(
                    ctx.ps[bk][:, 128 * a:128 * (a + 1)],
                    X[:, a, 128 * c:128 * (c + 1)], cst.identF[:, :],
                    start=(a == 0), stop=(a == 3))) for a in range(4)])
            consume(ctx, pg, "scalar", bk, [
                lambda eng, c=c, bk=bk: eng.copy(rstage[:, c, :], ctx.ps[bk][:, :])])
        acR = tinc(ctx, pg, "scalar")
        w8(pg, "gpsimd", tAC, acR)
        dma(pg, "gpsimd", fm_slice(ctx.dr["R"], TT * i, TT), rstage[:, :, :], sRst)
        dv2 = tinc(ctx, pg, "vector")
        w8(pg, "tensor", tDV, dv2)
        for c in range(8):
            bk = nb()
            pe_group(ctx, pg, bk, [
                (lambda eng, XN=XN, a=a, c=c, bk=bk: nc.tensor.matmul(
                    ctx.ps[bk][:, 128 * a:128 * (a + 1)],
                    XN[:, a, 128 * c:128 * (c + 1)], cst.identB[:, :],
                    start=(a == 0), stop=(a == 3))) for a in range(4)])
            consume(ctx, pg, "scalar", bk, [
                lambda eng, c=c, bk=bk, b=b: eng.copy(
                    xnf[b][:, c, :], ctx.ps[bk][:, :])])
        acX = tinc(ctx, pg, "scalar")
        w8(pg, "vector", tAC, acX)
        pg.em("vector", lambda eng, b=b: eng.tensor_reduce(
            gsum[:, 0:8], xnf[b][:, :, :], AX.X, ALU.add))
        spacer(pg, ctx)
        pg.em("vector", lambda eng: eng.tensor_add(
            ctx.sxn[:, :], ctx.sxn[:, :], gsum[:, 0:8]))
        dv3 = tinc(ctx, pg, "vector")
        w8(pg, "tensor", tAC, acX)
        w8(pg, "tensor", tDV, dv3)
        first_ev = {"vector": True, "scalar": True}
        for t_i, (Wt, dst) in enumerate([(Wk, kf), (Wv, vf), (Wq, qf)]):
            for m in range(8):
                bk = nb()
                pe_group(ctx, pg, bk, [
                    (lambda eng, Wt=Wt, kc=kc, m=m, bk=bk, b=b: nc.tensor.matmul(
                        ctx.ps[bk][:, :], Wt[:, kc, 128 * m:128 * (m + 1)],
                        xnf[b][:, kc, :], start=(kc == 0), stop=(kc == 7)))
                    for kc in range(8)])
                e = "vector" if (m + t_i) % 2 == 0 else "scalar"
                if first_ev[e]:
                    w8(pg, e, sQKV, 16 * 3 * i)
                    first_ev[e] = False
                if e == "vector":
                    fn = (lambda eng, dst=dst, m=m, bk=bk: eng.tensor_copy(
                        dst[:, m, :], ctx.ps[bk][:, :]))
                else:
                    fn = (lambda eng, dst=dst, m=m, bk=bk: eng.copy(
                        dst[:, m, :], ctx.ps[bk][:, :]))
                consume(ctx, pg, e, bk, fn if isinstance(fn, list) else [fn])
        pe1 = tinc(ctx, pg, "tensor")
        dv4 = tinc(ctx, pg, "vector")
        ac3 = tinc(ctx, pg, "scalar")
        w8(pg, "gpsimd", tDV, dv4)
        w8(pg, "gpsimd", tAC, ac3)
        w8(pg, "gpsimd", ctx.tile["tensor"], pe1)

        dma(pg, "gpsimd", fm_slice(dr["kS"], TT * i, TT), kf[:, :, :], sQKV)
        dma(pg, "gpsimd", fm_slice(dr["vS"], TT * i, TT), vf[:, :, :], sQKV)
        dma(pg, "gpsimd", fm_slice(dr["qnS"], TT * i, TT), qf[:, :, :], sQKV)

    w8(pg, "gpsimd", sQKV, sQKV.n)
    w8(pg, "gpsimd", sRst, sRst.n)
    if os.environ.get("DBG_NT1"):
        sdbg = ctx.dsemf("a1dbg2")
        sdbg2 = ctx.dsemf("a1dbg3")
        w8(pg, "sync", ctx.tile["tensor"], ctx.tile["tensor"].n)
        w8(pg, "gpsimd", ctx.tile["tensor"], ctx.tile["tensor"].n)
        dma(pg, "gpsimd", dr["dbgA"].ap()[0:128, 0:4096],
            xnt[0].ap().rearrange("p a e -> p (a e)"), sdbg2)
        dma(pg, "gpsimd", dr["dbgA"].ap()[128:256, 0:4096],
            xnf[0].ap().rearrange("p c s -> p (c s)"), sdbg2)
        dma(pg, "sync", dr["dbgA"].ap()[256:384, 0:24],
            stat.ap().rearrange("p a k -> p (a k)"), sdbg)
        w8(pg, "sync", sdbg, sdbg.n)
        w8(pg, "gpsimd", sdbg2, sdbg2.n)



def build_program(upto=None, taps=()):
    nc = bass.Bass("TRN2", target_bir_lowering=False, debug=False)
    ctx = Ctx()
    ctx.nc = nc
    ctx.phase_idx = 0
    dr = {}

    def din(name, shape, dtype=F32):
        dr[name] = nc.dram_tensor(name, list(shape), dtype, kind="ExternalInput")

    def dint(name, shape, dtype=F32):
        dr[name] = nc.dram_tensor(name, list(shape), dtype, kind="Internal")

    din("x0", (S, D))
    for nm in ("WkT", "WvT", "WqT", "W0T", "W1T", "W1N", "WoutT"):
        din(nm, (D, D))
    din("Wg3", (D, 3))
    din("bg3", (3, 1))
    din("identF", (128, 128))
    din("onesCol", (128, 1))
    din("epsCol", (128, 1))
    din("ln8Col", (128, 1))
    din("onesRow", (1, 128))
    din("Hsel", (128, 8, HEADS))
    din("maskW", (128, 384))
    din("mask0a", (128, 256))
    din("mask0b", (128, 256))
    din("maskTA", (128, 512))
    din("maskTB", (128, 512))
    din("HselT", (HEADS, 8, 128))
    for nm in ("rotCQ", "rotSQ", "rotCK", "rotSK"):
        din(nm, (128, SEQ))
    for i in range(NL):
        din(f"Wqkv{i}", (D, 3 * D))
        din(f"Wo{i}", (D, D))
        din(f"W1a{i}", (D, DI))
        din(f"W1g{i}", (D, DI))
        din(f"W2p{i}", (DI, D))
        din(f"b1a{i}", (128, NJ))
        din(f"b1g{i}", (128, NJ))
        din(f"b2c{i}", (128, 8))
        din(f"lng{i}", (128, 8))
        din(f"lnb{i}", (128, 8))
        din(f"Mq{i}", (HEADS, D))
        din(f"Mk{i}", (HEADS, D))
    y = nc.dram_tensor("y", [S, D], F32, kind="ExternalOutput")
    dint("R", (8, 128, S))
    dint("R1", (8, 128, S))
    dint("kS", (8, 128, S), BF16)
    dint("vS", (8, 128, S), BF16)
    dint("qnS", (8, 128, S), BF16)
    dint("aqnS", (8, 128, S), BF16)
    dint("aknS", (8, 128, S), BF16)
    dint("avtS", (64, 128, HEADS, DH + 1), BF16)
    dint("dbgA", (384, 4096))
    dint("rows", (2, S), BF16)
    dint("rrow", (HEADS * WIN,), BF16)
    dint("ccI", (2051, D))
    dint("ccO", (2051, D))
    ctx.dr = dr

    tap_outs = {}
    for t in taps:
        h = dr[t]
        tap_outs[t] = nc.dram_tensor("tap_" + t, list(h.shape), h.dtype,
                                     kind="ExternalOutput")

    with contextlib.ExitStack() as stk:
        ctx.stk = stk
        nsem = [0]

        def new_sem(tag):
            nsem[0] += 1
            return Sem(stk.enter_context(nc.semaphore(f"s{nsem[0]}_{tag}")))

        ctx.new_sem = new_sem
        ctx.ps = [stk.enter_context(nc.psum_tensor(f"ps{i}", [128, 512], F32))
                  for i in range(8)]
        ctx.mmS = [new_sem(f"mm{i}") for i in range(8)]
        ctx.evS = [new_sem(f"ev{i}") for i in range(8)]
        ctx.fills = [0] * 8
        ctx.frees = [0] * 8
        ctx.ph = {e: new_sem("ph_" + e) for e in ENGINES}
        ctx.dsem = {}

        def dsemf(tag):
            if tag not in ctx.dsem:
                ctx.dsem[tag] = new_sem("d_" + tag)
            return ctx.dsem[tag]

        ctx.dsemf = dsemf
        ctx.tile = {e: new_sem("t_" + e) for e in
                    ("vector", "scalar", "tensor", "gpsimd", "sync")}
        ctx.ccS = new_sem("cc")
        ctx.scope = [stk]

        nsb = [0]

        def sb(name, shape, dtype=F32):
            nsb[0] += 1
            return ctx.scope[-1].enter_context(
                nc.sbuf_tensor(f"{name}_{nsb[0]}", list(shape), dtype))

        ctx.sb = sb

        with nc.Block() as blk:
            ctx.blk = blk
            cst = Ctx()
            cst.identF = stk.enter_context(nc.sbuf_tensor("c_idF", [128, 128], F32))
            cst.identB = stk.enter_context(nc.sbuf_tensor("c_idB", [128, 128], BF16))
            cst.onesColB = stk.enter_context(nc.sbuf_tensor("c_o1", [128, 1], BF16))
            cst.onesRowB = stk.enter_context(nc.sbuf_tensor("c_or", [1, 128], BF16))
            cst.HselB = stk.enter_context(nc.sbuf_tensor("c_hs", [128, 8, HEADS], BF16))
            cst.maskWb = stk.enter_context(nc.sbuf_tensor("c_mw", [128, 384], BF16))
            cst.mask0ab = stk.enter_context(nc.sbuf_tensor("c_m0a", [128, 256], BF16))
            cst.mask0bb = stk.enter_context(nc.sbuf_tensor("c_m0b", [128, 256], BF16))
            cst.maskTAb = stk.enter_context(nc.sbuf_tensor("c_mta", [128, 512], BF16))
            cst.maskTBb = stk.enter_context(nc.sbuf_tensor("c_mtb", [128, 512], BF16))
            cst.HselT = stk.enter_context(
                nc.sbuf_tensor("c_hst", [HEADS, 8, 128], BF16))
            cst.bg3 = stk.enter_context(nc.sbuf_tensor("c_bg3", [3, 1], F32))
            cst.epsCol = stk.enter_context(nc.sbuf_tensor("c_eps", [128, 1], F32))
            cst.ln8Col = stk.enter_context(nc.sbuf_tensor("c_ln8", [128, 1], F32))
            ctx.cst = cst
            ctx.junk = stk.enter_context(nc.sbuf_tensor("junk", [128, D], BF16))
            ctx.sxn = stk.enter_context(nc.sbuf_tensor("sxn", [128, 8], F32))

            done = [False]

            def phase(fn, **kw):
                if done[0]:
                    return
                with contextlib.ExitStack() as pscope:
                    ctx.scope.append(pscope)
                    pg = Prog()
                    fn(ctx, pg, **kw)
                    run_phase(ctx, pg)
                    ctx.scope.pop()
                name = fn.__name__ + str(kw.get("layer", ""))
                if upto is not None and name == upto:
                    done[0] = True

            phase(ph_P0)
            phase(ph_A1)
            phase(ph_A2)
            with contextlib.ExitStack() as wscope:
                ctx.W0nT = wscope.enter_context(
                    nc.sbuf_tensor("W0nT", [128, 8, D], BF16))
                ctx.W1nT = wscope.enter_context(
                    nc.sbuf_tensor("W1nT", [128, 8, D], BF16))
                ctx.WoutTb = wscope.enter_context(
                    nc.sbuf_tensor("WoutTb", [128, 8, D], BF16))
                phase(ph_AR)
                phase(ph_E)
            phase(ph_E1)
            for i in range(NL):
                with contextlib.ExitStack() as lscope:
                    ctx.Wo_l = lscope.enter_context(
                        nc.sbuf_tensor(f"Wo_l{i}", [128, 8, D], BF16))
                    phase(ph_ATTN1, layer=i)
                    phase(ph_ATTN2, layer=i)
                phase(ph_FFN0, layer=i)
                phase(ph_FFN1, layer=i)
            phase(ph_FIN, y=y)

            if upto is not None:
                pg = Prog()
                sst = dsemf("dump")
                dma(pg, "sync",
                    y.ap().rearrange("(c p a) e -> c p (a e)", c=8, p=128),
                    dr["R"].ap(), sst)
                for t, o in tap_outs.items():
                    dma(pg, "sync", o.ap(), dr[t].ap(), sst)
                w8(pg, "sync", sst, sst.n)
                run_phase(ctx, pg)
    return nc


def ph_A2(ctx, pg):
    nc, dr, cst, sb = ctx.nc, ctx.dr, ctx.cst, ctx.sb
    TT2, NT2 = 256, 32
    W0T = sb("a2_W0T", (128, 8, D), BF16)
    W1T = sb("a2_W1T", (128, 8, D), BF16)
    W1N = sb("a2_W1N", (128, 8, D), BF16)
    ctx.G1T = sb("a2_G1T", (128, 8, D))
    ctx.G0T = sb("a2_G0T", (128, 8, D))
    kf = [sb(f"a2_kf{b}", (128, 8, TT2), BF16) for b in range(2)]
    vf = [sb(f"a2_vf{b}", (128, 8, TT2), BF16) for b in range(2)]
    sg = sb("a2_sg", (128, 8, TT2), BF16)
    a1f = sb("a2_a1", (128, 8, TT2), BF16)
    sdf = sb("a2_sd", (128, 8, TT2), BF16)
    erf = sb("a2_er", (128, 8, TT2), BF16)
    d1f = sb("a2_d1", (128, 8, TT2), BF16)
    e1t = sb("a2_e1t", (128, TT2))
    t1t = sb("a2_t1t", (128, TT2), BF16)
    t2t = sb("a2_t2t", (128, TT2))
    kT = sb("a2_kT", (128, 2, 8, 128), BF16)
    a1T = sb("a2_a1T", (128, 2, 8, 128), BF16)
    erT = sb("a2_erT", (128, 2, 8, 128), BF16)
    d1T = sb("a2_d1T", (128, 2, 8, 128), BF16)
    sW = ctx.dsemf("a2W")
    sK = [ctx.dsemf(f"a2k{b}") for b in range(2)]
    sG = ctx.dsemf("a2G")
    tPE, tDV, tAC = ctx.tile["tensor"], ctx.tile["vector"], ctx.tile["scalar"]

    dma(pg, "gpsimd", W0T[:, :, :], wchunks("W0T", ctx), sW)
    dma(pg, "gpsimd", W1T[:, :, :], wchunks("W1T", ctx), sW)
    dma(pg, "gpsimd", W1N[:, :, :], wchunks("W1N", ctx), sW)
    pg.em("vector", lambda eng: eng.memset(ctx.G1T[:, :, :], 0.0))
    pg.em("vector", lambda eng: eng.memset(ctx.G0T[:, :, :], 0.0))
    w8(pg, "tensor", sW, sW.n)
    nb = bank_cycler()
    pe_base = tPE.n

    def load(i):
        if i >= NT2:
            return
        b = i % 2
        w8(pg, "sync", tPE, pe_base + max(0, i - 1))
        dma(pg, "sync", kf[b][:, :, :], fm_slice(dr["kS"], TT2 * i, TT2), sK[b])
        dma(pg, "sync", vf[b][:, :, :], fm_slice(dr["vS"], TT2 * i, TT2), sK[b])

    load(0)
    for i in range(NT2):
        b = i % 2
        load(i + 1)
        nld = 32 * (i // 2 + 1)
        w8(pg, "tensor", sK[b], nld)
        for m in range(8):
            bk = nb()
            pe_group(ctx, pg, bk, [
                (lambda eng, kc=kc, m=m, bk=bk, b=b: nc.tensor.matmul(
                    ctx.ps[bk][:, 0:TT2], W0T[:, kc, 128 * m:128 * (m + 1)],
                    kf[b][:, kc, :], start=(kc == 0), stop=(kc == 7)))
                for kc in range(8)])
            consume(ctx, pg, "scalar", bk, [
                lambda eng, m=m, bk=bk: eng.activation(
                    sg[:, m, :], ctx.ps[bk][:, 0:TT2], AF.Sigmoid)])
            acm = tinc(ctx, pg, "scalar")
            w8(pg, "vector", tAC, acm)

            def dve_sd(eng, m=m, bk=bk):
                ps = ctx.ps[bk][:, 0:TT2]
                eng.tensor_tensor(a1f[:, m, :], ps, sg[:, m, :], ALU.mult)
                eng.tensor_scalar(t1t[:, :], sg[:, m, :], -1.0, 1.0,
                                  ALU.mult, ALU.add)
                eng.tensor_tensor(t2t[:, :], ps, t1t[:, :], ALU.mult)
                return eng.scalar_tensor_tensor(
                    sdf[:, m, :], t2t[:, :], 1.0, sg[:, m, :], ALU.add, ALU.mult)
            consume(ctx, pg, "vector", bk, [dve_sd])
        dv_a = tinc(ctx, pg, "vector")
        w8(pg, "tensor", tDV, dv_a)
        for m in range(8):
            bk = nb()
            pe_group(ctx, pg, bk, [
                (lambda eng, kc=kc, m=m, bk=bk: nc.tensor.matmul(
                    ctx.ps[bk][:, 0:TT2], W1T[:, kc, 128 * m:128 * (m + 1)],
                    a1f[:, kc, :], start=(kc == 0), stop=(kc == 7)))
                for kc in range(8)])

            def dve_err(eng, m=m, bk=bk, b=b):
                eng.tensor_sub(e1t[:, :], ctx.ps[bk][:, 0:TT2], vf[b][:, m, :])
                return eng.tensor_scalar(erf[:, m, :], e1t[:, :], ERR_CLIP,
                                         -ERR_CLIP, ALU.min, ALU.max)
            consume(ctx, pg, "vector", bk, [dve_err])
        dv_b = tinc(ctx, pg, "vector")
        w8(pg, "tensor", tDV, dv_b)
        for m in range(8):
            bk = nb()
            pe_group(ctx, pg, bk, [
                (lambda eng, kc=kc, m=m, bk=bk: nc.tensor.matmul(
                    ctx.ps[bk][:, 0:TT2], W1N[:, kc, 128 * m:128 * (m + 1)],
                    erf[:, kc, :], start=(kc == 0), stop=(kc == 7)))
                for kc in range(8)])
            consume(ctx, pg, "vector", bk, [
                lambda eng, m=m, bk=bk: eng.tensor_tensor(
                    d1f[:, m, :], ctx.ps[bk][:, 0:TT2], sdf[:, m, :], ALU.mult)])
        dv_c = tinc(ctx, pg, "vector")
        w8(pg, "tensor", tDV, dv_c)
        for (srcT, dstT) in ((kf[b], kT), (a1f, a1T), (erf, erT), (d1f, d1T)):
            for c in range(8):
                bk = nb()
                pe_group(ctx, pg, bk, [
                    (lambda eng, srcT=srcT, c=c, u=u, bk=bk: nc.tensor.matmul(
                        ctx.ps[bk][:, 128 * u:128 * (u + 1)],
                        srcT[:, c, 128 * u:128 * (u + 1)], cst.identB[:, :],
                        start=(u == 0), stop=(u == 1))) for u in range(2)])
                consume(ctx, pg, "vector", bk, [
                    lambda eng, dstT=dstT, c=c, bk=bk: eng.tensor_copy(
                        dstT[:, :, c, :], ctx.ps[bk][:, 0:TT2])])
        dv_d = tinc(ctx, pg, "vector")
        w8(pg, "tensor", tDV, dv_d)
        for (lT, rT, acc) in ((a1T, erT, ctx.G1T), (kT, d1T, ctx.G0T)):
            for m in range(8):
                for h in range(2):
                    bk = nb()
                    pe_group(ctx, pg, bk, [
                        (lambda eng, lT=lT, rT=rT, m=m, h=h, u=u, bk=bk:
                         nc.tensor.matmul(
                             ctx.ps[bk][:, :],
                             lT[:, u, m, :],
                             rT[:, u, 4 * h:4 * (h + 1), :], start=(u == 0),
                             stop=(u == 1))) for u in range(2)])
                    consume(ctx, pg, "vector", bk, [
                        lambda eng, acc=acc, m=m, h=h, bk=bk: eng.tensor_tensor(
                            acc[:, m, 512 * h:512 * (h + 1)],
                            acc[:, m, 512 * h:512 * (h + 1)],
                            ctx.ps[bk][:, :], ALU.add)])
        tinc(ctx, pg, "tensor")
    dv_e = tinc(ctx, pg, "vector")
    w8(pg, "sync", tDV, dv_e)
    dma(pg, "sync", dr["ccI"].ap()[0:1024, :].rearrange("(c p) d -> p c d", p=128),
        ctx.G1T[:, :, :], sG)
    dma(pg, "sync", dr["ccI"].ap()[1024:2048, :].rearrange("(c p) d -> p c d", p=128),
        ctx.G0T[:, :, :], sG)
    w8(pg, "sync", sG, sG.n)


def ph_AR(ctx, pg):
    nc, dr, cst, sb = ctx.nc, ctx.dr, ctx.cst, ctx.sb
    Wg3b = sb("ar_wg", (128, 8, 3), BF16)
    sxnb = sb("ar_sxnb", (128, 8), BF16)
    g3s = sb("ar_g3", (3, 1))
    a3r = sb("ar_a3r", (1, 3))
    a3rb = sb("ar_a3rb", (1, 3), BF16)
    gbf = sb("ar_gb", (128, 3))
    alc = sb("ar_alc", (128, 1))
    lrs = sb("ar_lrs", (128, 1))
    Gt = [sb(f"ar_gt{b}", (128, 1024)) for b in range(2)]
    Wf = [sb(f"ar_wf{b}", (128, 1024)) for b in range(2)]
    t2 = sb("ar_t2", (128, 1024))
    sL = ctx.dsemf("arL")
    sO = ctx.dsemf("arO")
    sW = ctx.dsemf("arW")
    tPE, tDV, tAC = ctx.tile["tensor"], ctx.tile["vector"], ctx.tile["scalar"]

    dma(pg, "gpsimd", Wg3b[:, :, :], dr["Wg3"].ap().rearrange(
        "(c p) n -> p c n", p=128), sL)
    dma(pg, "gpsimd", ctx.WoutTb[:, :, :], wchunks("WoutT", ctx), sL)
    pg.em("vector", lambda eng: eng.tensor_copy(sxnb[:, :], ctx.sxn[:, :]))
    dv1 = tinc(ctx, pg, "vector")
    w8(pg, "tensor", sL, 32)
    w8(pg, "tensor", tDV, dv1)
    bk = 0
    pe_group(ctx, pg, bk, [
        (lambda eng, kc=kc: nc.tensor.matmul(
            ctx.ps[0][0:3, 0:1], Wg3b[:, kc, :], sxnb[:, kc:kc + 1],
            start=(kc == 0), stop=(kc == 7))) for kc in range(8)])
    consume(ctx, pg, "vector", bk, [
        lambda eng: eng.tensor_copy(g3s[:, :], ctx.ps[0][0:3, 0:1])])
    dv2 = tinc(ctx, pg, "vector")
    w8(pg, "sync", tDV, dv2)
    dma(pg, "sync", dr["ccI"].ap()[2048:2051, 0:1], g3s[:, :],
        ctx.dsemf("arG"), allow_slow_non_contiguous=True)
    w8(pg, "sync", ctx.dsemf("arG"), ctx.dsemf("arG").n)
    sy1 = tinc(ctx, pg, "sync")
    w8(pg, "gpsimd", ctx.tile["sync"], sy1)
    w8(pg, "gpsimd", ctx.dsemf("a2G"), ctx.dsemf("a2G").n)

    def cc(eng):
        return eng.collective_compute(
            "AllReduce", ALU.add, replica_groups=[list(range(NCORES))],
            ins=[dr["ccI"].ap()], outs=[dr["ccO"].ap()])
    pg.em("gpsimd", lambda eng: cc(eng).then_inc(ctx.ccS.h))
    ctx.ccS.n += 1
    pg.anchor["gpsimd"] = (ctx.ccS.h, 1)
    pg.kind["gpsimd"] = "inc"
    w8(pg, "sync", ctx.ccS, 1)
    dma(pg, "sync", g3s[:, :], dr["ccO"].ap()[2048:2051, 0:1],
        ctx.dsemf("arG2"), allow_slow_non_contiguous=True)
    w8(pg, "scalar", ctx.dsemf("arG2"), 16)
    pg.em("scalar", lambda eng: eng.activation(
        g3s[:, :], g3s[:, :], AF.Sigmoid, bias=cst.bg3[:, :],
        scale=1.0 / (16 * SEQ)))
    ac1 = tinc(ctx, pg, "scalar")
    w8(pg, "tensor", tAC, ac1)
    pe_group(ctx, pg, 1, [lambda eng: nc.tensor.matmul(
        ctx.ps[1][0:1, 0:3], g3s[:, :], cst.identF[0:3, 0:3],
        start=True, stop=True)])
    consume(ctx, pg, "vector", 1, [
        lambda eng: eng.tensor_copy(a3r[:, :], ctx.ps[1][0:1, 0:3])])
    spacer(pg, ctx)
    pg.em("vector", lambda eng: eng.tensor_copy(a3rb[:, :], a3r[:, :]))
    dv3 = tinc(ctx, pg, "vector")
    w8(pg, "tensor", tDV, dv3)
    pe_group(ctx, pg, 2, [lambda eng: nc.tensor.matmul(
        ctx.ps[2][:, 0:3], cst.onesRowB[:, :], a3rb[:, :],
        start=True, stop=True)])
    consume(ctx, pg, "vector", 2, [
        lambda eng: eng.tensor_copy(gbf[:, :], ctx.ps[2][:, 0:3])])
    spacer(pg, ctx)
    pg.em("vector", lambda eng: eng.tensor_scalar(
        alc[:, :], gbf[:, 0:1], -1.0, 1.0, ALU.mult, ALU.add))
    spacer(pg, ctx)
    pg.em("vector", lambda eng: eng.tensor_copy(lrs[:, :], gbf[:, 1:2]))
    spacer(pg, ctx)
    dv4 = tinc(ctx, pg, "vector")
    ccO = dr["ccO"].ap()
    for wi, (rows, WTname, dst) in enumerate(
            ((1024, "W0T", ctx.W0nT), (0, "W1T", ctx.W1nT))):
        for c in range(8):
            b = (wi * 8 + c) % 2
            sGt = ctx.dsemf(f"arGt{b}")
            sWf = ctx.dsemf(f"arWf{b}")
            if wi * 8 + c >= 2:
                w8(pg, "sync", tDV, dv4 + (wi * 8 + c) - 1)
            dma(pg, "sync", Gt[b][:, :],
                ccO[rows:rows + 1024, :].rearrange("(c p) d -> p c d", p=128)[:, c, :],
                sGt)
            dma(pg, "sync", Wf[b][:, :],
                dr[WTname].ap().rearrange("(c p) d -> p c d", p=128)[:, c, :],
                sWf)
            w8(pg, "vector", sGt, sGt.n)
            w8(pg, "vector", sWf, sWf.n)

            def build(eng, b=b, c=c, dst=dst):
                eng.tensor_scalar(Gt[b][:, :], Gt[b][:, :], TTT_SCALE,
                                  GRAD_CLIP, ALU.mult, ALU.min)
                eng.tensor_scalar(t2[:, :], Gt[b][:, :], -GRAD_CLIP,
                                  lrs[:, :], ALU.max, ALU.mult)
                return eng.scalar_tensor_tensor(
                    dst[:, c, :], Wf[b][:, :], alc[:, :], t2[:, :],
                    ALU.mult, ALU.subtract)
            pg.em("vector", build)
            tinc(ctx, pg, "vector")
    w8(pg, "gpsimd", sL, sL.n)


def ph_E(ctx, pg):
    nc, dr, cst, sb = ctx.nc, ctx.dr, ctx.cst, ctx.sb
    TT, NT = 512, 16
    qf = [sb(f"e_qf{b}", (128, 8, TT), BF16) for b in range(2)]
    sg1 = sb("e_sg", (128, 8, TT), BF16)
    s1f = sb("e_s1", (128, 8, TT), BF16)
    h2f = sb("e_h2", (128, 8, TT), BF16)
    ost = sb("e_ost", (128, 8, TT))
    sQ = [ctx.dsemf(f"eq{b}") for b in range(2)]
    sE = ctx.dsemf("eR")
    tPE, tDV, tAC = ctx.tile["tensor"], ctx.tile["vector"], ctx.tile["scalar"]
    nb = bank_cycler()
    pe_base = tPE.n

    def load(i):
        if i >= NT:
            return
        b = i % 2
        w8(pg, "sync", tPE, pe_base + max(0, i - 1))
        dma(pg, "sync", qf[b][:, :, :], fm_slice(dr["qnS"], TT * i, TT), sQ[b])

    load(0)
    for i in range(NT):
        b = i % 2
        load(i + 1)
        w8(pg, "tensor", sQ[b], 16 * (i // 2 + 1))
        for m in range(8):
            bk = nb()
            pe_group(ctx, pg, bk, [
                (lambda eng, kc=kc, m=m, bk=bk, b=b: nc.tensor.matmul(
                    ctx.ps[bk][:, :], ctx.W0nT[:, kc, 128 * m:128 * (m + 1)],
                    qf[b][:, kc, :], start=(kc == 0), stop=(kc == 7)))
                for kc in range(8)])
            consume(ctx, pg, "scalar", bk, [
                lambda eng, m=m, bk=bk: eng.activation(
                    sg1[:, m, :], ctx.ps[bk][:, :], AF.Sigmoid)])
            acm = tinc(ctx, pg, "scalar")
            w8(pg, "vector", tAC, acm)
            consume(ctx, pg, "vector", bk, [
                lambda eng, m=m, bk=bk: eng.tensor_tensor(
                    s1f[:, m, :], ctx.ps[bk][:, :], sg1[:, m, :], ALU.mult)])
        dv1 = tinc(ctx, pg, "vector")
        w8(pg, "tensor", tDV, dv1)
        for m in range(8):
            bk = nb()
            pe_group(ctx, pg, bk, [
                (lambda eng, kc=kc, m=m, bk=bk: nc.tensor.matmul(
                    ctx.ps[bk][:, :], ctx.W1nT[:, kc, 128 * m:128 * (m + 1)],
                    s1f[:, kc, :], start=(kc == 0), stop=(kc == 7)))
                for kc in range(8)])
            consume(ctx, pg, "vector", bk, [
                lambda eng, m=m, bk=bk: eng.tensor_copy(
                    h2f[:, m, :], ctx.ps[bk][:, :])])
        dv2 = tinc(ctx, pg, "vector")
        w8(pg, "tensor", tDV, dv2)
        w8(pg, "scalar", sE, 16 * i)
        for m in range(8):
            bk = nb()
            pe_group(ctx, pg, bk, [
                (lambda eng, kc=kc, m=m, bk=bk: nc.tensor.matmul(
                    ctx.ps[bk][:, :], ctx.WoutTb[:, kc, 128 * m:128 * (m + 1)],
                    h2f[:, kc, :], start=(kc == 0), stop=(kc == 7)))
                for kc in range(8)])
            consume(ctx, pg, "scalar", bk, [
                lambda eng, m=m, bk=bk: eng.copy(ost[:, m, :], ctx.ps[bk][:, :])])
        ac2 = tinc(ctx, pg, "scalar")
        tinc(ctx, pg, "tensor")
        w8(pg, "gpsimd", tAC, ac2)
        dma(pg, "gpsimd", fm_slice(dr["R"], TT * i, TT), ost[:, :, :], sE,
            accum_op=ALU.add)
    w8(pg, "gpsimd", sE, sE.n)


def ph_E1(ctx, pg):
    nc, dr, cst, sb = ctx.nc, ctx.dr, ctx.cst, ctx.sb
    TT, NT = 512, 16
    Rt = [sb(f"e1_r{b}", (128, 8, TT)) for b in range(2)]
    x1b = sb("e1_x1b", (128, 8, TT), BF16)
    sqb = sb("e1_sqb", (128, 8, TT), BF16)
    xLs = sb("e1_xls", (128, 8, TT))
    muR = sb("e1_mu", (1, TT))
    muR2 = sb("e1_mu2", (1, TT))
    veR = sb("e1_ve", (1, TT))
    veE = sb("e1_vee", (1, TT))
    yR = sb("e1_y", (1, TT))
    tR = sb("e1_t", (1, TT))
    rowYb = sb("e1_yb", (1, TT), BF16)
    nmRb = sb("e1_nmb", (1, TT), BF16)
    sR = [ctx.dsemf(f"e1r{b}") for b in range(2)]
    sS1 = ctx.dsemf("e1s1")
    sSL = ctx.dsemf("e1sl")
    tPE, tDV, tAC = ctx.tile["tensor"], ctx.tile["vector"], ctx.tile["scalar"]
    pe_base = tPE.n

    dv_base = tDV.n
    ac_base = tAC.n

    def load(i):
        if i >= NT:
            return
        b = i % 2
        w8(pg, "sync", tDV, dv_base + max(0, i - 1) * 3)
        w8(pg, "sync", tAC, ac_base + max(0, i - 1) * 2)
        dma(pg, "sync", Rt[b][:, :, :], fm_slice(dr["R"], TT * i, TT), sR[b])

    load(0)
    for i in range(NT):
        b = i % 2
        load(i + 1)
        nld = 16 * (i // 2 + 1)
        w8(pg, "scalar", sR[b], nld)
        w8(pg, "scalar", tPE, pe_base + max(0, i))
        pg.em("scalar", lambda eng, b=b: eng.activation(
            x1b.ap().rearrange("p c s -> p (c s)"),
            Rt[b].ap().rearrange("p c s -> p (c s)"), AF.Copy))
        pg.em("scalar", lambda eng, b=b: eng.activation(
            sqb.ap().rearrange("p c s -> p (c s)"),
            Rt[b].ap().rearrange("p c s -> p (c s)"), AF.Square))
        ac1 = tinc(ctx, pg, "scalar")
        dma(pg, "scalar", fm_slice(dr["R1"], TT * i, TT), Rt[b][:, :, :], sS1)
        w8(pg, "tensor", tAC, ac1)
        pe_group(ctx, pg, 0, [
            (lambda eng, c=c: nc.tensor.matmul(
                ctx.ps[0][0:1, :], cst.onesColB[:, :], x1b[:, c, :],
                start=(c == 0), stop=(c == 7))) for c in range(8)])
        pe_group(ctx, pg, 1, [
            (lambda eng, c=c: nc.tensor.matmul(
                ctx.ps[1][0:1, :], cst.onesColB[:, :], sqb[:, c, :],
                start=(c == 0), stop=(c == 7))) for c in range(8)])

        def rows0(eng):
            eng.tensor_scalar(muR[:, :], ctx.ps[0][0:1, :], 1.0 / D, None, ALU.mult)
            return eng.tensor_scalar(muR2[:, :], ctx.ps[0][0:1, :], 1.0 / D,
                                     None, ALU.mult)

        def rows1(eng):
            eng.tensor_scalar(veR[:, :], ctx.ps[1][0:1, :], 1.0 / D, None, ALU.mult)
            eng.tensor_mul(tR[:, :], muR[:, :], muR2[:, :])
            eng.tensor_sub(veR[:, :], veR[:, :], tR[:, :])
            return eng.tensor_scalar(veE[:, :], veR[:, :], LN_EPS, None, ALU.add)
        consume(ctx, pg, "vector", 0, [rows0])
        consume(ctx, pg, "vector", 1, [rows1])
        dv1 = tinc(ctx, pg, "vector")
        w8(pg, "scalar", tDV, dv1)
        pg.em("scalar", lambda eng: eng.activation(
            yR[:, :], veR[:, :], AF.Ln, bias=cst.epsCol[0:1, :]))
        pg.em("scalar", lambda eng: eng.activation(
            yR[:, :], yR[:, :], AF.Exp, scale=-0.5))
        ac2 = tinc(ctx, pg, "scalar")
        w8(pg, "vector", tAC, ac2)
        rsqrt_nr(pg, yR[:, :], veE[:, :], tR[:, :])

        def rows2(eng):
            eng.tensor_mul(tR[:, :], muR[:, :], yR[:, :])
            eng.tensor_scalar(tR[:, :], tR[:, :], -1.0, None, ALU.mult)
            eng.tensor_copy(rowYb[:, :], yR[:, :])
            return eng.tensor_copy(nmRb[:, :], tR[:, :])
        pg.em("vector", rows2)
        dv2 = tinc(ctx, pg, "vector")
        w8(pg, "tensor", tDV, dv2)
        pe_group(ctx, pg, 2, [lambda eng: nc.tensor.matmul(
            ctx.ps[2][:, :], cst.onesRowB[:, :], rowYb[:, :],
            start=True, stop=True)])
        pe_group(ctx, pg, 3, [lambda eng: nc.tensor.matmul(
            ctx.ps[3][:, :], cst.onesRowB[:, :], nmRb[:, :],
            start=True, stop=True)])
        w8(pg, "vector", sSL, 16 * i)
        fns = []
        for c in range(8):
            fns.append(lambda eng, c=c, b=b: eng.tensor_tensor(
                xLs[:, c, :], Rt[b][:, c, :], ctx.ps[2][:, :], ALU.mult))
        consume(ctx, pg, "vector", 2, fns)
        fns2 = []
        for c in range(8):
            fns2.append(lambda eng, c=c: eng.tensor_tensor(
                xLs[:, c, :], xLs[:, c, :], ctx.ps[3][:, :], ALU.add))
        consume(ctx, pg, "vector", 3, fns2)
        dv3 = tinc(ctx, pg, "vector")
        tinc(ctx, pg, "tensor")
        w8(pg, "sync", tDV, dv3)
        w8(pg, "sync", sS1, sS1.n)
        dma(pg, "sync", fm_slice(dr["R"], TT * i, TT), xLs[:, :, :], sSL)
    w8(pg, "sync", sSL, sSL.n)
    w8(pg, "scalar", sS1, sS1.n)


def ph_ATTN1(ctx, pg, layer=0):
    nc, dr, cst, sb = ctx.nc, ctx.dr, ctx.cst, ctx.sb
    TT, NT = 512, 16
    Wq3 = sb("t1_wq3", (128, 8, 3 * D), BF16)
    Mqb = sb("t1_mq", (16, D), BF16)
    Mkb = sb("t1_mk", (16, D), BF16)
    xsb = [sb(f"t1_xs{b}", (128, 8, TT + 1), BF16) for b in range(2)]
    tabs = [[sb(f"t1_tb{b}_{j}", (128, TT), BF16) for j in range(4)]
            for b in range(2)]
    qraw = sb("t1_qr", (128, 8, TT), BF16)
    kraw = sb("t1_kr", (128, 8, TT), BF16)
    vraw = sb("t1_vr", (128, 8, TT), BF16)
    sq2 = sb("t1_sq2", (128, 8, TT), BF16)
    ssm = sb("t1_ssm", (16, 2 * TT))
    yss = sb("t1_yss", (16, 2 * TT))
    tss = sb("t1_tss", (16, 2 * TT))
    rq8 = sb("t1_rq8", (16, TT), BF16)
    rk1 = sb("t1_rk1", (16, TT), BF16)
    tq = sb("t1_tq", (128, 8, TT), BF16)
    tk = sb("t1_tk", (128, 8, TT), BF16)
    tqp = sb("t1_tqp", (128, 8, TT), BF16)
    tkp = sb("t1_tkp", (128, 8, TT), BF16)
    avs = sb("t1_avs", (128, 4, HEADS, DH + 1), BF16)
    sW = ctx.dsemf("t1W")
    sX = [ctx.dsemf(f"t1x{b}") for b in range(2)]
    sTb = [ctx.dsemf(f"t1t{b}") for b in range(2)]
    sRot = ctx.dsemf("t1rot")
    sSt = ctx.dsemf("t1st")
    sX0 = [s.n for s in sX]
    sTb0 = [s.n for s in sTb]
    sSt0 = sSt.n
    tPE, tDV, tAC = ctx.tile["tensor"], ctx.tile["vector"], ctx.tile["scalar"]
    nb = bank_cycler()
    dma(pg, "gpsimd", Wq3[:, :, :], wchunks(f"Wqkv{layer}", ctx), sW)
    dma(pg, "gpsimd", Mqb[:, :], dr[f"Mq{layer}"].ap(), sW)
    dma(pg, "gpsimd", Mkb[:, :], dr[f"Mk{layer}"].ap(), sW)
    dma(pg, "gpsimd", ctx.Wo_l[:, :, :], wchunks(f"Wo{layer}", ctx), sW)
    w8(pg, "tensor", sW, sW.n)
    dvends = []
    pe_mark = {}

    def load(i):
        if i >= NT:
            return
        b = i % 2
        if i >= 2:
            w8(pg, "gpsimd", *pe_mark[i - 2])
            w8(pg, "gpsimd", tDV, dvends[i - 2])
        s0 = TT * i
        if i % 8 == 0:
            dma(pg, "gpsimd", xsb[b][:, :, 1:TT + 1],
                fm_slice(dr["R"], s0, TT), sX[b])
        else:
            dma(pg, "gpsimd", xsb[b][:, :, :],
                fm_slice(dr["R"], s0 - 1, TT + 1), sX[b])
        c0 = s0 % SEQ
        for j, tn in enumerate(("rotCQ", "rotSQ", "rotCK", "rotSK")):
            dma(pg, "gpsimd", tabs[b][j][:, :],
                dr[tn].ap()[:, c0:c0 + TT], sTb[b])

    pg.em("vector", lambda eng: eng.memset(avs[:, :, :, DH:DH + 1], 1.0))
    load(0)
    for i in range(NT):
        b = i % 2
        load(i + 1)
        if i % 8 == 0:
            pg.em("vector", lambda eng, b=b: eng.memset(xsb[b][:, :, 0:1], 0.0))
            dvm = tincL(ctx, pg, "vector")
            w8(pg, "tensor", tDV, dvm)
        w8(pg, "tensor", sX[b], sX0[b] + 16 * (i // 2 + 1))
        for m in range(24):
            bk = nb()
            pe_group(ctx, pg, bk, [
                (lambda eng, kc=kc, m=m, bk=bk, b=b: nc.tensor.matmul(
                    ctx.ps[bk][:, :],
                    Wq3[:, kc, 128 * m:128 * (m + 1)],
                    xsb[b][:, kc, 1:TT + 1] if kc < 4 else xsb[b][:, kc, 0:TT],
                    start=(kc == 0), stop=(kc == 7)))
                for kc in range(8)])
            tgt, mm = (qraw, m) if m < 8 else ((kraw, m - 8) if m < 16
                                               else (vraw, m - 16))
            consume(ctx, pg, "vector", bk, [
                lambda eng, tgt=tgt, mm=mm, bk=bk: eng.tensor_copy(
                    tgt[:, mm, :], ctx.ps[bk][:, :])])
            qkv_done = (ctx.evS[bk], ctx.evS[bk].n)
        w8(pg, "scalar", *qkv_done)
        pg.em("scalar", lambda eng: eng.activation(
            sq2.ap().rearrange("p c s -> p (c s)"),
            qraw.ap().rearrange("p c s -> p (c s)"), AF.Square))
        ac1 = tinc(ctx, pg, "scalar")
        w8(pg, "tensor", tAC, ac1)
        bq = nb()
        pe_group(ctx, pg, bq, [
            (lambda eng, c=c, bq=bq: nc.tensor.matmul(
                ctx.ps[bq][0:16, :], cst.HselB[:, c, :], sq2[:, c, :],
                start=(c == 0), stop=(c == 7))) for c in range(8)])
        w8(pg, "scalar", ctx.mmS[bq], ctx.fills[bq])
        pg.em("scalar", lambda eng: eng.activation(
            sq2.ap().rearrange("p c s -> p (c s)"),
            kraw.ap().rearrange("p c s -> p (c s)"), AF.Square))
        ac2 = tinc(ctx, pg, "scalar")
        w8(pg, "tensor", tAC, ac2)
        bK = nb()
        pe_group(ctx, pg, bK, [
            (lambda eng, c=c, bK=bK: nc.tensor.matmul(
                ctx.ps[bK][0:16, :], cst.HselB[:, c, :], sq2[:, c, :],
                start=(c == 0), stop=(c == 7))) for c in range(8)])
        consume(ctx, pg, "vector", bq, [
            lambda eng, bq=bq: eng.tensor_scalar(
                ssm[:, 0:TT], ctx.ps[bq][0:16, :], 1e-24, None, ALU.max)])
        consume(ctx, pg, "vector", bK, [
            lambda eng, bK=bK: eng.tensor_scalar(
                ssm[:, TT:], ctx.ps[bK][0:16, :], 1e-24, None, ALU.max)])
        ssm_done = (ctx.evS[bK], ctx.evS[bK].n)
        w8(pg, "scalar", *ssm_done)
        pg.em("scalar", lambda eng: eng.activation(yss[:, :], ssm[:, :], AF.Ln))
        pg.em("scalar", lambda eng: eng.activation(
            yss[:, :], yss[:, :], AF.Exp, scale=-0.5))
        ac3 = tinc(ctx, pg, "scalar")
        w8(pg, "vector", tAC, ac3)
        rsqrt_nr(pg, yss[:, :], ssm[:, :], tss[:, :])

        def facs(eng):
            eng.tensor_scalar(rq8[:, :], yss[:, 0:TT], 8.0, None, ALU.mult)
            return eng.tensor_copy(rk1[:, :], yss[:, TT:])
        pg.em("vector", facs)
        dv3 = tincL(ctx, pg, "vector")
        w8(pg, "vector", sSt, sSt0 + 48 * i)
        w8(pg, "tensor", tDV, dv3)
        for (Mb, rb, raw, tdst) in ((Mqb, rq8, qraw, tq), (Mkb, rk1, kraw, tk)):
            for c in range(8):
                bk = nb()
                pe_group(ctx, pg, bk, [
                    lambda eng, Mb=Mb, rb=rb, c=c, bk=bk: nc.tensor.matmul(
                        ctx.ps[bk][:, :], Mb[:, 128 * c:128 * (c + 1)], rb[:, :],
                        start=True, stop=True)])
                consume(ctx, pg, "vector", bk, [
                    lambda eng, raw=raw, tdst=tdst, c=c, bk=bk: eng.tensor_tensor(
                        tdst[:, c, :], raw[:, c, :], ctx.ps[bk][:, :], ALU.mult)])
                bc_done = (ctx.evS[bk], ctx.evS[bk].n)
        w8(pg, "gpsimd", *bc_done)
        for (src_t, dst_t) in ((tq, tqp), (tk, tkp)):
            for (d0, s0_) in ((0, 32), (32, 0), (64, 96), (96, 64)):
                dma(pg, "gpsimd", dst_t[d0:d0 + 32, :, :],
                    src_t[s0_:s0_ + 32, :, :], sRot)
        w8(pg, "vector", sRot, sRot.n)
        w8(pg, "vector", sTb[b], sTb0[b] + 64 * (i // 2 + 1))

        def rot(eng, b=b):
            CQ = tabs[b][0].ap().unsqueeze(1).broadcast_to([128, 8, TT])
            SQ = tabs[b][1].ap().unsqueeze(1).broadcast_to([128, 8, TT])
            CK = tabs[b][2].ap().unsqueeze(1).broadcast_to([128, 8, TT])
            SK = tabs[b][3].ap().unsqueeze(1).broadcast_to([128, 8, TT])
            eng.tensor_tensor(tq[:, :, :], tq[:, :, :], CQ, ALU.mult)
            eng.tensor_tensor(tqp[:, :, :], tqp[:, :, :], SQ, ALU.mult)
            eng.tensor_tensor(tq[:, :, :], tq[:, :, :], tqp[:, :, :], ALU.add)
            eng.tensor_tensor(tk[:, :, :], tk[:, :, :], CK, ALU.mult)
            eng.tensor_tensor(tkp[:, :, :], tkp[:, :, :], SK, ALU.mult)
            return eng.tensor_tensor(tk[:, :, :], tk[:, :, :], tkp[:, :, :],
                                     ALU.add)
        pg.em("vector", rot)
        dv5 = tincL(ctx, pg, "vector")
        for u in range(4):
            for half in range(2):
                bk = nb()
                pe_group(ctx, pg, bk, [
                    (lambda eng, u=u, half=half, cc2=cc2, bk=bk: nc.tensor.matmul(
                        ctx.ps[bk][:, 128 * cc2:128 * (cc2 + 1)],
                        vraw[:, 4 * half + cc2, 128 * u:128 * (u + 1)],
                        cst.identB[:, :], start=(cc2 == 0), stop=(cc2 == 3)))
                    for cc2 in range(4)])
                consume(ctx, pg, "vector", bk, [
                    lambda eng, u=u, half=half, bk=bk: eng.tensor_copy(
                        avs[:, u, 8 * half:8 * (half + 1), 0:DH],
                        ctx.ps[bk].ap().rearrange("p (h e) -> p h e", h=8))])
                avs_done = (ctx.evS[bk], ctx.evS[bk].n)
                pe_mark[i] = (ctx.mmS[bk], ctx.mmS[bk].n)
        w8(pg, "sync", *avs_done)
        w8(pg, "sync", tDV, dv5)
        s0 = TT * i
        dma(pg, "sync", fm_slice(dr["aqnS"], s0, TT), tq[:, :, :], sSt)
        dma(pg, "sync", fm_slice(dr["aknS"], s0, TT), tk[:, :, :], sSt)
        dma(pg, "sync", dr["avtS"].ap()[4 * i:4 * (i + 1), :, :, :].transpose(
            [1, 0, 2, 3]), avs[:, :, :, :], sSt)
        dvends.append(tDV.n)
    w8(pg, "sync", sSt, sSt.n)
    w8(pg, "gpsimd", sRot, sRot.n)
    tinc(ctx, pg, "vector")
    tinc(ctx, pg, "scalar")
    tinc(ctx, pg, "tensor")


def ph_ATTN2(ctx, pg, layer=0):
    nc, dr, cst, sb = ctx.nc, ctx.dr, ctx.cst, ctx.sb
    aqn = [sb(f"t2_q{b}", (128, 8, WIN), BF16) for b in range(2)]
    akn = [sb(f"t2_k{b}", (128, 8, 2 * WIN), BF16) for b in range(2)]
    avt = [sb(f"t2_v{b}", (128, 4, HEADS, DH + 1), BF16) for b in range(2)]
    Pb2 = [sb(f"t2_p{b}", (128, 2, 512), BF16) for b in range(2)]
    AOr = sb("t2_aor", (128, 8, WIN), BF16)
    AOn = sb("t2_aon", (128, 8, WIN), BF16)
    LmatR = sb("t2_l", (1, HEADS * WIN))
    rcpR = sb("t2_rcf", (1, HEADS * WIN))
    rcpRb = sb("t2_rcb", (1, HEADS * WIN), BF16)
    rcp16 = sb("t2_rcp", (HEADS, WIN), BF16)
    ost = sb("t2_ost", (128, 8, WIN))
    sQ = [ctx.dsemf(f"t2q{b}") for b in range(2)]
    sK = [ctx.dsemf(f"t2k{b}") for b in range(2)]
    sV = [ctx.dsemf(f"t2v{b}") for b in range(2)]
    sO = ctx.dsemf("t2o")
    sLr = ctx.dsemf("t2lr")
    sQ0 = [s.n for s in sQ]
    sK0 = [s.n for s in sK]
    sV0 = [s.n for s in sV]
    sO0 = sO.n
    tPE, tDV, tAC = ctx.tile["tensor"], ctx.tile["vector"], ctx.tile["scalar"]
    nb = bank_cycler()
    NWT = 2 * NWIN
    pe_mark = {}

    def load(i):
        if i >= NWT:
            return
        b = i % 2
        seq, w = divmod(i, NWIN)
        s0 = seq * SEQ + w * WIN
        if i >= 2:
            w8(pg, "sync", *pe_mark[i - 2])
        dma(pg, "sync", aqn[b][:, :, :], fm_slice(dr["aqnS"], s0, WIN), sQ[b])
        if w == 0:
            dma(pg, "sync", akn[b][:, :, 0:WIN],
                fm_slice(dr["aknS"], s0, WIN), sK[b])
            dma(pg, "sync", avt[b][:, 0:2, :, :],
                dr["avtS"].ap()[2 * i:2 * i + 2].transpose([1, 0, 2, 3]), sV[b])
        else:
            dma(pg, "sync", akn[b][:, :, :],
                fm_slice(dr["aknS"], s0 - WIN, 2 * WIN), sK[b])
            dma(pg, "sync", avt[b][:, :, :, :],
                dr["avtS"].ap()[2 * i - 2:2 * i + 2].transpose([1, 0, 2, 3]),
                sV[b])

    load(0)
    for i in range(NWT):
        b = i % 2
        seq, w = divmod(i, NWIN)
        w0 = (w == 0)
        load(i + 1)
        nq = 16 * (i // 2 + 1)
        w8(pg, "tensor", sQ[b], sQ0[b] + nq)
        w8(pg, "tensor", sK[b], sK0[b] + nq)
        w8(pg, "tensor", sV[b], sV0[b] + nq)
        scs = (0, 1) if w0 else (0, 1, 2, 3)
        gs = (0,) if w0 else (0, 1)
        w8(pg, "vector", sLr, sLr.n)
        pmin = {}
        for h in range(HEADS + 1):
            if h < HEADS:
                hp = h % 2
                pb = 64 * (h % 2)
                hc = h // 2
                exp_mark = None
                for g in gs:
                    bk = nb()
                    pe_group(ctx, pg, bk, [
                        (lambda eng, sc=2 * g + d, d=d, bk=bk, hc=hc, pb=pb,
                                b=b: nc.tensor.matmul(
                            ctx.ps[bk][:, 256 * d:256 * (d + 1)],
                            akn[b][pb:pb + 64, hc, 128 * sc:128 * (sc + 1)],
                            aqn[b][pb:pb + 64, hc, :],
                            start=(d == 0), stop=(d == 1)))
                        for d in range(2)])
                    consume(ctx, pg, "scalar", bk, [
                        lambda eng, g=g, bk=bk, hp=hp: eng.activation(
                            Pb2[hp][:, g, :], ctx.ps[bk][:, :], AF.Exp)])
                    exp_mark = (ctx.evS[bk], ctx.evS[bk].n)
                w8(pg, "vector", *exp_mark)
                for g in gs:
                    mask = cst.maskTBb if (w0 or g == 1) else cst.maskTAb
                    pg.em("vector", lambda eng, g=g, hp=hp, mask=mask:
                          eng.tensor_tensor(Pb2[hp][:, g, :], Pb2[hp][:, g, :],
                                            mask[:, :], ALU.min))
                pmin[h] = tincL(ctx, pg, "vector")
            if h >= 1:
                hq = h - 1
                hqp = hq % 2
                pbq = 64 * (hq % 2)
                hcq = hq // 2
                w8(pg, "tensor", tDV, pmin[hq])
                bkv = nb()
                pe_group(ctx, pg, bkv, [
                    (lambda eng, sc=sc, z=z, bkv=bkv, b=b, hq=hq, hqp=hqp:
                     nc.tensor.matmul(
                        ctx.ps[bkv][0:DH + 1, 0:WIN],
                        avt[b][:, sc, hq, :],
                        Pb2[hqp][:, sc // 2, 256 * (sc % 2):256 * (sc % 2) + 256],
                        start=(z == 0), stop=(z == len(scs) - 1)))
                    for z, sc in enumerate(scs)])
                consume(ctx, pg, "scalar", bkv, [
                    (lambda eng, pbq=pbq, hcq=hcq, bkv=bkv: eng.copy(
                        AOr[pbq:pbq + 64, hcq, :], ctx.ps[bkv][0:DH, 0:WIN])),
                    (lambda eng, hq=hq, bkv=bkv: eng.copy(
                        LmatR[0:1, WIN * hq:WIN * (hq + 1)],
                        ctx.ps[bkv][DH:DH + 1, 0:WIN])),
                ])
                w8(pg, "vector", ctx.evS[bkv], ctx.evS[bkv].n)
                pg.em("vector", lambda eng, hq=hq: eng.reciprocal(
                    rcpR[0:1, WIN * hq:WIN * (hq + 1)],
                    LmatR[0:1, WIN * hq:WIN * (hq + 1)]))
                pg.em("vector", lambda eng, hq=hq: eng.tensor_copy(
                    rcpRb[0:1, WIN * hq:WIN * (hq + 1)],
                    rcpR[0:1, WIN * hq:WIN * (hq + 1)]))
                rcv = tincL(ctx, pg, "vector")
                w8(pg, "gpsimd", tDV, rcv)
                dma(pg, "gpsimd", rcp16[hq:hq + 1, :],
                    rcpRb[0:1, WIN * hq:WIN * (hq + 1)], sLr)
        w8(pg, "tensor", sLr, sLr.n)
        aon_mark = None
        for c in range(8):
            bk = nb()
            pe_group(ctx, pg, bk, [
                lambda eng, c=c, bk=bk: nc.tensor.matmul(
                    ctx.ps[bk][:, 0:WIN], cst.HselT[:, c, :], rcp16[:, :],
                    start=True, stop=True)])
            consume(ctx, pg, "vector", bk, [
                lambda eng, c=c, bk=bk: eng.tensor_tensor(
                    AOn[:, c, :], AOr[:, c, :], ctx.ps[bk][:, 0:WIN],
                    ALU.mult)])
            aon_mark = (ctx.evS[bk], ctx.evS[bk].n)
        w8(pg, "tensor", *aon_mark)
        w8(pg, "vector", sO, sO0 + 16 * i)
        ost_done = None
        for m in range(8):
            bk = nb()
            pe_group(ctx, pg, bk, [
                (lambda eng, kc=kc, m=m, bk=bk: nc.tensor.matmul(
                    ctx.ps[bk][:, 0:WIN], ctx.Wo_l[:, kc, 128 * m:128 * (m + 1)],
                    AOn[:, kc, :], start=(kc == 0), stop=(kc == 7)))
                for kc in range(8)])
            consume(ctx, pg, "vector", bk, [
                lambda eng, m=m, bk=bk: eng.tensor_copy(ost[:, m, :],
                                                        ctx.ps[bk][:, 0:WIN])])
            ost_done = (ctx.evS[bk], ctx.evS[bk].n)
            if m == 7:
                pe_mark[i] = (ctx.mmS[bk], ctx.mmS[bk].n)
        w8(pg, "gpsimd", *ost_done)
        s0 = seq * SEQ + w * WIN
        dma(pg, "gpsimd", fm_slice(dr["R"], s0, WIN), ost[:, :, :], sO,
            accum_op=ALU.add)
    w8(pg, "gpsimd", sO, sO.n)
    tinc(ctx, pg, "vector")
    tinc(ctx, pg, "scalar")
    tinc(ctx, pg, "tensor")


def ph_FFN0(ctx, pg, layer=0):
    nc, dr, cst, sb = ctx.nc, ctx.dr, ctx.cst, ctx.sb
    TT, NT = 512, 16
    xsb = [sb(f"f0_xs{b}", (128, 8, TT + 1), BF16) for b in range(2)]
    sqb = sb("f0_sq", (128, 8, TT + 1), BF16)
    muR = sb("f0_mu", (1, TT))
    muR2 = sb("f0_mu2", (1, TT))
    veR = sb("f0_ve", (1, TT))
    veE = sb("f0_vee", (1, TT))
    yR = sb("f0_y", (1, TT))
    tR = sb("f0_t", (1, TT))
    arL = sb("f0_ar", (1, TT), BF16)
    amL = sb("f0_am", (1, TT), BF16)
    sX = [ctx.dsemf(f"f0x{b}") for b in range(2)]
    sRw = ctx.dsemf("f0rw")
    sX0 = [s.n for s in sX]
    sRw0 = sRw.n
    tPE, tDV, tAC = ctx.tile["tensor"], ctx.tile["vector"], ctx.tile["scalar"]
    pe_base = tPE.n

    def load(i):
        if i >= NT:
            return
        b = i % 2
        w8(pg, "gpsimd", tPE, pe_base + max(0, i - 1))
        s0 = TT * i
        if i % 8 == 0:
            dma(pg, "gpsimd", xsb[b][:, :, 1:TT + 1],
                fm_slice(dr["R"], s0, TT), sX[b])
        else:
            dma(pg, "gpsimd", xsb[b][:, :, :],
                fm_slice(dr["R"], s0 - 1, TT + 1), sX[b])

    load(0)
    for i in range(NT):
        b = i % 2
        load(i + 1)
        if i % 8 == 0:
            pg.em("vector", lambda eng, b=b: eng.memset(xsb[b][:, :, 0:1], 0.0))
            dvm = tinc(ctx, pg, "vector")
            w8(pg, "scalar", tDV, dvm)
        w8(pg, "scalar", sX[b], sX0[b] + 16 * (i // 2 + 1))
        w8(pg, "scalar", tPE, pe_base + max(0, i))
        pg.em("scalar", lambda eng, b=b: eng.activation(
            sqb.ap().rearrange("p c s -> p (c s)"),
            xsb[b].ap().rearrange("p c s -> p (c s)"), AF.Square))
        ac1 = tinc(ctx, pg, "scalar")
        w8(pg, "tensor", tAC, ac1)
        pe_group(ctx, pg, 0, [
            (lambda eng, c=c, b=b: nc.tensor.matmul(
                ctx.ps[0][0:1, :], cst.onesColB[:, :],
                xsb[b][:, c, 1:TT + 1] if c < 4 else xsb[b][:, c, 0:TT],
                start=(c == 0), stop=(c == 7))) for c in range(8)])
        pe_group(ctx, pg, 1, [
            (lambda eng, c=c: nc.tensor.matmul(
                ctx.ps[1][0:1, :], cst.onesColB[:, :],
                sqb[:, c, 1:TT + 1] if c < 4 else sqb[:, c, 0:TT],
                start=(c == 0), stop=(c == 7))) for c in range(8)])

        def rows0(eng):
            eng.tensor_scalar(muR[:, :], ctx.ps[0][0:1, :], 1.0 / D, None,
                              ALU.mult)
            return eng.tensor_scalar(muR2[:, :], ctx.ps[0][0:1, :], 1.0 / D,
                                     None, ALU.mult)

        def rows1(eng):
            eng.tensor_scalar(veR[:, :], ctx.ps[1][0:1, :], 1.0 / D, None,
                              ALU.mult)
            eng.tensor_mul(tR[:, :], muR[:, :], muR2[:, :])
            eng.tensor_sub(veR[:, :], veR[:, :], tR[:, :])
            return eng.tensor_scalar(veE[:, :], veR[:, :], LN_EPS, None, ALU.add)
        consume(ctx, pg, "vector", 0, [rows0])
        consume(ctx, pg, "vector", 1, [rows1])
        dv1 = tinc(ctx, pg, "vector")
        w8(pg, "scalar", tDV, dv1)
        pg.em("scalar", lambda eng: eng.activation(
            yR[:, :], veR[:, :], AF.Ln, bias=cst.epsCol[0:1, :]))
        pg.em("scalar", lambda eng: eng.activation(
            yR[:, :], yR[:, :], AF.Exp, scale=-0.5))
        ac2 = tinc(ctx, pg, "scalar")
        w8(pg, "vector", tAC, ac2)
        rsqrt_nr(pg, yR[:, :], veE[:, :], tR[:, :])
        w8(pg, "vector", sRw, sRw0 + 32 * i)

        def rows2(eng):
            eng.tensor_mul(tR[:, :], muR[:, :], yR[:, :])
            eng.tensor_scalar(amL[:, :], tR[:, :], -1.0, None, ALU.mult)
            return eng.tensor_copy(arL[:, :], yR[:, :])
        pg.em("vector", rows2)
        dv2 = tinc(ctx, pg, "vector")
        tinc(ctx, pg, "tensor")
        w8(pg, "sync", tDV, dv2)
        s0 = TT * i
        dma(pg, "sync", dr["rows"].ap()[0:1, s0:s0 + TT], arL[:, :], sRw)
        dma(pg, "sync", dr["rows"].ap()[1:2, s0:s0 + TT], amL[:, :], sRw)
    w8(pg, "sync", sRw, sRw.n)


def ph_FFN1(ctx, pg, layer=0):
    nc, dr, cst, sb = ctx.nc, ctx.dr, ctx.cst, ctx.sb
    TT, NT = 512, 16
    W1a = sb("f1_w1a", (128, 8, DI), BF16)
    W1g = sb("f1_w1g", (128, 8, DI), BF16)
    W2p = sb("f1_w2", (128, NJ, D), BF16)
    b1a = sb("f1_b1a", (128, NJ))
    b1g = sb("f1_b1g", (128, NJ))
    b2c = sb("f1_b2", (128, 8))
    lng = sb("f1_g", (128, 8))
    lnb = sb("f1_b", (128, 8))
    rwr = [sb(f"f1_rr{b}", (1, TT), BF16) for b in range(2)]
    rwm = [sb(f"f1_rm{b}", (1, TT), BF16) for b in range(2)]
    xsb = sb("f1_xs", (128, 8, TT + 1), BF16)
    xn = sb("f1_xn", (128, 8, TT), BF16)
    gelG = sb("f1_gg", (128, TT), BF16)
    u = sb("f1_u", (128, NJ, TT), BF16)
    ost = sb("f1_ost", (128, 8, TT // 2))
    sW = ctx.dsemf("f1W")
    sWs = ctx.dsemf("f1Ws")
    sX = ctx.dsemf("f1x")
    sRw = [ctx.dsemf(f"f1rw{b}") for b in range(2)]
    sO = ctx.dsemf("f1o")
    sX0 = sX.n
    sRw0 = [s.n for s in sRw]
    sO0 = sO.n
    tPE, tDV, tAC = ctx.tile["tensor"], ctx.tile["vector"], ctx.tile["scalar"]
    nb = bank_cycler()
    dma(pg, "gpsimd", W1a[:, :, :], dr[f"W1a{layer}"].ap().rearrange(
        "(c p) n -> p c n", p=128), sW)
    dma(pg, "gpsimd", W1g[:, :, :], dr[f"W1g{layer}"].ap().rearrange(
        "(c p) n -> p c n", p=128), sW)
    dma(pg, "gpsimd", W2p[:, :, :], dr[f"W2p{layer}"].ap().rearrange(
        "(j p) n -> p j n", p=128), sW)
    dma(pg, "sync", b1a[:, :], dr[f"b1a{layer}"].ap(), sWs)
    dma(pg, "sync", b1g[:, :], dr[f"b1g{layer}"].ap(), sWs)
    dma(pg, "sync", b2c[:, :], dr[f"b2c{layer}"].ap(), sWs)
    dma(pg, "sync", lng[:, :], dr[f"lng{layer}"].ap(), sWs)
    dma(pg, "sync", lnb[:, :], dr[f"lnb{layer}"].ap(), sWs)
    w8(pg, "tensor", sW, sW.n)
    w8(pg, "tensor", sWs, sWs.n)
    pe_base = tPE.n
    dv2s = []

    def loadx(i):
        if i >= NT:
            return
        s0 = TT * i
        if i >= 1:
            w8(pg, "gpsimd", tDV, dv2s[i - 1])
        if i % 8 == 0:
            dma(pg, "gpsimd", xsb[:, :, 1:TT + 1],
                fm_slice(dr["R"], s0, TT), sX)
        else:
            dma(pg, "gpsimd", xsb[:, :, :],
                fm_slice(dr["R"], s0 - 1, TT + 1), sX)

    bc_mark = {}

    def loadrows(i):
        if i >= NT:
            return
        b = i % 2
        s0 = TT * i
        if i >= 2:
            w8(pg, "sync", *bc_mark[i - 2])
        dma(pg, "sync", rwr[b][:, :], dr["rows"].ap()[0:1, s0:s0 + TT], sRw[b])
        dma(pg, "sync", rwm[b][:, :], dr["rows"].ap()[1:2, s0:s0 + TT], sRw[b])

    loadx(0)
    loadrows(0)
    for i in range(NT):
        b = i % 2
        s0 = TT * i
        loadrows(i + 1)
        if i % 8 == 0:
            pg.em("vector", lambda eng: eng.memset(xsb[:, :, 0:1], 0.0))
        w8(pg, "tensor", sRw[b], sRw0[b] + 32 * (i // 2 + 1))
        bR, bM = nb(), nb()
        pe_group(ctx, pg, bR, [lambda eng, bR=bR, b=b: nc.tensor.matmul(
            ctx.ps[bR][:, :], cst.onesRowB[:, :], rwr[b][:, :],
            start=True, stop=True)])
        pe_group(ctx, pg, bM, [lambda eng, bM=bM, b=b: nc.tensor.matmul(
            ctx.ps[bM][:, :], cst.onesRowB[:, :], rwm[b][:, :],
            start=True, stop=True)])
        bc_mark[i] = (ctx.mmS[bM], ctx.mmS[bM].n)
        w8(pg, "vector", sX, sX0 + 16 * (i + 1))
        fnsR = []
        for c in range(8):
            off = 1 if c < 4 else 0
            fnsR.append(lambda eng, c=c, off=off, bR=bR: eng.tensor_tensor(
                xn[:, c, :], xsb[:, c, off:off + TT], ctx.ps[bR][:, :],
                ALU.mult))
        consume(ctx, pg, "vector", bR, fnsR)
        fnsM = []
        for c in range(8):
            fnsM.append(lambda eng, c=c, bM=bM: eng.tensor_tensor(
                xn[:, c, :], xn[:, c, :], ctx.ps[bM][:, :], ALU.add))
        consume(ctx, pg, "vector", bM, fnsM)
        for c in range(8):
            pg.em("vector", lambda eng, c=c: eng.tensor_scalar(
                xn[:, c, :], xn[:, c, :], lng[:, c:c + 1], lnb[:, c:c + 1],
                ALU.mult, ALU.add))
        dv2 = tincL(ctx, pg, "vector")
        dv2s.append(dv2)
        loadx(i + 1)
        w8(pg, "tensor", tDV, dv2)
        gel_free = None
        for j in range(NJ):
            bA, bG = nb(), nb()
            pe_group(ctx, pg, bA, [
                (lambda eng, kc=kc, j=j, bA=bA: nc.tensor.matmul(
                    ctx.ps[bA][:, :], W1a[:, kc, 128 * j:128 * (j + 1)],
                    xn[:, kc, :], start=(kc == 0), stop=(kc == 7)))
                for kc in range(8)])
            pe_group(ctx, pg, bG, [
                (lambda eng, kc=kc, j=j, bG=bG: nc.tensor.matmul(
                    ctx.ps[bG][:, :], W1g[:, kc, 128 * j:128 * (j + 1)],
                    xn[:, kc, :], start=(kc == 0), stop=(kc == 7)))
                for kc in range(8)])
            if gel_free is not None:
                w8(pg, "scalar", gel_free[0], gel_free[1])
            consume(ctx, pg, "scalar", bG, [
                lambda eng, j=j, bG=bG: eng.activation(
                    gelG[:, :], ctx.ps[bG][:, :], AF.Gelu,
                    bias=b1g[:, j:j + 1])])
            w8(pg, "vector", ctx.evS[bG], ctx.evS[bG].n)
            consume(ctx, pg, "vector", bA, [
                lambda eng, j=j, bA=bA: eng.scalar_tensor_tensor(
                    u[:, j, :], ctx.ps[bA][:, :], b1a[:, j:j + 1], gelG[:, :],
                    ALU.add, ALU.mult)])
            gel_free = (ctx.evS[bA], ctx.evS[bA].n)
        u_done = gel_free
        w8(pg, "tensor", *u_done)
        bks = [nb() for _ in range(8)]
        for m in range(8):
            pe_group(ctx, pg, bks[m], [
                (lambda eng, j=j, m=m, bk=bks[m]: nc.tensor.matmul(
                    ctx.ps[bk][:, :], W2p[:, j, 128 * m:128 * (m + 1)],
                    u[:, j, :], start=(j == 0), stop=(j == NJ - 1)))
                for j in range(NJ)])
        w8(pg, "vector", sO, sO0 + 32 * i)
        for m in range(8):
            w8(pg, "vector", ctx.mmS[bks[m]], ctx.fills[bks[m]])
            pg.em("vector", lambda eng, m=m, bk=bks[m]: eng.tensor_scalar(
                ost[:, m, :], ctx.ps[bk][:, 0:TT // 2], b2c[:, m:m + 1],
                None, ALU.add))
        dvA = tincL(ctx, pg, "vector")
        w8(pg, "gpsimd", tDV, dvA)
        if i + 1 < NT:
            w8(pg, "gpsimd", sX, sX0 + 16 * (i + 2))
        dma(pg, "gpsimd", fm_slice(dr["R"], s0, TT // 2), ost[:, :, :], sO,
            accum_op=ALU.add)
        w8(pg, "vector", sO, sO0 + 32 * i + 16)
        ostB_done = None
        for m in range(8):
            consume(ctx, pg, "vector", bks[m], [
                lambda eng, m=m, bk=bks[m]: eng.tensor_scalar(
                    ost[:, m, :], ctx.ps[bk][:, TT // 2:TT], b2c[:, m:m + 1],
                    None, ALU.add)])
            ostB_done = (ctx.evS[bks[m]], ctx.evS[bks[m]].n)
        w8(pg, "gpsimd", *ostB_done)
        dma(pg, "gpsimd", fm_slice(dr["R"], s0 + TT // 2, TT // 2),
            ost[:, :, :], sO, accum_op=ALU.add)
    w8(pg, "gpsimd", sO, sO.n)
    tinc(ctx, pg, "vector")
    tinc(ctx, pg, "scalar")
    tinc(ctx, pg, "tensor")


def ph_FIN(ctx, pg, y=None):
    nc, dr, cst, sb = ctx.nc, ctx.dr, ctx.cst, ctx.sb
    TT, NT = 512, 16
    rt = [sb(f"fn_r{b}", (128, 8, TT)) for b in range(2)]
    r1t = [sb(f"fn_r1{b}", (128, 8, TT)) for b in range(2)]
    sm = sb("fn_sm", (128, 8, TT))
    yt = sb("fn_yt", (128, 4, D))
    sR = [ctx.dsemf(f"fnr{b}") for b in range(2)]
    sY = ctx.dsemf("fny")
    tPE, tDV, tAC = ctx.tile["tensor"], ctx.tile["vector"], ctx.tile["scalar"]
    pe_base, dv_base = tPE.n, tDV.n
    nb = bank_cycler()

    def load(i):
        if i >= NT:
            return
        b = i % 2
        w8(pg, "sync", tDV, dv_base + max(0, i - 1))
        dma(pg, "sync", rt[b][:, :, :], fm_slice(dr["R"], TT * i, TT), sR[b])
        dma(pg, "sync", r1t[b][:, :, :], fm_slice(dr["R1"], TT * i, TT), sR[b])

    load(0)
    for i in range(NT):
        b = i % 2
        load(i + 1)
        w8(pg, "vector", sR[b], 32 * (i // 2 + 1))
        w8(pg, "vector", tPE, pe_base + max(0, i))
        pg.em("vector", lambda eng, b=b: eng.tensor_tensor(
            sm.ap().rearrange("p c s -> p (c s)"),
            rt[b].ap().rearrange("p c s -> p (c s)"),
            r1t[b].ap().rearrange("p c s -> p (c s)"), ALU.add))
        dv1 = tinc(ctx, pg, "vector")
        w8(pg, "tensor", tDV, dv1)
        w8(pg, "scalar", sY, 16 * i)
        for a in range(4):
            for half in range(2):
                bk = nb()
                pe_group(ctx, pg, bk, [
                    (lambda eng, a=a, c=4 * half + cc, cc=cc, bk=bk:
                     nc.tensor.matmul(
                        ctx.ps[bk][:, 128 * cc:128 * (cc + 1)],
                        sm[:, c, 128 * a:128 * (a + 1)],
                        cst.identF[:, :], start=(cc == 0), stop=(cc == 3)))
                    for cc in range(4)])
                consume(ctx, pg, "scalar", bk, [
                    lambda eng, a=a, half=half, bk=bk: eng.copy(
                        yt[:, a, 512 * half:512 * (half + 1)],
                        ctx.ps[bk][:, :])])
        tinc(ctx, pg, "tensor")
        ac1 = tinc(ctx, pg, "scalar")
        w8(pg, "sync", tAC, ac1)
        dma(pg, "sync",
            y.ap()[TT * i:TT * (i + 1), :].rearrange("(a p) e -> p a e", p=128),
            yt[:, :, :], sY)
    w8(pg, "sync", sY, sY.n)


def kernel(**inputs):
    shared, shards = _host_prep(inputs)
    nc = build_program()
    in_maps = []
    for c in range(NCORES):
        m = dict(shared)
        m["x0"] = shards[c]
        in_maps.append(m)
    res = run_bass_kernel_spmd(nc, in_maps, core_ids=list(range(NCORES)))
    out = np.concatenate([np.asarray(res.results[c]["y"]).reshape(2, SEQ, D)
                          for c in range(NCORES)], axis=0)
    return out.reshape(16, SEQ, D)


# version 06
# speedup vs baseline: 1.2920x; 1.0400x over previous
import contextlib

import numpy as np

import concourse.bass as bass
import concourse.mybir as mybir
from concourse.bass_utils import run_bass_kernel_spmd

F32 = mybir.dt.float32
BF16 = mybir.dt.bfloat16
AF = mybir.ActivationFunctionType
ALU = mybir.AluOpType
AX = mybir.AxisListType

NCORES = 8
D = 1024
S = 8192
SEQ = 4096
NL = 6
HEADS = 16
DH = 64
WIN = 256
NWIN = SEQ // WIN
DI = 2816
NJ = DI // 128
ERR_CLIP = 5.0
GRAD_CLIP = 1.0
TTT_SCALE = 2.0 / 4096.0
LN_EPS = 1e-5
LN8 = float(np.log(8.0))
NEGM = -1.0e30
ENGINES = ("sync", "gpsimd", "vector", "scalar", "tensor")



def _host_consts():
    c = {}
    c["identF"] = np.eye(128, dtype=np.float32)
    c["onesCol"] = np.ones((128, 1), dtype=np.float32)
    c["epsCol"] = np.full((128, 1), LN_EPS, dtype=np.float32)
    c["ln8Col"] = np.full((128, 1), LN8, dtype=np.float32)
    c["onesRow"] = np.ones((1, 128), dtype=np.float32)
    hs = np.zeros((128, 8, HEADS), dtype=np.float32)
    for ch in range(8):
        hs[:64, ch, 2 * ch] = 1.0
        hs[64:, ch, 2 * ch + 1] = 1.0
    c["Hsel"] = hs
    p = np.arange(128)[:, None]
    j = np.arange(384)[None, :]
    c["maskW"] = np.where((j >= p) & (j <= p + 256), 0.0, NEGM).astype(np.float32)
    j2 = np.arange(256)[None, :]
    c["mask0a"] = np.where(j2 <= p, 0.0, NEGM).astype(np.float32)
    c["mask0b"] = np.where(j2 <= p + 128, 0.0, NEGM).astype(np.float32)
    BIG = 3.0e38
    qq = np.arange(256)[None, :]
    va = [qq <= p, (qq >= p - 128) & (qq <= p + 128), qq >= p, qq >= p + 128]
    c["maskTA"] = np.concatenate(
        [np.where(va[0], BIG, 0.0), np.where(va[1], BIG, 0.0)],
        axis=1).astype(np.float32)
    c["maskTB"] = np.concatenate(
        [np.where(va[2], BIG, 0.0), np.where(va[3], BIG, 0.0)],
        axis=1).astype(np.float32)
    hst = np.zeros((HEADS, 8, 128), dtype=np.float32)
    for ch in range(8):
        hst[2 * ch, ch, :64] = 1.0
        hst[2 * ch + 1, ch, 64:] = 1.0
    c["HselT"] = hst
    half = np.arange(0, DH, 2, dtype=np.float64)
    inv_freq = 1.0 / 10000.0 ** (half / DH)
    t = np.arange(SEQ, dtype=np.float64)
    freqs = np.concatenate([t[:, None] * inv_freq[None, :]] * 2, axis=1)
    base_scale = (half + 0.4 * DH) / (1.4 * DH)
    power = (t - SEQ // 2) / (WIN // 2)
    scale = np.concatenate([base_scale[None, :] ** power[:, None]] * 2, axis=1)
    cos, sin = np.cos(freqs), np.sin(freqs)
    sgn = np.ones(DH)
    sgn[: DH // 2] = -1.0
    for nm, tb in (("CQ", cos * scale), ("SQ", sin * scale * sgn[None, :]),
                   ("CK", cos / scale), ("SK", sin / scale * sgn[None, :])):
        tbT = tb.T
        c["rot" + nm] = np.ascontiguousarray(
            np.concatenate([tbT, tbT], axis=0).astype(np.float32))
    return c


def _chunk_cols(v, n):
    return np.ascontiguousarray(v.reshape(n, 128).T.astype(np.float32))


def _host_prep(inp):
    f = lambda a: np.ascontiguousarray(np.asarray(a, dtype=np.float32))
    tr = lambda a: np.ascontiguousarray(np.asarray(a, dtype=np.float32).T)
    sh = dict(_host_consts())
    sh["WkT"] = tr(inp["L0_attn_Wk"])
    sh["WvT"] = tr(inp["L0_attn_Wv"])
    sh["WqT"] = tr(inp["L0_attn_Wq"])
    sh["W0T"] = tr(inp["L0_attn_W0"])
    sh["W1T"] = tr(inp["L0_attn_W1"])
    sh["W1N"] = f(inp["L0_attn_W1"])
    sh["WoutT"] = tr(inp["L0_attn_Wout"])
    sh["Wg3"] = np.ascontiguousarray(np.concatenate(
        [f(inp["L0_attn_Wg_decay"]), f(inp["L0_attn_Wg_lr"]),
         f(inp["L0_attn_Wg_mom"])], axis=0).T)
    sh["bg3"] = np.ascontiguousarray(np.stack(
        [f(inp["L0_attn_bg_decay"]).ravel(), f(inp["L0_attn_bg_lr"]).ravel(),
         f(inp["L0_attn_bg_mom"]).ravel()]).reshape(3, 1))
    for i in range(NL):
        sh[f"Wqkv{i}"] = f(inp["L0_ffn_Wqkv"][i])
        sh[f"Wo{i}"] = f(inp["L0_ffn_Wo"][i])
        w1 = f(inp["L0_ffn_W1"][i])
        sh[f"W1a{i}"] = np.ascontiguousarray(np.pad(w1[:, :2730], ((0, 0), (0, 86))))
        sh[f"W1g{i}"] = np.ascontiguousarray(np.pad(w1[:, 2730:], ((0, 0), (0, 86))))
        sh[f"W2p{i}"] = np.ascontiguousarray(np.pad(f(inp["L0_ffn_W2"][i]),
                                                    ((0, 86), (0, 0))))
        b1 = f(inp["L0_ffn_b1"][i])
        sh[f"b1a{i}"] = _chunk_cols(np.pad(b1[:2730], (0, 86)), NJ)
        sh[f"b1g{i}"] = _chunk_cols(np.pad(b1[2730:], (0, 86)), NJ)
        sh[f"b2c{i}"] = _chunk_cols(f(inp["L0_ffn_b2"][i]), 8)
        sh[f"lng{i}"] = _chunk_cols(f(inp["L0_ffn_ln_g"][i]), 8)
        sh[f"lnb{i}"] = _chunk_cols(f(inp["L0_ffn_ln_b"][i]), 8)
        mq = np.zeros((HEADS, D), dtype=np.float32)
        mk = np.zeros((HEADS, D), dtype=np.float32)
        for h in range(HEADS):
            mq[h, DH * h: DH * h + DH] = f(inp["L0_ffn_q_scale"][i])
            mk[h, DH * h: DH * h + DH] = f(inp["L0_ffn_k_scale"][i])
        sh[f"Mq{i}"] = mq
        sh[f"Mk{i}"] = mk
    x = np.asarray(inp["L0_attn_x"], dtype=np.float32)
    shards = [np.ascontiguousarray(x[2 * c: 2 * c + 2].reshape(S, D))
              for c in range(NCORES)]
    return sh, shards



class Sem:
    __slots__ = ("h", "n")

    def __init__(self, h):
        self.h = h
        self.n = 0


class Prog:
    def __init__(self):
        self.ops = {e: [] for e in ENGINES}
        self.anchor = {e: None for e in ENGINES}
        self.kind = {e: None for e in ENGINES}

    def em(self, e, fn):
        self.ops[e].append(fn)
        self.kind[e] = "plain"


class Ctx:
    pass


def w8(pg, e, sem, val):
    if val > 0:
        def fn(eng, s=sem.h, v=val):
            eng.wait_ge(s, v)
        pg.em(e, fn)
        pg.anchor[e] = ("wait",)
        pg.kind[e] = "wait"


def dma(pg, q, out, in_, sem, **kw):
    def fn(eng, o=out, i=in_, s=sem.h, kw=kw):
        eng.dma_start(out=o, in_=i, **kw).then_inc(s, 16)
    pg.em(q, fn)
    sem.n += 16
    pg.anchor[q] = (sem.h, sem.n)
    pg.kind[q] = "inc"
    return sem.n


def tinc(ctx, pg, e, key=None):
    sem = ctx.tile[key or e]

    def fn(eng, s=sem.h):
        return eng.drain().then_inc(s, 1)
    pg.em(e, fn)
    pg.anchor[e] = (sem.h, sem.n + 1)
    pg.kind[e] = "inc"
    sem.n += 1
    return sem.n


def tincL(ctx, pg, e, key=None):
    sem = ctx.tile[key or e]
    assert pg.kind[e] == "plain", pg.kind[e]
    f = pg.ops[e][-1]

    def wrap(eng, f=f, s=sem.h):
        return f(eng).then_inc(s, 1)
    pg.ops[e][-1] = wrap
    sem.n += 1
    pg.anchor[e] = (sem.h, sem.n)
    pg.kind[e] = "inc"
    return sem.n


def pe_group(ctx, pg, b, fns):
    w8(pg, "tensor", ctx.evS[b], ctx.frees[b])
    for f in fns[:-1]:
        pg.em("tensor", f)
    pg.em("tensor", lambda eng, f=fns[-1], s=ctx.mmS[b].h: f(eng).then_inc(s, 1))
    ctx.mmS[b].n += 1
    ctx.fills[b] += 1
    pg.anchor["tensor"] = (ctx.mmS[b].h, ctx.mmS[b].n)
    pg.kind["tensor"] = "inc"


def consume(ctx, pg, e, b, fns):
    w8(pg, e, ctx.mmS[b], ctx.fills[b])
    for f in fns[:-1]:
        pg.em(e, f)
    pg.em(e, lambda eng, f=fns[-1], s=ctx.evS[b].h: f(eng).then_inc(s, 1))
    ctx.evS[b].n += 1
    ctx.frees[b] += 1
    pg.anchor[e] = (ctx.evS[b].h, ctx.evS[b].n)
    pg.kind[e] = "inc"


def run_phase(ctx, pg):
    p = ctx.phase_idx
    ctx.phase_idx += 1
    blk = ctx.blk
    for name in ENGINES:
        sec = getattr(blk, name)

        anchor = pg.anchor[name]

        @sec
        def _(eng, name=name, p=p, ops=pg.ops[name], anchor=anchor):
            if p > 0:
                for s in ctx.ph.values():
                    eng.wait_ge(s.h, p)
            for fn in ops:
                fn(eng)
            if anchor is not None and anchor[0] != "wait":
                eng.wait_ge(anchor[0], anchor[1])
            eng.sem_inc(ctx.ph[name].h, 1)
    for name in ENGINES:
        ctx.ph[name].n += 1


def spacer(pg, ctx, e="vector"):
    pg.em(e, lambda eng: eng.tensor_scalar(
        ctx.junk[:, 0:256], ctx.junk[:, 0:256], 1.0, None, ALU.mult))


def rsqrt_nr(pg, y, ve, tmp, n=2):
    for _ in range(n):
        pg.em("vector", lambda eng, y=y, t=tmp: eng.tensor_mul(t, y, y))
        pg.em("vector", lambda eng, v=ve, t=tmp: eng.tensor_mul(t, t, v))
        pg.em("vector", lambda eng, t=tmp: eng.tensor_scalar(
            t, t, -0.5, 1.5, ALU.mult, ALU.add))
        pg.em("vector", lambda eng, y=y, t=tmp: eng.tensor_mul(y, y, t))


def fm_slice(t, s0, n):
    return t.ap()[:, :, s0:s0 + n].transpose([1, 0, 2])


def wchunks(dr_name, ctx):
    return ctx.dr[dr_name].ap().rearrange("(c p) n -> p c n", p=128)


def bank_cycler(start=0, n=8):
    i = [start]

    def nb():
        b = i[0] % n
        i[0] += 1
        return b
    return nb



def ph_P0(ctx, pg):
    dr, cst = ctx.dr, ctx.cst
    sG = ctx.dsemf("p0g")
    sS = ctx.dsemf("p0s")
    dma(pg, "gpsimd", cst.identB[:, :], dr["identF"].ap(), sG)
    dma(pg, "gpsimd", cst.onesColB[:, :], dr["onesCol"].ap(), sG)
    dma(pg, "gpsimd", cst.onesRowB[:, :], dr["onesRow"].ap(), sG)
    dma(pg, "gpsimd", cst.HselB[:, :, :], dr["Hsel"].ap(), sG)
    dma(pg, "gpsimd", cst.maskWb[:, :], dr["maskW"].ap(), sG)
    dma(pg, "gpsimd", cst.mask0ab[:, :], dr["mask0a"].ap(), sG)
    dma(pg, "gpsimd", cst.mask0bb[:, :], dr["mask0b"].ap(), sG)
    dma(pg, "gpsimd", cst.maskTAb[:, :], dr["maskTA"].ap(), sG)
    dma(pg, "gpsimd", cst.maskTBb[:, :], dr["maskTB"].ap(), sG)
    dma(pg, "gpsimd", cst.HselT[:, :, :], dr["HselT"].ap(), sG)
    dma(pg, "sync", cst.identF[:, :], dr["identF"].ap(), sS)
    dma(pg, "sync", cst.bg3[:, :], dr["bg3"].ap(), sS)
    dma(pg, "sync", cst.epsCol[:, :], dr["epsCol"].ap(), sS)
    dma(pg, "sync", cst.ln8Col[:, :], dr["ln8Col"].ap(), sS)
    w8(pg, "gpsimd", sG, sG.n)
    w8(pg, "sync", sS, sS.n)


def ph_A1(ctx, pg):
    import os
    nc, dr, cst, sb = ctx.nc, ctx.dr, ctx.cst, ctx.sb
    TT, NT = 512, 16
    if os.environ.get("DBG_NT1"):
        NT = 1
    Wk = sb("a1_Wk", (128, 8, D), BF16)
    Wv = sb("a1_Wv", (128, 8, D), BF16)
    Wq = sb("a1_Wq", (128, 8, D), BF16)
    xt = [sb(f"a1_xt{b}", (128, 4, D)) for b in range(2)]
    xnt = [sb(f"a1_xnt{b}", (128, 4, D), BF16) for b in range(2)]
    xnf = [sb(f"a1_xnf{b}", (128, 8, TT), BF16) for b in range(2)]
    rstage = sb("a1_rst", (128, 8, TT))
    qf = sb("a1_qf", (128, 8, TT), BF16)
    kf = sb("a1_kf", (128, 8, TT), BF16)
    vf = sb("a1_vf", (128, 8, TT), BF16)
    stS = sb("a1_sts", (128, 64))
    stQ = sb("a1_stq", (128, 64))
    stM = sb("a1_stm", (128, 64))
    stV = sb("a1_stv", (128, 64))
    stE = sb("a1_ste", (128, 64))
    stY = sb("a1_sty", (128, 64))
    stT = sb("a1_stt", (128, 64))
    gsum = sb("a1_gsum", (128, 64))
    sW = ctx.dsemf("a1W")
    sX = [ctx.dsemf(f"a1x{b}") for b in range(2)]
    sRst = ctx.dsemf("a1rs")
    sQKV = ctx.dsemf("a1qkv")
    tPE, tDV, tAC = ctx.tile["tensor"], ctx.tile["vector"], ctx.tile["scalar"]

    dma(pg, "gpsimd", Wk[:, :, :], wchunks("WkT", ctx), sW)
    dma(pg, "gpsimd", Wv[:, :, :], wchunks("WvT", ctx), sW)
    dma(pg, "gpsimd", Wq[:, :, :], wchunks("WqT", ctx), sW)
    pg.em("vector", lambda eng: eng.memset(ctx.sxn[:, :], 0.0))
    x0 = dr["x0"].ap()
    junk = ctx.junk
    w8(pg, "tensor", sW, 48)
    nb = bank_cycler()
    pe_base, dv_base, ac_base = tPE.n, tDV.n, tAC.n

    def load(i):
        if i >= NT:
            return
        b = i % 2
        w8(pg, "sync", tDV, dv_base + max(0, i - 1) * 4)
        w8(pg, "sync", tPE, pe_base + max(0, i - 1))
        w8(pg, "sync", tAC, ac_base + max(0, i - 1) * 4)
        dma(pg, "sync", xt[b][:, :, :],
            x0[TT * i: TT * (i + 1), :].rearrange("(a p) e -> p a e", p=128),
            sX[b])

    load(0)
    for i in range(NT):
        b = i % 2
        load(i + 1)
        X, XN = xt[b], xnt[b]
        nld = 16 * (i // 2 + 1)
        w8(pg, "scalar", sX[b], nld)
        for a in range(4):
            pg.em("scalar", lambda eng, X=X, a=a: eng.activation(
                junk[:, :D], X[:, a, :], AF.Copy, accum_out=stS[:, a:a + 1]))
            pg.em("scalar", lambda eng, X=X, a=a: eng.activation(
                junk[:, :D], X[:, a, :], AF.Square, accum_out=stQ[:, a:a + 1]))
        ac1 = tinc(ctx, pg, "scalar")
        w8(pg, "vector", tAC, ac1)

        def dve_stats(eng):
            eng.tensor_scalar(stM[:, :], stS[:, :], 1.0 / D, None, ALU.mult)
            eng.tensor_scalar(stV[:, :], stQ[:, :], 1.0 / D, None, ALU.mult)
            eng.tensor_mul(stT[:, :], stM[:, :], stM[:, :])
            eng.tensor_sub(stV[:, :], stV[:, :], stT[:, :])
            return eng.tensor_scalar(stE[:, :], stV[:, :], LN_EPS, None, ALU.add)
        pg.em("vector", dve_stats)
        dv1 = tinc(ctx, pg, "vector")
        w8(pg, "scalar", tDV, dv1)
        pg.em("scalar", lambda eng: eng.activation(
            stY[:, :], stV[:, :], AF.Ln, bias=cst.epsCol[:, :]))
        pg.em("scalar", lambda eng: eng.activation(
            stY[:, :], stY[:, :], AF.Exp, scale=-0.5))
        ac2 = tinc(ctx, pg, "scalar")
        w8(pg, "vector", tAC, ac2)
        rsqrt_nr(pg, stY[:, :], stE[:, :], stT[:, :])
        w8(pg, "vector", tPE, pe_base + max(0, i - 1))

        def dve_norm(eng, X=X, XN=XN):
            ins = None
            for a in range(4):
                ins = eng.tensor_scalar(XN[:, a, :], X[:, a, :], stM[:, a:a + 1],
                                        stY[:, a:a + 1], ALU.subtract, ALU.mult)
            return ins
        pg.em("vector", dve_norm)
        w8(pg, "tensor", sX[b], nld)
        w8(pg, "scalar", sRst, 16 * i)
        for c in range(8):
            bk = nb()
            pe_group(ctx, pg, bk, [
                (lambda eng, X=X, a=a, c=c, bk=bk: nc.tensor.matmul(
                    ctx.ps[bk][:, 128 * a:128 * (a + 1)],
                    X[:, a, 128 * c:128 * (c + 1)], cst.identF[:, :],
                    start=(a == 0), stop=(a == 3))) for a in range(4)])
            consume(ctx, pg, "scalar", bk, [
                lambda eng, c=c, bk=bk: eng.copy(rstage[:, c, :], ctx.ps[bk][:, :])])
        acR = tinc(ctx, pg, "scalar")
        w8(pg, "gpsimd", tAC, acR)
        dma(pg, "gpsimd", fm_slice(ctx.dr["R"], TT * i, TT), rstage[:, :, :], sRst)
        dv2 = tinc(ctx, pg, "vector")
        w8(pg, "tensor", tDV, dv2)
        for c in range(8):
            bk = nb()
            pe_group(ctx, pg, bk, [
                (lambda eng, XN=XN, a=a, c=c, bk=bk: nc.tensor.matmul(
                    ctx.ps[bk][:, 128 * a:128 * (a + 1)],
                    XN[:, a, 128 * c:128 * (c + 1)], cst.identB[:, :],
                    start=(a == 0), stop=(a == 3))) for a in range(4)])
            consume(ctx, pg, "scalar", bk, [
                lambda eng, c=c, bk=bk, b=b: eng.copy(
                    xnf[b][:, c, :], ctx.ps[bk][:, :])])
        acX = tinc(ctx, pg, "scalar")
        w8(pg, "vector", tAC, acX)
        pg.em("vector", lambda eng, b=b: eng.tensor_reduce(
            gsum[:, 0:8], xnf[b][:, :, :], AX.X, ALU.add))
        spacer(pg, ctx)
        pg.em("vector", lambda eng: eng.tensor_add(
            ctx.sxn[:, :], ctx.sxn[:, :], gsum[:, 0:8]))
        dv3 = tinc(ctx, pg, "vector")
        w8(pg, "tensor", tAC, acX)
        w8(pg, "tensor", tDV, dv3)
        first_ev = {"vector": True, "scalar": True}
        for t_i, (Wt, dst) in enumerate([(Wk, kf), (Wv, vf), (Wq, qf)]):
            for m in range(8):
                bk = nb()
                pe_group(ctx, pg, bk, [
                    (lambda eng, Wt=Wt, kc=kc, m=m, bk=bk, b=b: nc.tensor.matmul(
                        ctx.ps[bk][:, :], Wt[:, kc, 128 * m:128 * (m + 1)],
                        xnf[b][:, kc, :], start=(kc == 0), stop=(kc == 7)))
                    for kc in range(8)])
                e = "vector" if (m + t_i) % 2 == 0 else "scalar"
                if first_ev[e]:
                    w8(pg, e, sQKV, 16 * 3 * i)
                    first_ev[e] = False
                if e == "vector":
                    fn = (lambda eng, dst=dst, m=m, bk=bk: eng.tensor_copy(
                        dst[:, m, :], ctx.ps[bk][:, :]))
                else:
                    fn = (lambda eng, dst=dst, m=m, bk=bk: eng.copy(
                        dst[:, m, :], ctx.ps[bk][:, :]))
                consume(ctx, pg, e, bk, fn if isinstance(fn, list) else [fn])
        pe1 = tinc(ctx, pg, "tensor")
        dv4 = tinc(ctx, pg, "vector")
        ac3 = tinc(ctx, pg, "scalar")
        w8(pg, "gpsimd", tDV, dv4)
        w8(pg, "gpsimd", tAC, ac3)
        w8(pg, "gpsimd", ctx.tile["tensor"], pe1)

        dma(pg, "gpsimd", fm_slice(dr["kS"], TT * i, TT), kf[:, :, :], sQKV)
        dma(pg, "gpsimd", fm_slice(dr["vS"], TT * i, TT), vf[:, :, :], sQKV)
        dma(pg, "gpsimd", fm_slice(dr["qnS"], TT * i, TT), qf[:, :, :], sQKV)

    w8(pg, "gpsimd", sQKV, sQKV.n)
    w8(pg, "gpsimd", sRst, sRst.n)
    if os.environ.get("DBG_NT1"):
        sdbg = ctx.dsemf("a1dbg2")
        sdbg2 = ctx.dsemf("a1dbg3")
        w8(pg, "sync", ctx.tile["tensor"], ctx.tile["tensor"].n)
        w8(pg, "gpsimd", ctx.tile["tensor"], ctx.tile["tensor"].n)
        dma(pg, "gpsimd", dr["dbgA"].ap()[0:128, 0:4096],
            xnt[0].ap().rearrange("p a e -> p (a e)"), sdbg2)
        dma(pg, "gpsimd", dr["dbgA"].ap()[128:256, 0:4096],
            xnf[0].ap().rearrange("p c s -> p (c s)"), sdbg2)
        dma(pg, "sync", dr["dbgA"].ap()[256:384, 0:24],
            stat.ap().rearrange("p a k -> p (a k)"), sdbg)
        w8(pg, "sync", sdbg, sdbg.n)
        w8(pg, "gpsimd", sdbg2, sdbg2.n)



def build_program(upto=None, taps=()):
    nc = bass.Bass("TRN2", target_bir_lowering=False, debug=False)
    ctx = Ctx()
    ctx.nc = nc
    ctx.phase_idx = 0
    dr = {}

    def din(name, shape, dtype=F32):
        dr[name] = nc.dram_tensor(name, list(shape), dtype, kind="ExternalInput")

    def dint(name, shape, dtype=F32):
        dr[name] = nc.dram_tensor(name, list(shape), dtype, kind="Internal")

    din("x0", (S, D))
    for nm in ("WkT", "WvT", "WqT", "W0T", "W1T", "W1N", "WoutT"):
        din(nm, (D, D))
    din("Wg3", (D, 3))
    din("bg3", (3, 1))
    din("identF", (128, 128))
    din("onesCol", (128, 1))
    din("epsCol", (128, 1))
    din("ln8Col", (128, 1))
    din("onesRow", (1, 128))
    din("Hsel", (128, 8, HEADS))
    din("maskW", (128, 384))
    din("mask0a", (128, 256))
    din("mask0b", (128, 256))
    din("maskTA", (128, 512))
    din("maskTB", (128, 512))
    din("HselT", (HEADS, 8, 128))
    for nm in ("rotCQ", "rotSQ", "rotCK", "rotSK"):
        din(nm, (128, SEQ))
    for i in range(NL):
        din(f"Wqkv{i}", (D, 3 * D))
        din(f"Wo{i}", (D, D))
        din(f"W1a{i}", (D, DI))
        din(f"W1g{i}", (D, DI))
        din(f"W2p{i}", (DI, D))
        din(f"b1a{i}", (128, NJ))
        din(f"b1g{i}", (128, NJ))
        din(f"b2c{i}", (128, 8))
        din(f"lng{i}", (128, 8))
        din(f"lnb{i}", (128, 8))
        din(f"Mq{i}", (HEADS, D))
        din(f"Mk{i}", (HEADS, D))
    y = nc.dram_tensor("y", [S, D], F32, kind="ExternalOutput")
    dint("R", (8, 128, S))
    dint("R1", (8, 128, S))
    dint("kS", (8, 128, S), BF16)
    dint("vS", (8, 128, S), BF16)
    dint("qnS", (8, 128, S), BF16)
    dint("aqnS", (8, 128, S), BF16)
    dint("aknS", (8, 128, S), BF16)
    dint("avtS", (64, 128, HEADS, DH + 1), BF16)
    dint("dbgA", (384, 4096))
    dint("rows", (2, S), BF16)
    dint("rrow", (HEADS * WIN,), BF16)
    dint("W1ab", (D, DI), BF16)
    dint("W1gb", (D, DI), BF16)
    dint("W2pb", (DI, D), BF16)
    dint("Wqkvb", (D, 3 * D), BF16)
    dint("Wob", (D, D), BF16)
    dint("ccI", (2051, D))
    dint("ccO", (2051, D))
    ctx.dr = dr

    tap_outs = {}
    for t in taps:
        h = dr[t]
        tap_outs[t] = nc.dram_tensor("tap_" + t, list(h.shape), h.dtype,
                                     kind="ExternalOutput")

    with contextlib.ExitStack() as stk:
        ctx.stk = stk
        nsem = [0]

        def new_sem(tag):
            nsem[0] += 1
            return Sem(stk.enter_context(nc.semaphore(f"s{nsem[0]}_{tag}")))

        ctx.new_sem = new_sem
        ctx.ps = [stk.enter_context(nc.psum_tensor(f"ps{i}", [128, 512], F32))
                  for i in range(8)]
        ctx.mmS = [new_sem(f"mm{i}") for i in range(8)]
        ctx.evS = [new_sem(f"ev{i}") for i in range(8)]
        ctx.fills = [0] * 8
        ctx.frees = [0] * 8
        ctx.ph = {e: new_sem("ph_" + e) for e in ENGINES}
        ctx.dsem = {}

        def dsemf(tag):
            if tag not in ctx.dsem:
                ctx.dsem[tag] = new_sem("d_" + tag)
            return ctx.dsem[tag]

        ctx.dsemf = dsemf
        ctx.tile = {e: new_sem("t_" + e) for e in
                    ("vector", "scalar", "tensor", "gpsimd", "sync")}
        ctx.ccS = new_sem("cc")
        ctx.scope = [stk]

        nsb = [0]

        def sb(name, shape, dtype=F32):
            nsb[0] += 1
            return ctx.scope[-1].enter_context(
                nc.sbuf_tensor(f"{name}_{nsb[0]}", list(shape), dtype))

        ctx.sb = sb

        with nc.Block() as blk:
            ctx.blk = blk
            cst = Ctx()
            cst.identF = stk.enter_context(nc.sbuf_tensor("c_idF", [128, 128], F32))
            cst.identB = stk.enter_context(nc.sbuf_tensor("c_idB", [128, 128], BF16))
            cst.onesColB = stk.enter_context(nc.sbuf_tensor("c_o1", [128, 1], BF16))
            cst.onesRowB = stk.enter_context(nc.sbuf_tensor("c_or", [1, 128], BF16))
            cst.HselB = stk.enter_context(nc.sbuf_tensor("c_hs", [128, 8, HEADS], BF16))
            cst.maskWb = stk.enter_context(nc.sbuf_tensor("c_mw", [128, 384], BF16))
            cst.mask0ab = stk.enter_context(nc.sbuf_tensor("c_m0a", [128, 256], BF16))
            cst.mask0bb = stk.enter_context(nc.sbuf_tensor("c_m0b", [128, 256], BF16))
            cst.maskTAb = stk.enter_context(nc.sbuf_tensor("c_mta", [128, 512], BF16))
            cst.maskTBb = stk.enter_context(nc.sbuf_tensor("c_mtb", [128, 512], BF16))
            cst.HselT = stk.enter_context(
                nc.sbuf_tensor("c_hst", [HEADS, 8, 128], BF16))
            cst.bg3 = stk.enter_context(nc.sbuf_tensor("c_bg3", [3, 1], F32))
            cst.epsCol = stk.enter_context(nc.sbuf_tensor("c_eps", [128, 1], F32))
            cst.ln8Col = stk.enter_context(nc.sbuf_tensor("c_ln8", [128, 1], F32))
            ctx.cst = cst
            ctx.junk = stk.enter_context(nc.sbuf_tensor("junk", [128, D], BF16))
            ctx.sxn = stk.enter_context(nc.sbuf_tensor("sxn", [128, 8], F32))

            done = [False]

            def phase(fn, **kw):
                if done[0]:
                    return
                with contextlib.ExitStack() as pscope:
                    ctx.scope.append(pscope)
                    pg = Prog()
                    fn(ctx, pg, **kw)
                    run_phase(ctx, pg)
                    ctx.scope.pop()
                name = fn.__name__ + str(kw.get("layer", ""))
                if upto is not None and name == upto:
                    done[0] = True

            phase(ph_P0)
            phase(ph_A1)
            phase(ph_A2)
            with contextlib.ExitStack() as wscope:
                ctx.W0nT = wscope.enter_context(
                    nc.sbuf_tensor("W0nT", [128, 8, D], BF16))
                ctx.W1nT = wscope.enter_context(
                    nc.sbuf_tensor("W1nT", [128, 8, D], BF16))
                ctx.WoutTb = wscope.enter_context(
                    nc.sbuf_tensor("WoutTb", [128, 8, D], BF16))
                phase(ph_AR)
                phase(ph_E)
            phase(ph_E1)
            for i in range(NL):
                with contextlib.ExitStack() as lscope:
                    ctx.Wo_l = lscope.enter_context(
                        nc.sbuf_tensor(f"Wo_l{i}", [128, 8, D], BF16))
                    phase(ph_ATTN1, layer=i)
                    phase(ph_ATTN2, layer=i)
                with contextlib.ExitStack() as fscope:
                    ctx.W1a_l = fscope.enter_context(
                        nc.sbuf_tensor(f"W1a_l{i}", [128, 8, DI], BF16))
                    ctx.W1g_l = fscope.enter_context(
                        nc.sbuf_tensor(f"W1g_l{i}", [128, 8, DI], BF16))
                    ctx.W2p_l = fscope.enter_context(
                        nc.sbuf_tensor(f"W2p_l{i}", [128, NJ, D], BF16))
                    phase(ph_FFN0, layer=i)
                    phase(ph_FFN1, layer=i)
            phase(ph_FIN, y=y)

            if upto is not None:
                pg = Prog()
                sst = dsemf("dump")
                dma(pg, "sync",
                    y.ap().rearrange("(c p a) e -> c p (a e)", c=8, p=128),
                    dr["R"].ap(), sst)
                for t, o in tap_outs.items():
                    dma(pg, "sync", o.ap(), dr[t].ap(), sst)
                w8(pg, "sync", sst, sst.n)
                run_phase(ctx, pg)
    return nc


def ph_A2(ctx, pg):
    nc, dr, cst, sb = ctx.nc, ctx.dr, ctx.cst, ctx.sb
    TT2, NT2 = 256, 32
    W0T = sb("a2_W0T", (128, 8, D), BF16)
    W1T = sb("a2_W1T", (128, 8, D), BF16)
    W1N = sb("a2_W1N", (128, 8, D), BF16)
    ctx.G1T = sb("a2_G1T", (128, 8, D))
    ctx.G0T = sb("a2_G0T", (128, 8, D))
    kf = [sb(f"a2_kf{b}", (128, 8, TT2), BF16) for b in range(2)]
    vf = [sb(f"a2_vf{b}", (128, 8, TT2), BF16) for b in range(2)]
    sg = sb("a2_sg", (128, 8, TT2), BF16)
    a1f = sb("a2_a1", (128, 8, TT2), BF16)
    sdf = sb("a2_sd", (128, 8, TT2), BF16)
    erf = sb("a2_er", (128, 8, TT2), BF16)
    d1f = sb("a2_d1", (128, 8, TT2), BF16)
    e1t = sb("a2_e1t", (128, TT2))
    t1t = sb("a2_t1t", (128, TT2), BF16)
    t2t = sb("a2_t2t", (128, TT2))
    kT = sb("a2_kT", (128, 2, 8, 128), BF16)
    a1T = sb("a2_a1T", (128, 2, 8, 128), BF16)
    erT = sb("a2_erT", (128, 2, 8, 128), BF16)
    d1T = sb("a2_d1T", (128, 2, 8, 128), BF16)
    sW = ctx.dsemf("a2W")
    sK = [ctx.dsemf(f"a2k{b}") for b in range(2)]
    sG = ctx.dsemf("a2G")
    tPE, tDV, tAC = ctx.tile["tensor"], ctx.tile["vector"], ctx.tile["scalar"]

    dma(pg, "gpsimd", W0T[:, :, :], wchunks("W0T", ctx), sW)
    dma(pg, "gpsimd", W1T[:, :, :], wchunks("W1T", ctx), sW)
    dma(pg, "gpsimd", W1N[:, :, :], wchunks("W1N", ctx), sW)
    pg.em("vector", lambda eng: eng.memset(ctx.G1T[:, :, :], 0.0))
    pg.em("vector", lambda eng: eng.memset(ctx.G0T[:, :, :], 0.0))
    w8(pg, "tensor", sW, sW.n)
    nb = bank_cycler()
    pe_base = tPE.n

    def load(i):
        if i >= NT2:
            return
        b = i % 2
        w8(pg, "sync", tPE, pe_base + max(0, i - 1))
        dma(pg, "sync", kf[b][:, :, :], fm_slice(dr["kS"], TT2 * i, TT2), sK[b])
        dma(pg, "sync", vf[b][:, :, :], fm_slice(dr["vS"], TT2 * i, TT2), sK[b])

    load(0)
    for i in range(NT2):
        b = i % 2
        load(i + 1)
        nld = 32 * (i // 2 + 1)
        w8(pg, "tensor", sK[b], nld)
        for m in range(8):
            bk = nb()
            pe_group(ctx, pg, bk, [
                (lambda eng, kc=kc, m=m, bk=bk, b=b: nc.tensor.matmul(
                    ctx.ps[bk][:, 0:TT2], W0T[:, kc, 128 * m:128 * (m + 1)],
                    kf[b][:, kc, :], start=(kc == 0), stop=(kc == 7)))
                for kc in range(8)])
            consume(ctx, pg, "scalar", bk, [
                lambda eng, m=m, bk=bk: eng.activation(
                    sg[:, m, :], ctx.ps[bk][:, 0:TT2], AF.Sigmoid)])
            acm = tinc(ctx, pg, "scalar")
            w8(pg, "vector", tAC, acm)

            def dve_sd(eng, m=m, bk=bk):
                ps = ctx.ps[bk][:, 0:TT2]
                eng.tensor_tensor(a1f[:, m, :], ps, sg[:, m, :], ALU.mult)
                eng.tensor_scalar(t1t[:, :], sg[:, m, :], -1.0, 1.0,
                                  ALU.mult, ALU.add)
                eng.tensor_tensor(t2t[:, :], ps, t1t[:, :], ALU.mult)
                return eng.scalar_tensor_tensor(
                    sdf[:, m, :], t2t[:, :], 1.0, sg[:, m, :], ALU.add, ALU.mult)
            consume(ctx, pg, "vector", bk, [dve_sd])
        dv_a = tinc(ctx, pg, "vector")
        w8(pg, "tensor", tDV, dv_a)
        for m in range(8):
            bk = nb()
            pe_group(ctx, pg, bk, [
                (lambda eng, kc=kc, m=m, bk=bk: nc.tensor.matmul(
                    ctx.ps[bk][:, 0:TT2], W1T[:, kc, 128 * m:128 * (m + 1)],
                    a1f[:, kc, :], start=(kc == 0), stop=(kc == 7)))
                for kc in range(8)])

            def dve_err(eng, m=m, bk=bk, b=b):
                eng.tensor_sub(e1t[:, :], ctx.ps[bk][:, 0:TT2], vf[b][:, m, :])
                return eng.tensor_scalar(erf[:, m, :], e1t[:, :], ERR_CLIP,
                                         -ERR_CLIP, ALU.min, ALU.max)
            consume(ctx, pg, "vector", bk, [dve_err])
        dv_b = tinc(ctx, pg, "vector")
        w8(pg, "tensor", tDV, dv_b)
        for m in range(8):
            bk = nb()
            pe_group(ctx, pg, bk, [
                (lambda eng, kc=kc, m=m, bk=bk: nc.tensor.matmul(
                    ctx.ps[bk][:, 0:TT2], W1N[:, kc, 128 * m:128 * (m + 1)],
                    erf[:, kc, :], start=(kc == 0), stop=(kc == 7)))
                for kc in range(8)])
            consume(ctx, pg, "vector", bk, [
                lambda eng, m=m, bk=bk: eng.tensor_tensor(
                    d1f[:, m, :], ctx.ps[bk][:, 0:TT2], sdf[:, m, :], ALU.mult)])
        dv_c = tinc(ctx, pg, "vector")
        w8(pg, "tensor", tDV, dv_c)
        for (srcT, dstT) in ((kf[b], kT), (a1f, a1T), (erf, erT), (d1f, d1T)):
            for c in range(8):
                bk = nb()
                pe_group(ctx, pg, bk, [
                    (lambda eng, srcT=srcT, c=c, u=u, bk=bk: nc.tensor.matmul(
                        ctx.ps[bk][:, 128 * u:128 * (u + 1)],
                        srcT[:, c, 128 * u:128 * (u + 1)], cst.identB[:, :],
                        start=(u == 0), stop=(u == 1))) for u in range(2)])
                consume(ctx, pg, "vector", bk, [
                    lambda eng, dstT=dstT, c=c, bk=bk: eng.tensor_copy(
                        dstT[:, :, c, :], ctx.ps[bk][:, 0:TT2])])
        dv_d = tinc(ctx, pg, "vector")
        w8(pg, "tensor", tDV, dv_d)
        for (lT, rT, acc) in ((a1T, erT, ctx.G1T), (kT, d1T, ctx.G0T)):
            for m in range(8):
                for h in range(2):
                    bk = nb()
                    pe_group(ctx, pg, bk, [
                        (lambda eng, lT=lT, rT=rT, m=m, h=h, u=u, bk=bk:
                         nc.tensor.matmul(
                             ctx.ps[bk][:, :],
                             lT[:, u, m, :],
                             rT[:, u, 4 * h:4 * (h + 1), :], start=(u == 0),
                             stop=(u == 1))) for u in range(2)])
                    consume(ctx, pg, "vector", bk, [
                        lambda eng, acc=acc, m=m, h=h, bk=bk: eng.tensor_tensor(
                            acc[:, m, 512 * h:512 * (h + 1)],
                            acc[:, m, 512 * h:512 * (h + 1)],
                            ctx.ps[bk][:, :], ALU.add)])
        tinc(ctx, pg, "tensor")
    dv_e = tinc(ctx, pg, "vector")
    w8(pg, "sync", tDV, dv_e)
    dma(pg, "sync", dr["ccI"].ap()[0:1024, :].rearrange("(c p) d -> p c d", p=128),
        ctx.G1T[:, :, :], sG)
    dma(pg, "sync", dr["ccI"].ap()[1024:2048, :].rearrange("(c p) d -> p c d", p=128),
        ctx.G0T[:, :, :], sG)
    w8(pg, "sync", sG, sG.n)


def ph_AR(ctx, pg):
    nc, dr, cst, sb = ctx.nc, ctx.dr, ctx.cst, ctx.sb
    Wg3b = sb("ar_wg", (128, 8, 3), BF16)
    sxnb = sb("ar_sxnb", (128, 8), BF16)
    g3s = sb("ar_g3", (3, 1))
    a3r = sb("ar_a3r", (1, 3))
    a3rb = sb("ar_a3rb", (1, 3), BF16)
    gbf = sb("ar_gb", (128, 3))
    alc = sb("ar_alc", (128, 1))
    lrs = sb("ar_lrs", (128, 1))
    Gt = [sb(f"ar_gt{b}", (128, 1024)) for b in range(2)]
    Wf = [sb(f"ar_wf{b}", (128, 1024)) for b in range(2)]
    t2 = sb("ar_t2", (128, 1024))
    sL = ctx.dsemf("arL")
    sO = ctx.dsemf("arO")
    sW = ctx.dsemf("arW")
    tPE, tDV, tAC = ctx.tile["tensor"], ctx.tile["vector"], ctx.tile["scalar"]

    dma(pg, "gpsimd", Wg3b[:, :, :], dr["Wg3"].ap().rearrange(
        "(c p) n -> p c n", p=128), sL)
    dma(pg, "gpsimd", ctx.WoutTb[:, :, :], wchunks("WoutT", ctx), sL)
    pg.em("vector", lambda eng: eng.tensor_copy(sxnb[:, :], ctx.sxn[:, :]))
    dv1 = tinc(ctx, pg, "vector")
    w8(pg, "tensor", sL, 32)
    w8(pg, "tensor", tDV, dv1)
    bk = 0
    pe_group(ctx, pg, bk, [
        (lambda eng, kc=kc: nc.tensor.matmul(
            ctx.ps[0][0:3, 0:1], Wg3b[:, kc, :], sxnb[:, kc:kc + 1],
            start=(kc == 0), stop=(kc == 7))) for kc in range(8)])
    consume(ctx, pg, "vector", bk, [
        lambda eng: eng.tensor_copy(g3s[:, :], ctx.ps[0][0:3, 0:1])])
    dv2 = tinc(ctx, pg, "vector")
    w8(pg, "sync", tDV, dv2)
    dma(pg, "sync", dr["ccI"].ap()[2048:2051, 0:1], g3s[:, :],
        ctx.dsemf("arG"), allow_slow_non_contiguous=True)
    w8(pg, "sync", ctx.dsemf("arG"), ctx.dsemf("arG").n)
    sy1 = tinc(ctx, pg, "sync")
    w8(pg, "gpsimd", ctx.tile["sync"], sy1)
    w8(pg, "gpsimd", ctx.dsemf("a2G"), ctx.dsemf("a2G").n)

    def cc(eng):
        return eng.collective_compute(
            "AllReduce", ALU.add, replica_groups=[list(range(NCORES))],
            ins=[dr["ccI"].ap()], outs=[dr["ccO"].ap()])
    pg.em("gpsimd", lambda eng: cc(eng).then_inc(ctx.ccS.h))
    ctx.ccS.n += 1
    pg.anchor["gpsimd"] = (ctx.ccS.h, 1)
    pg.kind["gpsimd"] = "inc"
    w8(pg, "sync", ctx.ccS, 1)
    dma(pg, "sync", g3s[:, :], dr["ccO"].ap()[2048:2051, 0:1],
        ctx.dsemf("arG2"), allow_slow_non_contiguous=True)
    w8(pg, "scalar", ctx.dsemf("arG2"), 16)
    pg.em("scalar", lambda eng: eng.activation(
        g3s[:, :], g3s[:, :], AF.Sigmoid, bias=cst.bg3[:, :],
        scale=1.0 / (16 * SEQ)))
    ac1 = tinc(ctx, pg, "scalar")
    w8(pg, "tensor", tAC, ac1)
    pe_group(ctx, pg, 1, [lambda eng: nc.tensor.matmul(
        ctx.ps[1][0:1, 0:3], g3s[:, :], cst.identF[0:3, 0:3],
        start=True, stop=True)])
    consume(ctx, pg, "vector", 1, [
        lambda eng: eng.tensor_copy(a3r[:, :], ctx.ps[1][0:1, 0:3])])
    spacer(pg, ctx)
    pg.em("vector", lambda eng: eng.tensor_copy(a3rb[:, :], a3r[:, :]))
    dv3 = tinc(ctx, pg, "vector")
    w8(pg, "tensor", tDV, dv3)
    pe_group(ctx, pg, 2, [lambda eng: nc.tensor.matmul(
        ctx.ps[2][:, 0:3], cst.onesRowB[:, :], a3rb[:, :],
        start=True, stop=True)])
    consume(ctx, pg, "vector", 2, [
        lambda eng: eng.tensor_copy(gbf[:, :], ctx.ps[2][:, 0:3])])
    spacer(pg, ctx)
    pg.em("vector", lambda eng: eng.tensor_scalar(
        alc[:, :], gbf[:, 0:1], -1.0, 1.0, ALU.mult, ALU.add))
    spacer(pg, ctx)
    pg.em("vector", lambda eng: eng.tensor_copy(lrs[:, :], gbf[:, 1:2]))
    spacer(pg, ctx)
    dv4 = tinc(ctx, pg, "vector")
    ccO = dr["ccO"].ap()
    for wi, (rows, WTname, dst) in enumerate(
            ((1024, "W0T", ctx.W0nT), (0, "W1T", ctx.W1nT))):
        for c in range(8):
            b = (wi * 8 + c) % 2
            sGt = ctx.dsemf(f"arGt{b}")
            sWf = ctx.dsemf(f"arWf{b}")
            if wi * 8 + c >= 2:
                w8(pg, "sync", tDV, dv4 + (wi * 8 + c) - 1)
            dma(pg, "sync", Gt[b][:, :],
                ccO[rows:rows + 1024, :].rearrange("(c p) d -> p c d", p=128)[:, c, :],
                sGt)
            dma(pg, "sync", Wf[b][:, :],
                dr[WTname].ap().rearrange("(c p) d -> p c d", p=128)[:, c, :],
                sWf)
            w8(pg, "vector", sGt, sGt.n)
            w8(pg, "vector", sWf, sWf.n)

            def build(eng, b=b, c=c, dst=dst):
                eng.tensor_scalar(Gt[b][:, :], Gt[b][:, :], TTT_SCALE,
                                  GRAD_CLIP, ALU.mult, ALU.min)
                eng.tensor_scalar(t2[:, :], Gt[b][:, :], -GRAD_CLIP,
                                  lrs[:, :], ALU.max, ALU.mult)
                return eng.scalar_tensor_tensor(
                    dst[:, c, :], Wf[b][:, :], alc[:, :], t2[:, :],
                    ALU.mult, ALU.subtract)
            pg.em("vector", build)
            tinc(ctx, pg, "vector")
    w8(pg, "gpsimd", sL, sL.n)


def ph_E(ctx, pg):
    nc, dr, cst, sb = ctx.nc, ctx.dr, ctx.cst, ctx.sb
    TT, NT = 512, 16
    qf = [sb(f"e_qf{b}", (128, 8, TT), BF16) for b in range(2)]
    sg1 = sb("e_sg", (128, 8, TT), BF16)
    s1f = sb("e_s1", (128, 8, TT), BF16)
    h2f = sb("e_h2", (128, 8, TT), BF16)
    ost = sb("e_ost", (128, 8, TT))
    sQ = [ctx.dsemf(f"eq{b}") for b in range(2)]
    sE = ctx.dsemf("eR")
    tPE, tDV, tAC = ctx.tile["tensor"], ctx.tile["vector"], ctx.tile["scalar"]
    nb = bank_cycler()
    pe_base = tPE.n

    def load(i):
        if i >= NT:
            return
        b = i % 2
        w8(pg, "sync", tPE, pe_base + max(0, i - 1))
        dma(pg, "sync", qf[b][:, :, :], fm_slice(dr["qnS"], TT * i, TT), sQ[b])

    load(0)
    for i in range(NT):
        b = i % 2
        load(i + 1)
        w8(pg, "tensor", sQ[b], 16 * (i // 2 + 1))
        for m in range(8):
            bk = nb()
            pe_group(ctx, pg, bk, [
                (lambda eng, kc=kc, m=m, bk=bk, b=b: nc.tensor.matmul(
                    ctx.ps[bk][:, :], ctx.W0nT[:, kc, 128 * m:128 * (m + 1)],
                    qf[b][:, kc, :], start=(kc == 0), stop=(kc == 7)))
                for kc in range(8)])
            consume(ctx, pg, "scalar", bk, [
                lambda eng, m=m, bk=bk: eng.activation(
                    sg1[:, m, :], ctx.ps[bk][:, :], AF.Sigmoid)])
            acm = tinc(ctx, pg, "scalar")
            w8(pg, "vector", tAC, acm)
            consume(ctx, pg, "vector", bk, [
                lambda eng, m=m, bk=bk: eng.tensor_tensor(
                    s1f[:, m, :], ctx.ps[bk][:, :], sg1[:, m, :], ALU.mult)])
        dv1 = tinc(ctx, pg, "vector")
        w8(pg, "tensor", tDV, dv1)
        for m in range(8):
            bk = nb()
            pe_group(ctx, pg, bk, [
                (lambda eng, kc=kc, m=m, bk=bk: nc.tensor.matmul(
                    ctx.ps[bk][:, :], ctx.W1nT[:, kc, 128 * m:128 * (m + 1)],
                    s1f[:, kc, :], start=(kc == 0), stop=(kc == 7)))
                for kc in range(8)])
            consume(ctx, pg, "vector", bk, [
                lambda eng, m=m, bk=bk: eng.tensor_copy(
                    h2f[:, m, :], ctx.ps[bk][:, :])])
        dv2 = tinc(ctx, pg, "vector")
        w8(pg, "tensor", tDV, dv2)
        w8(pg, "scalar", sE, 16 * i)
        for m in range(8):
            bk = nb()
            pe_group(ctx, pg, bk, [
                (lambda eng, kc=kc, m=m, bk=bk: nc.tensor.matmul(
                    ctx.ps[bk][:, :], ctx.WoutTb[:, kc, 128 * m:128 * (m + 1)],
                    h2f[:, kc, :], start=(kc == 0), stop=(kc == 7)))
                for kc in range(8)])
            consume(ctx, pg, "scalar", bk, [
                lambda eng, m=m, bk=bk: eng.copy(ost[:, m, :], ctx.ps[bk][:, :])])
        ac2 = tinc(ctx, pg, "scalar")
        tinc(ctx, pg, "tensor")
        w8(pg, "gpsimd", tAC, ac2)
        dma(pg, "gpsimd", fm_slice(dr["R"], TT * i, TT), ost[:, :, :], sE,
            accum_op=ALU.add)
    w8(pg, "gpsimd", sE, sE.n)


def ph_E1(ctx, pg):
    nc, dr, cst, sb = ctx.nc, ctx.dr, ctx.cst, ctx.sb
    TT, NT = 512, 16
    for k in range(10):
        cnv_pieces(ctx, pg, 0, k)
    Rt = [sb(f"e1_r{b}", (128, 8, TT)) for b in range(2)]
    x1b = sb("e1_x1b", (128, 8, TT), BF16)
    sqb = sb("e1_sqb", (128, 8, TT), BF16)
    xLs = sb("e1_xls", (128, 8, TT))
    muR = sb("e1_mu", (1, TT))
    muR2 = sb("e1_mu2", (1, TT))
    veR = sb("e1_ve", (1, TT))
    veE = sb("e1_vee", (1, TT))
    yR = sb("e1_y", (1, TT))
    tR = sb("e1_t", (1, TT))
    rowYb = sb("e1_yb", (1, TT), BF16)
    nmRb = sb("e1_nmb", (1, TT), BF16)
    sR = [ctx.dsemf(f"e1r{b}") for b in range(2)]
    sS1 = ctx.dsemf("e1s1")
    sSL = ctx.dsemf("e1sl")
    tPE, tDV, tAC = ctx.tile["tensor"], ctx.tile["vector"], ctx.tile["scalar"]
    pe_base = tPE.n

    dv_base = tDV.n
    ac_base = tAC.n

    def load(i):
        if i >= NT:
            return
        b = i % 2
        w8(pg, "sync", tDV, dv_base + max(0, i - 1) * 3)
        w8(pg, "sync", tAC, ac_base + max(0, i - 1) * 2)
        dma(pg, "sync", Rt[b][:, :, :], fm_slice(dr["R"], TT * i, TT), sR[b])

    load(0)
    for i in range(NT):
        b = i % 2
        load(i + 1)
        nld = 16 * (i // 2 + 1)
        w8(pg, "scalar", sR[b], nld)
        w8(pg, "scalar", tPE, pe_base + max(0, i))
        pg.em("scalar", lambda eng, b=b: eng.activation(
            x1b.ap().rearrange("p c s -> p (c s)"),
            Rt[b].ap().rearrange("p c s -> p (c s)"), AF.Copy))
        pg.em("scalar", lambda eng, b=b: eng.activation(
            sqb.ap().rearrange("p c s -> p (c s)"),
            Rt[b].ap().rearrange("p c s -> p (c s)"), AF.Square))
        ac1 = tinc(ctx, pg, "scalar")
        dma(pg, "scalar", fm_slice(dr["R1"], TT * i, TT), Rt[b][:, :, :], sS1)
        w8(pg, "tensor", tAC, ac1)
        pe_group(ctx, pg, 0, [
            (lambda eng, c=c: nc.tensor.matmul(
                ctx.ps[0][0:1, :], cst.onesColB[:, :], x1b[:, c, :],
                start=(c == 0), stop=(c == 7))) for c in range(8)])
        pe_group(ctx, pg, 1, [
            (lambda eng, c=c: nc.tensor.matmul(
                ctx.ps[1][0:1, :], cst.onesColB[:, :], sqb[:, c, :],
                start=(c == 0), stop=(c == 7))) for c in range(8)])

        def rows0(eng):
            eng.tensor_scalar(muR[:, :], ctx.ps[0][0:1, :], 1.0 / D, None, ALU.mult)
            return eng.tensor_scalar(muR2[:, :], ctx.ps[0][0:1, :], 1.0 / D,
                                     None, ALU.mult)

        def rows1(eng):
            eng.tensor_scalar(veR[:, :], ctx.ps[1][0:1, :], 1.0 / D, None, ALU.mult)
            eng.tensor_mul(tR[:, :], muR[:, :], muR2[:, :])
            eng.tensor_sub(veR[:, :], veR[:, :], tR[:, :])
            return eng.tensor_scalar(veE[:, :], veR[:, :], LN_EPS, None, ALU.add)
        consume(ctx, pg, "vector", 0, [rows0])
        consume(ctx, pg, "vector", 1, [rows1])
        dv1 = tinc(ctx, pg, "vector")
        w8(pg, "scalar", tDV, dv1)
        pg.em("scalar", lambda eng: eng.activation(
            yR[:, :], veR[:, :], AF.Ln, bias=cst.epsCol[0:1, :]))
        pg.em("scalar", lambda eng: eng.activation(
            yR[:, :], yR[:, :], AF.Exp, scale=-0.5))
        ac2 = tinc(ctx, pg, "scalar")
        w8(pg, "vector", tAC, ac2)
        rsqrt_nr(pg, yR[:, :], veE[:, :], tR[:, :])

        def rows2(eng):
            eng.tensor_mul(tR[:, :], muR[:, :], yR[:, :])
            eng.tensor_scalar(tR[:, :], tR[:, :], -1.0, None, ALU.mult)
            eng.tensor_copy(rowYb[:, :], yR[:, :])
            return eng.tensor_copy(nmRb[:, :], tR[:, :])
        pg.em("vector", rows2)
        dv2 = tinc(ctx, pg, "vector")
        w8(pg, "tensor", tDV, dv2)
        pe_group(ctx, pg, 2, [lambda eng: nc.tensor.matmul(
            ctx.ps[2][:, :], cst.onesRowB[:, :], rowYb[:, :],
            start=True, stop=True)])
        pe_group(ctx, pg, 3, [lambda eng: nc.tensor.matmul(
            ctx.ps[3][:, :], cst.onesRowB[:, :], nmRb[:, :],
            start=True, stop=True)])
        w8(pg, "vector", sSL, 16 * i)
        fns = []
        for c in range(8):
            fns.append(lambda eng, c=c, b=b: eng.tensor_tensor(
                xLs[:, c, :], Rt[b][:, c, :], ctx.ps[2][:, :], ALU.mult))
        consume(ctx, pg, "vector", 2, fns)
        fns2 = []
        for c in range(8):
            fns2.append(lambda eng, c=c: eng.tensor_tensor(
                xLs[:, c, :], xLs[:, c, :], ctx.ps[3][:, :], ALU.add))
        consume(ctx, pg, "vector", 3, fns2)
        dv3 = tinc(ctx, pg, "vector")
        tinc(ctx, pg, "tensor")
        w8(pg, "sync", tDV, dv3)
        w8(pg, "sync", sS1, sS1.n)
        dma(pg, "sync", fm_slice(dr["R"], TT * i, TT), xLs[:, :, :], sSL)
    w8(pg, "sync", sSL, sSL.n)
    w8(pg, "scalar", sS1, sS1.n)


def ph_ATTN1(ctx, pg, layer=0):
    nc, dr, cst, sb = ctx.nc, ctx.dr, ctx.cst, ctx.sb
    TT, NT = 512, 16
    Wq3 = sb("t1_wq3", (128, 8, 3 * D), BF16)
    Mqb = sb("t1_mq", (16, D), BF16)
    Mkb = sb("t1_mk", (16, D), BF16)
    xsb = [sb(f"t1_xs{b}", (128, 8, TT + 1), BF16) for b in range(2)]
    tabs = [[sb(f"t1_tb{b}_{j}", (128, TT), BF16) for j in range(4)]
            for b in range(2)]
    qraw = sb("t1_qr", (128, 8, TT), BF16)
    kraw = sb("t1_kr", (128, 8, TT), BF16)
    sq2 = sb("t1_sq2", (128, 8, TT), BF16)
    ssm = sb("t1_ssm", (16, 2 * TT))
    yss = sb("t1_yss", (16, 2 * TT))
    tss = sb("t1_tss", (16, 2 * TT))
    rq8 = sb("t1_rq8", (16, TT), BF16)
    rk1 = sb("t1_rk1", (16, TT), BF16)
    tq = sb("t1_tq", (128, 8, TT), BF16)
    tk = sb("t1_tk", (128, 8, TT), BF16)
    tqp = sb("t1_tqp", (128, 8, TT), BF16)
    tkp = sb("t1_tkp", (128, 8, TT), BF16)
    avs = sb("t1_avs", (128, 4, HEADS, DH + 1), BF16)
    sW = ctx.dsemf("t1W")
    sX = [ctx.dsemf(f"t1x{b}") for b in range(2)]
    sTb = [ctx.dsemf(f"t1t{b}") for b in range(2)]
    sRot = ctx.dsemf("t1rot")
    sSt = ctx.dsemf("t1st")
    sX0 = [s.n for s in sX]
    sTb0 = [s.n for s in sTb]
    sSt0 = sSt.n
    tPE, tDV, tAC = ctx.tile["tensor"], ctx.tile["vector"], ctx.tile["scalar"]
    nb = bank_cycler()
    sWq = ctx.dsemf("t1Wq")
    w8(pg, "sync", ctx.dsemf("cnv"), ctx.dsemf("cnv").n)
    dma(pg, "sync", Wq3[:, :, :],
        dr["Wqkvb"].ap().rearrange("(c p) n -> p c n", p=128), sWq)
    dma(pg, "sync", ctx.Wo_l[:, :, :],
        dr["Wob"].ap().rearrange("(c p) n -> p c n", p=128), sWq)
    dma(pg, "gpsimd", Mqb[:, :], dr[f"Mq{layer}"].ap(), sW)
    dma(pg, "gpsimd", Mkb[:, :], dr[f"Mk{layer}"].ap(), sW)
    w8(pg, "tensor", sW, sW.n)
    w8(pg, "tensor", sWq, sWq.n)
    dvends = []
    pe_mark = {}

    def load(i):
        if i >= NT:
            return
        b = i % 2
        if i >= 2:
            w8(pg, "gpsimd", *pe_mark[i - 2])
            w8(pg, "gpsimd", tDV, dvends[i - 2])
        s0 = TT * i
        if i % 8 == 0:
            dma(pg, "gpsimd", xsb[b][:, :, 1:TT + 1],
                fm_slice(dr["R"], s0, TT), sX[b])
        else:
            dma(pg, "gpsimd", xsb[b][:, :, :],
                fm_slice(dr["R"], s0 - 1, TT + 1), sX[b])
        c0 = s0 % SEQ
        for j, tn in enumerate(("rotCQ", "rotSQ", "rotCK", "rotSK")):
            dma(pg, "gpsimd", tabs[b][j][:, :],
                dr[tn].ap()[:, c0:c0 + TT], sTb[b])

    pg.em("vector", lambda eng: eng.memset(avs[:, :, :, DH:DH + 1], 1.0))
    load(0)
    for i in range(NT):
        b = i % 2
        load(i + 1)
        if i % 8 == 0:
            pg.em("vector", lambda eng, b=b: eng.memset(xsb[b][:, :, 0:1], 0.0))
            dvm = tincL(ctx, pg, "vector")
            w8(pg, "tensor", tDV, dvm)
        w8(pg, "tensor", sX[b], sX0[b] + 16 * (i // 2 + 1))
        for m in range(16):
            bk = nb()
            pe_group(ctx, pg, bk, [
                (lambda eng, kc=kc, m=m, bk=bk, b=b: nc.tensor.matmul(
                    ctx.ps[bk][:, :],
                    Wq3[:, kc, 128 * m:128 * (m + 1)],
                    xsb[b][:, kc, 1:TT + 1] if kc < 4 else xsb[b][:, kc, 0:TT],
                    start=(kc == 0), stop=(kc == 7)))
                for kc in range(8)])
            tgt, mm = (qraw, m) if m < 8 else (kraw, m - 8)
            e = "vector" if m % 2 == 0 else "scalar"
            fn = ((lambda eng, tgt=tgt, mm=mm, bk=bk: eng.tensor_copy(
                tgt[:, mm, :], ctx.ps[bk][:, :])) if e == "vector" else
                (lambda eng, tgt=tgt, mm=mm, bk=bk: eng.copy(
                    tgt[:, mm, :], ctx.ps[bk][:, :])))
            consume(ctx, pg, e, bk, [fn])
            if e == "vector":
                qkv_dv = (ctx.evS[bk], ctx.evS[bk].n)
        w8(pg, "scalar", *qkv_dv)
        pg.em("scalar", lambda eng: eng.activation(
            sq2.ap().rearrange("p c s -> p (c s)"),
            qraw.ap().rearrange("p c s -> p (c s)"), AF.Square))
        ac1 = tinc(ctx, pg, "scalar")
        w8(pg, "tensor", tAC, ac1)
        bq = nb()
        pe_group(ctx, pg, bq, [
            (lambda eng, c=c, bq=bq: nc.tensor.matmul(
                ctx.ps[bq][0:16, :], cst.HselB[:, c, :], sq2[:, c, :],
                start=(c == 0), stop=(c == 7))) for c in range(8)])
        w8(pg, "scalar", ctx.mmS[bq], ctx.fills[bq])
        pg.em("scalar", lambda eng: eng.activation(
            sq2.ap().rearrange("p c s -> p (c s)"),
            kraw.ap().rearrange("p c s -> p (c s)"), AF.Square))
        ac2 = tinc(ctx, pg, "scalar")
        w8(pg, "tensor", tAC, ac2)
        bK = nb()
        pe_group(ctx, pg, bK, [
            (lambda eng, c=c, bK=bK: nc.tensor.matmul(
                ctx.ps[bK][0:16, :], cst.HselB[:, c, :], sq2[:, c, :],
                start=(c == 0), stop=(c == 7))) for c in range(8)])
        consume(ctx, pg, "vector", bq, [
            lambda eng, bq=bq: eng.tensor_scalar(
                ssm[:, 0:TT], ctx.ps[bq][0:16, :], 1e-24, None, ALU.max)])
        consume(ctx, pg, "vector", bK, [
            lambda eng, bK=bK: eng.tensor_scalar(
                ssm[:, TT:], ctx.ps[bK][0:16, :], 1e-24, None, ALU.max)])
        ssm_done = (ctx.evS[bK], ctx.evS[bK].n)
        w8(pg, "scalar", *ssm_done)
        pg.em("scalar", lambda eng: eng.activation(yss[:, :], ssm[:, :], AF.Ln))
        pg.em("scalar", lambda eng: eng.activation(
            yss[:, :], yss[:, :], AF.Exp, scale=-0.5))
        ac3 = tinc(ctx, pg, "scalar")
        w8(pg, "vector", tAC, ac3)
        rsqrt_nr(pg, yss[:, :], ssm[:, :], tss[:, :])

        def facs(eng):
            eng.tensor_scalar(rq8[:, :], yss[:, 0:TT], 8.0, None, ALU.mult)
            return eng.tensor_copy(rk1[:, :], yss[:, TT:])
        pg.em("vector", facs)
        dv3 = tincL(ctx, pg, "vector")
        w8(pg, "vector", sSt, sSt0 + 48 * i)
        w8(pg, "tensor", tDV, dv3)
        for (Mb, rb, raw, tdst) in ((Mqb, rq8, qraw, tq), (Mkb, rk1, kraw, tk)):
            for c in range(8):
                bk = nb()
                pe_group(ctx, pg, bk, [
                    lambda eng, Mb=Mb, rb=rb, c=c, bk=bk: nc.tensor.matmul(
                        ctx.ps[bk][:, :], Mb[:, 128 * c:128 * (c + 1)], rb[:, :],
                        start=True, stop=True)])
                consume(ctx, pg, "vector", bk, [
                    lambda eng, raw=raw, tdst=tdst, c=c, bk=bk: eng.tensor_tensor(
                        tdst[:, c, :], raw[:, c, :], ctx.ps[bk][:, :], ALU.mult)])
                bc_done = (ctx.evS[bk], ctx.evS[bk].n)
        w8(pg, "gpsimd", *bc_done)
        for (src_t, dst_t) in ((tq, tqp), (tk, tkp)):
            for (d0, s0_) in ((0, 32), (32, 0), (64, 96), (96, 64)):
                dma(pg, "gpsimd", dst_t[d0:d0 + 32, :, :],
                    src_t[s0_:s0_ + 32, :, :], sRot)
        w8(pg, "vector", sRot, sRot.n)
        w8(pg, "vector", sTb[b], sTb0[b] + 64 * (i // 2 + 1))

        def rot(eng, b=b):
            CQ = tabs[b][0].ap().unsqueeze(1).broadcast_to([128, 8, TT])
            SQ = tabs[b][1].ap().unsqueeze(1).broadcast_to([128, 8, TT])
            CK = tabs[b][2].ap().unsqueeze(1).broadcast_to([128, 8, TT])
            SK = tabs[b][3].ap().unsqueeze(1).broadcast_to([128, 8, TT])
            eng.tensor_tensor(tq[:, :, :], tq[:, :, :], CQ, ALU.mult)
            eng.tensor_tensor(tqp[:, :, :], tqp[:, :, :], SQ, ALU.mult)
            eng.tensor_tensor(tq[:, :, :], tq[:, :, :], tqp[:, :, :], ALU.add)
            eng.tensor_tensor(tk[:, :, :], tk[:, :, :], CK, ALU.mult)
            eng.tensor_tensor(tkp[:, :, :], tkp[:, :, :], SK, ALU.mult)
            return eng.tensor_tensor(tk[:, :, :], tk[:, :, :], tkp[:, :, :],
                                     ALU.add)
        pg.em("vector", rot)
        dv5 = tincL(ctx, pg, "vector")
        for u in range(4):
            for fh in range(2):
                bk = nb()
                pe_group(ctx, pg, bk, [
                    (lambda eng, u=u, fh=fh, kc=kc, bk=bk, b=b:
                     nc.tensor.matmul(
                        ctx.ps[bk][:, :],
                        xsb[b][:, kc, 1 + 128 * u:1 + 128 * (u + 1)]
                        if kc < 4 else xsb[b][:, kc, 128 * u:128 * (u + 1)],
                        Wq3[:, kc, 2048 + 512 * fh:2048 + 512 * (fh + 1)],
                        start=(kc == 0), stop=(kc == 7)))
                    for kc in range(8)])
                consume(ctx, pg, "vector", bk, [
                    lambda eng, u=u, fh=fh, bk=bk: eng.tensor_copy(
                        avs[:, u, 8 * fh:8 * (fh + 1), 0:DH],
                        ctx.ps[bk].ap().rearrange("p (h e) -> p h e", h=8))])
                avs_done = (ctx.evS[bk], ctx.evS[bk].n)
                pe_mark[i] = (ctx.mmS[bk], ctx.mmS[bk].n)
        w8(pg, "sync", *avs_done)
        w8(pg, "sync", tDV, dv5)
        s0 = TT * i
        dma(pg, "sync", fm_slice(dr["aqnS"], s0, TT), tq[:, :, :], sSt)
        dma(pg, "sync", fm_slice(dr["aknS"], s0, TT), tk[:, :, :], sSt)
        dma(pg, "sync", dr["avtS"].ap()[4 * i:4 * (i + 1), :, :, :].transpose(
            [1, 0, 2, 3]), avs[:, :, :, :], sSt)
        dvends.append(tDV.n)
    w8(pg, "sync", sSt, sSt.n)
    w8(pg, "gpsimd", sRot, sRot.n)
    tinc(ctx, pg, "vector")
    tinc(ctx, pg, "scalar")
    tinc(ctx, pg, "tensor")


def ph_ATTN2(ctx, pg, layer=0):
    nc, dr, cst, sb = ctx.nc, ctx.dr, ctx.cst, ctx.sb
    aqn = [sb(f"t2_q{b}", (128, 8, WIN), BF16) for b in range(2)]
    akn = [sb(f"t2_k{b}", (128, 8, 2 * WIN), BF16) for b in range(2)]
    avt = [sb(f"t2_v{b}", (128, 4, HEADS, DH + 1), BF16) for b in range(2)]
    Pb2 = [sb(f"t2_p{b}", (128, 2, 512), BF16) for b in range(2)]
    AOr = sb("t2_aor", (128, 8, WIN), BF16)
    AOn = sb("t2_aon", (128, 8, WIN), BF16)
    LmatR = sb("t2_l", (1, HEADS * WIN))
    rcpR = sb("t2_rcf", (1, HEADS * WIN))
    rcpRb = sb("t2_rcb", (1, HEADS * WIN), BF16)
    rcp16 = sb("t2_rcp", (HEADS, WIN), BF16)
    ost = sb("t2_ost", (128, 8, WIN))
    sQ = [ctx.dsemf(f"t2q{b}") for b in range(2)]
    sK = [ctx.dsemf(f"t2k{b}") for b in range(2)]
    sV = [ctx.dsemf(f"t2v{b}") for b in range(2)]
    sO = ctx.dsemf("t2o")
    sLr = ctx.dsemf("t2lr")
    sW1 = ctx.dsemf("f1W")
    sQ0 = [s.n for s in sQ]
    sK0 = [s.n for s in sK]
    sV0 = [s.n for s in sV]
    sO0 = sO.n
    tPE, tDV, tAC = ctx.tile["tensor"], ctx.tile["vector"], ctx.tile["scalar"]
    nb = bank_cycler()
    NWT = 2 * NWIN
    pe_mark = {}

    def load(i):
        if i >= NWT:
            return
        b = i % 2
        seq, w = divmod(i, NWIN)
        s0 = seq * SEQ + w * WIN
        if i >= 2:
            w8(pg, "sync", *pe_mark[i - 2])
        dma(pg, "sync", aqn[b][:, :, :], fm_slice(dr["aqnS"], s0, WIN), sQ[b])
        if w == 0:
            dma(pg, "sync", akn[b][:, :, 0:WIN],
                fm_slice(dr["aknS"], s0, WIN), sK[b])
            dma(pg, "sync", avt[b][:, 0:2, :, :],
                dr["avtS"].ap()[2 * i:2 * i + 2].transpose([1, 0, 2, 3]), sV[b])
        else:
            dma(pg, "sync", akn[b][:, :, :],
                fm_slice(dr["aknS"], s0 - WIN, 2 * WIN), sK[b])
            dma(pg, "sync", avt[b][:, :, :, :],
                dr["avtS"].ap()[2 * i - 2:2 * i + 2].transpose([1, 0, 2, 3]),
                sV[b])

    load(0)
    for i in range(NWT):
        b = i % 2
        seq, w = divmod(i, NWIN)
        w0 = (w == 0)
        load(i + 1)
        nq = 16 * (i // 2 + 1)
        w8(pg, "tensor", sQ[b], sQ0[b] + nq)
        w8(pg, "tensor", sK[b], sK0[b] + nq)
        w8(pg, "tensor", sV[b], sV0[b] + nq)
        scs = (0, 1) if w0 else (0, 1, 2, 3)
        gs = (0,) if w0 else (0, 1)
        w8(pg, "vector", sLr, sLr.n)
        pmin = {}
        for h in range(HEADS + 1):
            if h < HEADS:
                hp = h % 2
                pb = 64 * (h % 2)
                hc = h // 2
                exp_mark = None
                for g in gs:
                    bk = nb()
                    pe_group(ctx, pg, bk, [
                        (lambda eng, sc=2 * g + d, d=d, bk=bk, hc=hc, pb=pb,
                                b=b: nc.tensor.matmul(
                            ctx.ps[bk][:, 256 * d:256 * (d + 1)],
                            akn[b][pb:pb + 64, hc, 128 * sc:128 * (sc + 1)],
                            aqn[b][pb:pb + 64, hc, :],
                            start=(d == 0), stop=(d == 1)))
                        for d in range(2)])
                    consume(ctx, pg, "scalar", bk, [
                        lambda eng, g=g, bk=bk, hp=hp: eng.activation(
                            Pb2[hp][:, g, :], ctx.ps[bk][:, :], AF.Exp)])
                    exp_mark = (ctx.evS[bk], ctx.evS[bk].n)
                w8(pg, "vector", *exp_mark)
                for g in gs:
                    mask = cst.maskTBb if (w0 or g == 1) else cst.maskTAb
                    pg.em("vector", lambda eng, g=g, hp=hp, mask=mask:
                          eng.tensor_tensor(Pb2[hp][:, g, :], Pb2[hp][:, g, :],
                                            mask[:, :], ALU.min))
                pmin[h] = tincL(ctx, pg, "vector")
            if h >= 1:
                hq = h - 1
                hqp = hq % 2
                pbq = 64 * (hq % 2)
                hcq = hq // 2
                w8(pg, "tensor", tDV, pmin[hq])
                bkv = nb()
                pe_group(ctx, pg, bkv, [
                    (lambda eng, sc=sc, z=z, bkv=bkv, b=b, hq=hq, hqp=hqp:
                     nc.tensor.matmul(
                        ctx.ps[bkv][0:DH + 1, 0:WIN],
                        avt[b][:, sc, hq, :],
                        Pb2[hqp][:, sc // 2, 256 * (sc % 2):256 * (sc % 2) + 256],
                        start=(z == 0), stop=(z == len(scs) - 1)))
                    for z, sc in enumerate(scs)])
                consume(ctx, pg, "scalar", bkv, [
                    (lambda eng, pbq=pbq, hcq=hcq, bkv=bkv: eng.copy(
                        AOr[pbq:pbq + 64, hcq, :], ctx.ps[bkv][0:DH, 0:WIN])),
                    (lambda eng, hq=hq, bkv=bkv: eng.copy(
                        LmatR[0:1, WIN * hq:WIN * (hq + 1)],
                        ctx.ps[bkv][DH:DH + 1, 0:WIN])),
                ])
                w8(pg, "vector", ctx.evS[bkv], ctx.evS[bkv].n)
                pg.em("vector", lambda eng, hq=hq: eng.reciprocal(
                    rcpR[0:1, WIN * hq:WIN * (hq + 1)],
                    LmatR[0:1, WIN * hq:WIN * (hq + 1)]))
                pg.em("vector", lambda eng, hq=hq: eng.tensor_copy(
                    rcpRb[0:1, WIN * hq:WIN * (hq + 1)],
                    rcpR[0:1, WIN * hq:WIN * (hq + 1)]))
                rcv = tincL(ctx, pg, "vector")
                w8(pg, "gpsimd", tDV, rcv)
                dma(pg, "gpsimd", rcp16[hq:hq + 1, :],
                    rcpRb[0:1, WIN * hq:WIN * (hq + 1)], sLr)
        w8(pg, "tensor", sLr, sLr.n)
        aon_mark = None
        for c in range(8):
            bk = nb()
            pe_group(ctx, pg, bk, [
                lambda eng, c=c, bk=bk: nc.tensor.matmul(
                    ctx.ps[bk][:, 0:WIN], cst.HselT[:, c, :], rcp16[:, :],
                    start=True, stop=True)])
            consume(ctx, pg, "vector", bk, [
                lambda eng, c=c, bk=bk: eng.tensor_tensor(
                    AOn[:, c, :], AOr[:, c, :], ctx.ps[bk][:, 0:WIN],
                    ALU.mult)])
            aon_mark = (ctx.evS[bk], ctx.evS[bk].n)
        w8(pg, "tensor", *aon_mark)
        w8(pg, "vector", sO, sO0 + 16 * i)
        ost_done = None
        for m in range(8):
            bk = nb()
            pe_group(ctx, pg, bk, [
                (lambda eng, kc=kc, m=m, bk=bk: nc.tensor.matmul(
                    ctx.ps[bk][:, 0:WIN], ctx.Wo_l[:, kc, 128 * m:128 * (m + 1)],
                    AOn[:, kc, :], start=(kc == 0), stop=(kc == 7)))
                for kc in range(8)])
            consume(ctx, pg, "vector", bk, [
                lambda eng, m=m, bk=bk: eng.tensor_copy(ost[:, m, :],
                                                        ctx.ps[bk][:, 0:WIN])])
            ost_done = (ctx.evS[bk], ctx.evS[bk].n)
            if m == 7:
                pe_mark[i] = (ctx.mmS[bk], ctx.mmS[bk].n)
        w8(pg, "gpsimd", *ost_done)
        s0 = seq * SEQ + w * WIN
        dma(pg, "gpsimd", fm_slice(dr["R"], s0, WIN), ost[:, :, :], sO,
            accum_op=ALU.add)
        if i < 12:
            Q = D // 4
            if i < 4:
                dma(pg, "gpsimd", dr["W1ab"].ap()[Q * i:Q * (i + 1), :],
                    dr[f"W1a{layer}"].ap()[Q * i:Q * (i + 1), :], sW1)
            elif i < 8:
                k = i - 4
                dma(pg, "gpsimd", dr["W1gb"].ap()[Q * k:Q * (k + 1), :],
                    dr[f"W1g{layer}"].ap()[Q * k:Q * (k + 1), :], sW1)
            else:
                k = i - 8
                Q2 = DI // 4
                dma(pg, "gpsimd", dr["W2pb"].ap()[Q2 * k:Q2 * (k + 1), :],
                    dr[f"W2p{layer}"].ap()[Q2 * k:Q2 * (k + 1), :], sW1)
    w8(pg, "gpsimd", sO, sO.n)
    tinc(ctx, pg, "vector")
    tinc(ctx, pg, "scalar")
    tinc(ctx, pg, "tensor")


def ph_FFN0(ctx, pg, layer=0):
    nc, dr, cst, sb = ctx.nc, ctx.dr, ctx.cst, ctx.sb
    TT, NT = 512, 16
    sWL = ctx.dsemf("f1WL")
    w8(pg, "sync", ctx.dsemf("f1W"), ctx.dsemf("f1W").n)
    dma(pg, "sync", ctx.W1a_l[:, :, :],
        dr["W1ab"].ap().rearrange("(c p) n -> p c n", p=128), sWL)
    dma(pg, "sync", ctx.W1g_l[:, :, :],
        dr["W1gb"].ap().rearrange("(c p) n -> p c n", p=128), sWL)
    dma(pg, "sync", ctx.W2p_l[:, :, :],
        dr["W2pb"].ap().rearrange("(j p) n -> p j n", p=128), sWL)

    xsb = [sb(f"f0_xs{b}", (128, 8, TT + 1), BF16) for b in range(2)]
    sqb = sb("f0_sq", (128, 8, TT + 1), BF16)
    muR = sb("f0_mu", (1, TT))
    muR2 = sb("f0_mu2", (1, TT))
    veR = sb("f0_ve", (1, TT))
    veE = sb("f0_vee", (1, TT))
    yR = sb("f0_y", (1, TT))
    tR = sb("f0_t", (1, TT))
    arL = sb("f0_ar", (1, TT), BF16)
    amL = sb("f0_am", (1, TT), BF16)
    sX = [ctx.dsemf(f"f0x{b}") for b in range(2)]
    sRw = ctx.dsemf("f0rw")
    sX0 = [s.n for s in sX]
    sRw0 = sRw.n
    tPE, tDV, tAC = ctx.tile["tensor"], ctx.tile["vector"], ctx.tile["scalar"]
    pe_base = tPE.n

    def load(i):
        if i >= NT:
            return
        b = i % 2
        w8(pg, "gpsimd", tPE, pe_base + max(0, i - 1))
        s0 = TT * i
        if i % 8 == 0:
            dma(pg, "gpsimd", xsb[b][:, :, 1:TT + 1],
                fm_slice(dr["R"], s0, TT), sX[b])
        else:
            dma(pg, "gpsimd", xsb[b][:, :, :],
                fm_slice(dr["R"], s0 - 1, TT + 1), sX[b])

    load(0)
    for i in range(NT):
        b = i % 2
        load(i + 1)
        if i % 8 == 0:
            pg.em("vector", lambda eng, b=b: eng.memset(xsb[b][:, :, 0:1], 0.0))
            dvm = tinc(ctx, pg, "vector")
            w8(pg, "scalar", tDV, dvm)
        w8(pg, "scalar", sX[b], sX0[b] + 16 * (i // 2 + 1))
        w8(pg, "scalar", tPE, pe_base + max(0, i))
        pg.em("scalar", lambda eng, b=b: eng.activation(
            sqb.ap().rearrange("p c s -> p (c s)"),
            xsb[b].ap().rearrange("p c s -> p (c s)"), AF.Square))
        ac1 = tinc(ctx, pg, "scalar")
        w8(pg, "tensor", tAC, ac1)
        pe_group(ctx, pg, 0, [
            (lambda eng, c=c, b=b: nc.tensor.matmul(
                ctx.ps[0][0:1, :], cst.onesColB[:, :],
                xsb[b][:, c, 1:TT + 1] if c < 4 else xsb[b][:, c, 0:TT],
                start=(c == 0), stop=(c == 7))) for c in range(8)])
        pe_group(ctx, pg, 1, [
            (lambda eng, c=c: nc.tensor.matmul(
                ctx.ps[1][0:1, :], cst.onesColB[:, :],
                sqb[:, c, 1:TT + 1] if c < 4 else sqb[:, c, 0:TT],
                start=(c == 0), stop=(c == 7))) for c in range(8)])

        def rows0(eng):
            eng.tensor_scalar(muR[:, :], ctx.ps[0][0:1, :], 1.0 / D, None,
                              ALU.mult)
            return eng.tensor_scalar(muR2[:, :], ctx.ps[0][0:1, :], 1.0 / D,
                                     None, ALU.mult)

        def rows1(eng):
            eng.tensor_scalar(veR[:, :], ctx.ps[1][0:1, :], 1.0 / D, None,
                              ALU.mult)
            eng.tensor_mul(tR[:, :], muR[:, :], muR2[:, :])
            eng.tensor_sub(veR[:, :], veR[:, :], tR[:, :])
            return eng.tensor_scalar(veE[:, :], veR[:, :], LN_EPS, None, ALU.add)
        consume(ctx, pg, "vector", 0, [rows0])
        consume(ctx, pg, "vector", 1, [rows1])
        dv1 = tinc(ctx, pg, "vector")
        w8(pg, "scalar", tDV, dv1)
        pg.em("scalar", lambda eng: eng.activation(
            yR[:, :], veR[:, :], AF.Ln, bias=cst.epsCol[0:1, :]))
        pg.em("scalar", lambda eng: eng.activation(
            yR[:, :], yR[:, :], AF.Exp, scale=-0.5))
        ac2 = tinc(ctx, pg, "scalar")
        w8(pg, "vector", tAC, ac2)
        rsqrt_nr(pg, yR[:, :], veE[:, :], tR[:, :])
        w8(pg, "vector", sRw, sRw0 + 32 * i)

        def rows2(eng):
            eng.tensor_mul(tR[:, :], muR[:, :], yR[:, :])
            eng.tensor_scalar(amL[:, :], tR[:, :], -1.0, None, ALU.mult)
            return eng.tensor_copy(arL[:, :], yR[:, :])
        pg.em("vector", rows2)
        dv2 = tinc(ctx, pg, "vector")
        tinc(ctx, pg, "tensor")
        w8(pg, "sync", tDV, dv2)
        s0 = TT * i
        dma(pg, "sync", dr["rows"].ap()[0:1, s0:s0 + TT], arL[:, :], sRw)
        dma(pg, "sync", dr["rows"].ap()[1:2, s0:s0 + TT], amL[:, :], sRw)
    w8(pg, "sync", sRw, sRw.n)


def ph_FFN1(ctx, pg, layer=0):
    nc, dr, cst, sb = ctx.nc, ctx.dr, ctx.cst, ctx.sb
    TT, NT = 512, 16
    W1a, W1g, W2p = ctx.W1a_l, ctx.W1g_l, ctx.W2p_l
    b1a = sb("f1_b1a", (128, NJ))
    b1g = sb("f1_b1g", (128, NJ))
    b2c = sb("f1_b2", (128, 8))
    lng = sb("f1_g", (128, 8))
    lnb = sb("f1_b", (128, 8))
    rwr = [sb(f"f1_rr{b}", (1, TT), BF16) for b in range(2)]
    rwm = [sb(f"f1_rm{b}", (1, TT), BF16) for b in range(2)]
    xsb = sb("f1_xs", (128, 8, TT + 1), BF16)
    xn = sb("f1_xn", (128, 8, TT), BF16)
    gelG = sb("f1_gg", (128, TT), BF16)
    u = sb("f1_u", (128, NJ, TT), BF16)
    ost = sb("f1_ost", (128, 8, TT // 2))
    sWs = ctx.dsemf("f1Ws")
    sX = ctx.dsemf("f1x")
    sRw = [ctx.dsemf(f"f1rw{b}") for b in range(2)]
    sO = ctx.dsemf("f1o")
    sX0 = sX.n
    sRw0 = [s.n for s in sRw]
    sO0 = sO.n
    tPE, tDV, tAC = ctx.tile["tensor"], ctx.tile["vector"], ctx.tile["scalar"]
    nb = bank_cycler()
    dma(pg, "sync", b1a[:, :], dr[f"b1a{layer}"].ap(), sWs)
    dma(pg, "sync", b1g[:, :], dr[f"b1g{layer}"].ap(), sWs)
    dma(pg, "sync", b2c[:, :], dr[f"b2c{layer}"].ap(), sWs)
    dma(pg, "sync", lng[:, :], dr[f"lng{layer}"].ap(), sWs)
    dma(pg, "sync", lnb[:, :], dr[f"lnb{layer}"].ap(), sWs)
    w8(pg, "tensor", ctx.dsemf("f1WL"), ctx.dsemf("f1WL").n)
    w8(pg, "tensor", sWs, sWs.n)
    pe_base = tPE.n
    dv2s = []

    def loadx(i):
        if i >= NT:
            return
        s0 = TT * i
        if i >= 1:
            w8(pg, "gpsimd", tDV, dv2s[i - 1])
        if i % 8 == 0:
            dma(pg, "gpsimd", xsb[:, :, 1:TT + 1],
                fm_slice(dr["R"], s0, TT), sX)
        else:
            dma(pg, "gpsimd", xsb[:, :, :],
                fm_slice(dr["R"], s0 - 1, TT + 1), sX)

    bc_mark = {}

    def loadrows(i):
        if i >= NT:
            return
        b = i % 2
        s0 = TT * i
        if i >= 2:
            w8(pg, "sync", *bc_mark[i - 2])
        dma(pg, "sync", rwr[b][:, :], dr["rows"].ap()[0:1, s0:s0 + TT], sRw[b])
        dma(pg, "sync", rwm[b][:, :], dr["rows"].ap()[1:2, s0:s0 + TT], sRw[b])

    loadx(0)
    loadrows(0)
    for i in range(NT):
        b = i % 2
        s0 = TT * i
        loadrows(i + 1)
        if i % 8 == 0:
            pg.em("vector", lambda eng: eng.memset(xsb[:, :, 0:1], 0.0))
        w8(pg, "tensor", sRw[b], sRw0[b] + 32 * (i // 2 + 1))
        bR, bM = nb(), nb()
        pe_group(ctx, pg, bR, [lambda eng, bR=bR, b=b: nc.tensor.matmul(
            ctx.ps[bR][:, :], cst.onesRowB[:, :], rwr[b][:, :],
            start=True, stop=True)])
        pe_group(ctx, pg, bM, [lambda eng, bM=bM, b=b: nc.tensor.matmul(
            ctx.ps[bM][:, :], cst.onesRowB[:, :], rwm[b][:, :],
            start=True, stop=True)])
        bc_mark[i] = (ctx.mmS[bM], ctx.mmS[bM].n)
        w8(pg, "vector", sX, sX0 + 16 * (i + 1))
        fnsR = []
        for c in range(8):
            off = 1 if c < 4 else 0
            fnsR.append(lambda eng, c=c, off=off, bR=bR: eng.tensor_tensor(
                xn[:, c, :], xsb[:, c, off:off + TT], ctx.ps[bR][:, :],
                ALU.mult))
        consume(ctx, pg, "vector", bR, fnsR)
        fnsM = []
        for c in range(8):
            fnsM.append(lambda eng, c=c, bM=bM: eng.tensor_tensor(
                xn[:, c, :], xn[:, c, :], ctx.ps[bM][:, :], ALU.add))
        consume(ctx, pg, "vector", bM, fnsM)
        for c in range(8):
            pg.em("vector", lambda eng, c=c: eng.tensor_scalar(
                xn[:, c, :], xn[:, c, :], lng[:, c:c + 1], lnb[:, c:c + 1],
                ALU.mult, ALU.add))
        dv2 = tincL(ctx, pg, "vector")
        dv2s.append(dv2)
        loadx(i + 1)
        w8(pg, "tensor", tDV, dv2)
        gel_free = None
        for j in range(NJ):
            bA, bG = nb(), nb()
            pe_group(ctx, pg, bA, [
                (lambda eng, kc=kc, j=j, bA=bA: nc.tensor.matmul(
                    ctx.ps[bA][:, :], W1a[:, kc, 128 * j:128 * (j + 1)],
                    xn[:, kc, :], start=(kc == 0), stop=(kc == 7)))
                for kc in range(8)])
            pe_group(ctx, pg, bG, [
                (lambda eng, kc=kc, j=j, bG=bG: nc.tensor.matmul(
                    ctx.ps[bG][:, :], W1g[:, kc, 128 * j:128 * (j + 1)],
                    xn[:, kc, :], start=(kc == 0), stop=(kc == 7)))
                for kc in range(8)])
            if gel_free is not None:
                w8(pg, "scalar", gel_free[0], gel_free[1])
            consume(ctx, pg, "scalar", bG, [
                lambda eng, j=j, bG=bG: eng.activation(
                    gelG[:, :], ctx.ps[bG][:, :], AF.Gelu,
                    bias=b1g[:, j:j + 1])])
            w8(pg, "vector", ctx.evS[bG], ctx.evS[bG].n)
            consume(ctx, pg, "vector", bA, [
                lambda eng, j=j, bA=bA: eng.scalar_tensor_tensor(
                    u[:, j, :], ctx.ps[bA][:, :], b1a[:, j:j + 1], gelG[:, :],
                    ALU.add, ALU.mult)])
            gel_free = (ctx.evS[bA], ctx.evS[bA].n)
        u_done = gel_free
        w8(pg, "tensor", *u_done)
        bks = [nb() for _ in range(8)]
        for m in range(8):
            pe_group(ctx, pg, bks[m], [
                (lambda eng, j=j, m=m, bk=bks[m]: nc.tensor.matmul(
                    ctx.ps[bk][:, :], W2p[:, j, 128 * m:128 * (m + 1)],
                    u[:, j, :], start=(j == 0), stop=(j == NJ - 1)))
                for j in range(NJ)])
        w8(pg, "vector", sO, sO0 + 32 * i)
        for m in range(8):
            w8(pg, "vector", ctx.mmS[bks[m]], ctx.fills[bks[m]])
            pg.em("vector", lambda eng, m=m, bk=bks[m]: eng.tensor_scalar(
                ost[:, m, :], ctx.ps[bk][:, 0:TT // 2], b2c[:, m:m + 1],
                None, ALU.add))
        dvA = tincL(ctx, pg, "vector")
        w8(pg, "gpsimd", tDV, dvA)
        if i + 1 < NT:
            w8(pg, "gpsimd", sX, sX0 + 16 * (i + 2))
        dma(pg, "gpsimd", fm_slice(dr["R"], s0, TT // 2), ost[:, :, :], sO,
            accum_op=ALU.add)
        w8(pg, "vector", sO, sO0 + 32 * i + 16)
        ostB_done = None
        for m in range(8):
            consume(ctx, pg, "vector", bks[m], [
                lambda eng, m=m, bk=bks[m]: eng.tensor_scalar(
                    ost[:, m, :], ctx.ps[bk][:, TT // 2:TT], b2c[:, m:m + 1],
                    None, ALU.add)])
            ostB_done = (ctx.evS[bks[m]], ctx.evS[bks[m]].n)
        w8(pg, "gpsimd", *ostB_done)
        dma(pg, "gpsimd", fm_slice(dr["R"], s0 + TT // 2, TT // 2),
            ost[:, :, :], sO, accum_op=ALU.add)
        if layer + 1 < NL and i < 10:
            cnv_pieces(ctx, pg, layer + 1, i)
    w8(pg, "gpsimd", sO, sO.n)
    tinc(ctx, pg, "vector")
    tinc(ctx, pg, "scalar")
    tinc(ctx, pg, "tensor")


def cnv_pieces(ctx, pg, layer, k):
    dr = ctx.dr
    sC = ctx.dsemf("cnv")
    if k < 8:
        Q = 3 * D // 8
        dma(pg, "gpsimd", dr["Wqkvb"].ap()[:, Q * k:Q * (k + 1)],
            dr[f"Wqkv{layer}"].ap()[:, Q * k:Q * (k + 1)], sC)
    else:
        H2 = D // 2
        j = k - 8
        dma(pg, "gpsimd", dr["Wob"].ap()[H2 * j:H2 * (j + 1), :],
            dr[f"Wo{layer}"].ap()[H2 * j:H2 * (j + 1), :], sC)


def ph_FIN(ctx, pg, y=None):
    nc, dr, cst, sb = ctx.nc, ctx.dr, ctx.cst, ctx.sb
    TT, NT = 512, 16
    rt = [sb(f"fn_r{b}", (128, 8, TT)) for b in range(2)]
    r1t = [sb(f"fn_r1{b}", (128, 8, TT)) for b in range(2)]
    sm = sb("fn_sm", (128, 8, TT))
    yt = sb("fn_yt", (128, 4, D))
    sR = [ctx.dsemf(f"fnr{b}") for b in range(2)]
    sY = ctx.dsemf("fny")
    tPE, tDV, tAC = ctx.tile["tensor"], ctx.tile["vector"], ctx.tile["scalar"]
    pe_base, dv_base = tPE.n, tDV.n
    nb = bank_cycler()

    def load(i):
        if i >= NT:
            return
        b = i % 2
        w8(pg, "sync", tDV, dv_base + max(0, i - 1))
        dma(pg, "sync", rt[b][:, :, :], fm_slice(dr["R"], TT * i, TT), sR[b])
        dma(pg, "sync", r1t[b][:, :, :], fm_slice(dr["R1"], TT * i, TT), sR[b])

    load(0)
    for i in range(NT):
        b = i % 2
        load(i + 1)
        w8(pg, "vector", sR[b], 32 * (i // 2 + 1))
        w8(pg, "vector", tPE, pe_base + max(0, i))
        pg.em("vector", lambda eng, b=b: eng.tensor_tensor(
            sm.ap().rearrange("p c s -> p (c s)"),
            rt[b].ap().rearrange("p c s -> p (c s)"),
            r1t[b].ap().rearrange("p c s -> p (c s)"), ALU.add))
        dv1 = tinc(ctx, pg, "vector")
        w8(pg, "tensor", tDV, dv1)
        w8(pg, "scalar", sY, 16 * i)
        for a in range(4):
            for half in range(2):
                bk = nb()
                pe_group(ctx, pg, bk, [
                    (lambda eng, a=a, c=4 * half + cc, cc=cc, bk=bk:
                     nc.tensor.matmul(
                        ctx.ps[bk][:, 128 * cc:128 * (cc + 1)],
                        sm[:, c, 128 * a:128 * (a + 1)],
                        cst.identF[:, :], start=(cc == 0), stop=(cc == 3)))
                    for cc in range(4)])
                consume(ctx, pg, "scalar", bk, [
                    lambda eng, a=a, half=half, bk=bk: eng.copy(
                        yt[:, a, 512 * half:512 * (half + 1)],
                        ctx.ps[bk][:, :])])
        tinc(ctx, pg, "tensor")
        ac1 = tinc(ctx, pg, "scalar")
        w8(pg, "sync", tAC, ac1)
        dma(pg, "sync",
            y.ap()[TT * i:TT * (i + 1), :].rearrange("(a p) e -> p a e", p=128),
            yt[:, :, :], sY)
    w8(pg, "sync", sY, sY.n)


def kernel(**inputs):
    shared, shards = _host_prep(inputs)
    nc = build_program()
    in_maps = []
    for c in range(NCORES):
        m = dict(shared)
        m["x0"] = shards[c]
        in_maps.append(m)
    res = run_bass_kernel_spmd(nc, in_maps, core_ids=list(range(NCORES)))
    out = np.concatenate([np.asarray(res.results[c]["y"]).reshape(2, SEQ, D)
                          for c in range(NCORES)], axis=0)
    return out.reshape(16, SEQ, D)


# version 07
# speedup vs baseline: 1.2945x; 1.0019x over previous
import contextlib

import numpy as np

import concourse.bass as bass
import concourse.mybir as mybir
from concourse.bass_utils import run_bass_kernel_spmd

F32 = mybir.dt.float32
BF16 = mybir.dt.bfloat16
AF = mybir.ActivationFunctionType
ALU = mybir.AluOpType
AX = mybir.AxisListType

NCORES = 8
D = 1024
S = 8192
SEQ = 4096
NL = 6
HEADS = 16
DH = 64
WIN = 256
NWIN = SEQ // WIN
DI = 2816
NJ = DI // 128
ERR_CLIP = 5.0
GRAD_CLIP = 1.0
TTT_SCALE = 2.0 / 4096.0
LN_EPS = 1e-5
LN8 = float(np.log(8.0))
NEGM = -1.0e30
ENGINES = ("sync", "gpsimd", "vector", "scalar", "tensor")



def _host_consts():
    c = {}
    c["identF"] = np.eye(128, dtype=np.float32)
    c["onesCol"] = np.ones((128, 1), dtype=np.float32)
    c["epsCol"] = np.full((128, 1), LN_EPS, dtype=np.float32)
    c["ln8Col"] = np.full((128, 1), LN8, dtype=np.float32)
    c["onesRow"] = np.ones((1, 128), dtype=np.float32)
    hs = np.zeros((128, 8, HEADS), dtype=np.float32)
    for ch in range(8):
        hs[:64, ch, 2 * ch] = 1.0
        hs[64:, ch, 2 * ch + 1] = 1.0
    c["Hsel"] = hs
    p = np.arange(128)[:, None]
    j = np.arange(384)[None, :]
    c["maskW"] = np.where((j >= p) & (j <= p + 256), 0.0, NEGM).astype(np.float32)
    j2 = np.arange(256)[None, :]
    c["mask0a"] = np.where(j2 <= p, 0.0, NEGM).astype(np.float32)
    c["mask0b"] = np.where(j2 <= p + 128, 0.0, NEGM).astype(np.float32)
    BIG = 3.0e38
    qq = np.arange(256)[None, :]
    va = [qq <= p, (qq >= p - 128) & (qq <= p + 128), qq >= p, qq >= p + 128]
    c["maskTA"] = np.concatenate(
        [np.where(va[0], BIG, 0.0), np.where(va[1], BIG, 0.0)],
        axis=1).astype(np.float32)
    c["maskTB"] = np.concatenate(
        [np.where(va[2], BIG, 0.0), np.where(va[3], BIG, 0.0)],
        axis=1).astype(np.float32)
    hst = np.zeros((HEADS, 8, 128), dtype=np.float32)
    for ch in range(8):
        hst[2 * ch, ch, :64] = 1.0
        hst[2 * ch + 1, ch, 64:] = 1.0
    c["HselT"] = hst
    half = np.arange(0, DH, 2, dtype=np.float64)
    inv_freq = 1.0 / 10000.0 ** (half / DH)
    t = np.arange(SEQ, dtype=np.float64)
    freqs = np.concatenate([t[:, None] * inv_freq[None, :]] * 2, axis=1)
    base_scale = (half + 0.4 * DH) / (1.4 * DH)
    power = (t - SEQ // 2) / (WIN // 2)
    scale = np.concatenate([base_scale[None, :] ** power[:, None]] * 2, axis=1)
    cos, sin = np.cos(freqs), np.sin(freqs)
    sgn = np.ones(DH)
    sgn[: DH // 2] = -1.0
    for nm, tb in (("CQ", cos * scale), ("SQ", sin * scale * sgn[None, :]),
                   ("CK", cos / scale), ("SK", sin / scale * sgn[None, :])):
        tbT = tb.T
        c["rot" + nm] = np.ascontiguousarray(
            np.concatenate([tbT, tbT], axis=0).astype(np.float32))
    return c


def _chunk_cols(v, n):
    return np.ascontiguousarray(v.reshape(n, 128).T.astype(np.float32))


def _host_prep(inp):
    f = lambda a: np.ascontiguousarray(np.asarray(a, dtype=np.float32))
    tr = lambda a: np.ascontiguousarray(np.asarray(a, dtype=np.float32).T)
    sh = dict(_host_consts())
    sh["WkT"] = tr(inp["L0_attn_Wk"])
    sh["WvT"] = tr(inp["L0_attn_Wv"])
    sh["WqT"] = tr(inp["L0_attn_Wq"])
    sh["W0T"] = tr(inp["L0_attn_W0"])
    sh["W1T"] = tr(inp["L0_attn_W1"])
    sh["W1N"] = f(inp["L0_attn_W1"])
    sh["WoutT"] = tr(inp["L0_attn_Wout"])
    sh["Wg3"] = np.ascontiguousarray(np.concatenate(
        [f(inp["L0_attn_Wg_decay"]), f(inp["L0_attn_Wg_lr"]),
         f(inp["L0_attn_Wg_mom"])], axis=0).T)
    sh["bg3"] = np.ascontiguousarray(np.stack(
        [f(inp["L0_attn_bg_decay"]).ravel(), f(inp["L0_attn_bg_lr"]).ravel(),
         f(inp["L0_attn_bg_mom"]).ravel()]).reshape(3, 1))
    for i in range(NL):
        sh[f"Wqkv{i}"] = f(inp["L0_ffn_Wqkv"][i])
        sh[f"Wo{i}"] = f(inp["L0_ffn_Wo"][i])
        w1 = f(inp["L0_ffn_W1"][i])
        sh[f"W1a{i}"] = np.ascontiguousarray(np.pad(w1[:, :2730], ((0, 0), (0, 86))))
        sh[f"W1g{i}"] = np.ascontiguousarray(np.pad(w1[:, 2730:], ((0, 0), (0, 86))))
        sh[f"W2p{i}"] = np.ascontiguousarray(np.pad(f(inp["L0_ffn_W2"][i]),
                                                    ((0, 86), (0, 0))))
        b1 = f(inp["L0_ffn_b1"][i])
        sh[f"b1a{i}"] = _chunk_cols(np.pad(b1[:2730], (0, 86)), NJ)
        sh[f"b1g{i}"] = _chunk_cols(np.pad(b1[2730:], (0, 86)), NJ)
        sh[f"b2c{i}"] = _chunk_cols(f(inp["L0_ffn_b2"][i]), 8)
        sh[f"lng{i}"] = _chunk_cols(f(inp["L0_ffn_ln_g"][i]), 8)
        sh[f"lnb{i}"] = _chunk_cols(f(inp["L0_ffn_ln_b"][i]), 8)
        mq = np.zeros((HEADS, D), dtype=np.float32)
        mk = np.zeros((HEADS, D), dtype=np.float32)
        for h in range(HEADS):
            mq[h, DH * h: DH * h + DH] = f(inp["L0_ffn_q_scale"][i])
            mk[h, DH * h: DH * h + DH] = f(inp["L0_ffn_k_scale"][i])
        sh[f"Mq{i}"] = mq
        sh[f"Mk{i}"] = mk
    x = np.asarray(inp["L0_attn_x"], dtype=np.float32)
    shards = [np.ascontiguousarray(x[2 * c: 2 * c + 2].reshape(S, D))
              for c in range(NCORES)]
    return sh, shards



class Sem:
    __slots__ = ("h", "n")

    def __init__(self, h):
        self.h = h
        self.n = 0


class Prog:
    def __init__(self):
        self.ops = {e: [] for e in ENGINES}
        self.anchor = {e: None for e in ENGINES}
        self.kind = {e: None for e in ENGINES}

    def em(self, e, fn):
        self.ops[e].append(fn)
        self.kind[e] = "plain"


class Ctx:
    pass


def w8(pg, e, sem, val):
    if val > 0:
        def fn(eng, s=sem.h, v=val):
            eng.wait_ge(s, v)
        pg.em(e, fn)
        pg.anchor[e] = ("wait",)
        pg.kind[e] = "wait"


def dma(pg, q, out, in_, sem, **kw):
    def fn(eng, o=out, i=in_, s=sem.h, kw=kw):
        eng.dma_start(out=o, in_=i, **kw).then_inc(s, 16)
    pg.em(q, fn)
    sem.n += 16
    pg.anchor[q] = (sem.h, sem.n)
    pg.kind[q] = "inc"
    return sem.n


def tinc(ctx, pg, e, key=None):
    sem = ctx.tile[key or e]

    def fn(eng, s=sem.h):
        return eng.drain().then_inc(s, 1)
    pg.em(e, fn)
    pg.anchor[e] = (sem.h, sem.n + 1)
    pg.kind[e] = "inc"
    sem.n += 1
    return sem.n


def tincL(ctx, pg, e, key=None):
    sem = ctx.tile[key or e]
    assert pg.kind[e] == "plain", pg.kind[e]
    f = pg.ops[e][-1]

    def wrap(eng, f=f, s=sem.h):
        return f(eng).then_inc(s, 1)
    pg.ops[e][-1] = wrap
    sem.n += 1
    pg.anchor[e] = (sem.h, sem.n)
    pg.kind[e] = "inc"
    return sem.n


def pe_group(ctx, pg, b, fns):
    w8(pg, "tensor", ctx.evS[b], ctx.frees[b])
    for f in fns[:-1]:
        pg.em("tensor", f)
    pg.em("tensor", lambda eng, f=fns[-1], s=ctx.mmS[b].h: f(eng).then_inc(s, 1))
    ctx.mmS[b].n += 1
    ctx.fills[b] += 1
    pg.anchor["tensor"] = (ctx.mmS[b].h, ctx.mmS[b].n)
    pg.kind["tensor"] = "inc"


def consume(ctx, pg, e, b, fns):
    w8(pg, e, ctx.mmS[b], ctx.fills[b])
    for f in fns[:-1]:
        pg.em(e, f)
    pg.em(e, lambda eng, f=fns[-1], s=ctx.evS[b].h: f(eng).then_inc(s, 1))
    ctx.evS[b].n += 1
    ctx.frees[b] += 1
    pg.anchor[e] = (ctx.evS[b].h, ctx.evS[b].n)
    pg.kind[e] = "inc"


def run_phase(ctx, pg):
    p = ctx.phase_idx
    ctx.phase_idx += 1
    blk = ctx.blk
    for name in ENGINES:
        sec = getattr(blk, name)

        anchor = pg.anchor[name]

        @sec
        def _(eng, name=name, p=p, ops=pg.ops[name], anchor=anchor):
            if p > 0:
                for s in ctx.ph.values():
                    eng.wait_ge(s.h, p)
            for fn in ops:
                fn(eng)
            if anchor is not None and anchor[0] != "wait":
                eng.wait_ge(anchor[0], anchor[1])
            eng.sem_inc(ctx.ph[name].h, 1)
    for name in ENGINES:
        ctx.ph[name].n += 1


def spacer(pg, ctx, e="vector"):
    pg.em(e, lambda eng: eng.tensor_scalar(
        ctx.junkV[:, 0:256], ctx.junkV[:, 0:256], 1.0, None, ALU.mult))


def rsqrt_nr(pg, y, ve, tmp, n=2):
    for _ in range(n):
        pg.em("vector", lambda eng, y=y, t=tmp: eng.tensor_mul(t, y, y))
        pg.em("vector", lambda eng, v=ve, t=tmp: eng.tensor_mul(t, t, v))
        pg.em("vector", lambda eng, t=tmp: eng.tensor_scalar(
            t, t, -0.5, 1.5, ALU.mult, ALU.add))
        pg.em("vector", lambda eng, y=y, t=tmp: eng.tensor_mul(y, y, t))


def fm_slice(t, s0, n):
    return t.ap()[:, :, s0:s0 + n].transpose([1, 0, 2])


def wchunks(dr_name, ctx):
    return ctx.dr[dr_name].ap().rearrange("(c p) n -> p c n", p=128)


def bank_cycler(start=0, n=8):
    i = [start]

    def nb():
        b = i[0] % n
        i[0] += 1
        return b
    return nb



def ph_P0(ctx, pg):
    dr, cst = ctx.dr, ctx.cst
    sG = ctx.dsemf("p0g")
    sS = ctx.dsemf("p0s")
    dma(pg, "gpsimd", cst.identB[:, :], dr["identF"].ap(), sG)
    dma(pg, "gpsimd", cst.onesColB[:, :], dr["onesCol"].ap(), sG)
    dma(pg, "gpsimd", cst.onesRowB[:, :], dr["onesRow"].ap(), sG)
    dma(pg, "gpsimd", cst.HselB[:, :, :], dr["Hsel"].ap(), sG)
    dma(pg, "gpsimd", cst.maskWb[:, :], dr["maskW"].ap(), sG)
    dma(pg, "gpsimd", cst.mask0ab[:, :], dr["mask0a"].ap(), sG)
    dma(pg, "gpsimd", cst.mask0bb[:, :], dr["mask0b"].ap(), sG)
    dma(pg, "gpsimd", cst.maskTAb[:, :], dr["maskTA"].ap(), sG)
    dma(pg, "gpsimd", cst.maskTBb[:, :], dr["maskTB"].ap(), sG)
    dma(pg, "gpsimd", cst.HselT[:, :, :], dr["HselT"].ap(), sG)
    dma(pg, "sync", cst.identF[:, :], dr["identF"].ap(), sS)
    dma(pg, "sync", cst.bg3[:, :], dr["bg3"].ap(), sS)
    dma(pg, "sync", cst.epsCol[:, :], dr["epsCol"].ap(), sS)
    dma(pg, "sync", cst.ln8Col[:, :], dr["ln8Col"].ap(), sS)
    w8(pg, "gpsimd", sG, sG.n)
    w8(pg, "sync", sS, sS.n)


def ph_A1(ctx, pg):
    import os
    nc, dr, cst, sb = ctx.nc, ctx.dr, ctx.cst, ctx.sb
    TT, NT = 512, 16
    if os.environ.get("DBG_NT1"):
        NT = 1
    Wk = sb("a1_Wk", (128, 8, D), BF16)
    Wv = sb("a1_Wv", (128, 8, D), BF16)
    Wq = sb("a1_Wq", (128, 8, D), BF16)
    xt = [sb(f"a1_xt{b}", (128, 4, D)) for b in range(2)]
    xnt = [sb(f"a1_xnt{b}", (128, 4, D), BF16) for b in range(2)]
    xnf = [sb(f"a1_xnf{b}", (128, 8, TT), BF16) for b in range(2)]
    rstage = sb("a1_rst", (128, 8, TT))
    qf = sb("a1_qf", (128, 8, TT), BF16)
    kf = sb("a1_kf", (128, 8, TT), BF16)
    vf = sb("a1_vf", (128, 8, TT), BF16)
    stS = sb("a1_sts", (128, 64))
    stQ = sb("a1_stq", (128, 64))
    stM = sb("a1_stm", (128, 64))
    stV = sb("a1_stv", (128, 64))
    stE = sb("a1_ste", (128, 64))
    stY = sb("a1_sty", (128, 64))
    stT = sb("a1_stt", (128, 64))
    gsum = sb("a1_gsum", (128, 64))
    sW = ctx.dsemf("a1W")
    sX = [ctx.dsemf(f"a1x{b}") for b in range(2)]
    sRst = ctx.dsemf("a1rs")
    sQKV = ctx.dsemf("a1qkv")
    tPE, tDV, tAC = ctx.tile["tensor"], ctx.tile["vector"], ctx.tile["scalar"]

    dma(pg, "gpsimd", Wk[:, :, :], wchunks("WkT", ctx), sW)
    dma(pg, "gpsimd", Wv[:, :, :], wchunks("WvT", ctx), sW)
    dma(pg, "gpsimd", Wq[:, :, :], wchunks("WqT", ctx), sW)
    pg.em("vector", lambda eng: eng.memset(ctx.sxn[:, :], 0.0))
    x0 = dr["x0"].ap()
    junk = ctx.junk
    w8(pg, "tensor", sW, 48)
    nb = bank_cycler()
    pe_base, dv_base, ac_base = tPE.n, tDV.n, tAC.n

    def load(i):
        if i >= NT:
            return
        b = i % 2
        w8(pg, "sync", tDV, dv_base + max(0, i - 1) * 4)
        w8(pg, "sync", tPE, pe_base + max(0, i - 1))
        w8(pg, "sync", tAC, ac_base + max(0, i - 1) * 4)
        dma(pg, "sync", xt[b][:, :, :],
            x0[TT * i: TT * (i + 1), :].rearrange("(a p) e -> p a e", p=128),
            sX[b])

    load(0)
    for i in range(NT):
        b = i % 2
        load(i + 1)
        X, XN = xt[b], xnt[b]
        nld = 16 * (i // 2 + 1)
        w8(pg, "scalar", sX[b], nld)
        for a in range(4):
            pg.em("scalar", lambda eng, X=X, a=a: eng.activation(
                junk[:, :D], X[:, a, :], AF.Copy, accum_out=stS[:, a:a + 1]))
            pg.em("scalar", lambda eng, X=X, a=a: eng.activation(
                junk[:, :D], X[:, a, :], AF.Square, accum_out=stQ[:, a:a + 1]))
        ac1 = tinc(ctx, pg, "scalar")
        w8(pg, "vector", tAC, ac1)

        def dve_stats(eng):
            eng.tensor_scalar(stM[:, :], stS[:, :], 1.0 / D, None, ALU.mult)
            eng.tensor_scalar(stV[:, :], stQ[:, :], 1.0 / D, None, ALU.mult)
            eng.tensor_mul(stT[:, :], stM[:, :], stM[:, :])
            eng.tensor_sub(stV[:, :], stV[:, :], stT[:, :])
            return eng.tensor_scalar(stE[:, :], stV[:, :], LN_EPS, None, ALU.add)
        pg.em("vector", dve_stats)
        dv1 = tinc(ctx, pg, "vector")
        w8(pg, "scalar", tDV, dv1)
        pg.em("scalar", lambda eng: eng.activation(
            stY[:, :], stV[:, :], AF.Ln, bias=cst.epsCol[:, :]))
        pg.em("scalar", lambda eng: eng.activation(
            stY[:, :], stY[:, :], AF.Exp, scale=-0.5))
        ac2 = tinc(ctx, pg, "scalar")
        w8(pg, "vector", tAC, ac2)
        rsqrt_nr(pg, stY[:, :], stE[:, :], stT[:, :])
        w8(pg, "vector", tPE, pe_base + max(0, i - 1))

        def dve_norm(eng, X=X, XN=XN):
            ins = None
            for a in range(4):
                ins = eng.tensor_scalar(XN[:, a, :], X[:, a, :], stM[:, a:a + 1],
                                        stY[:, a:a + 1], ALU.subtract, ALU.mult)
            return ins
        pg.em("vector", dve_norm)
        w8(pg, "tensor", sX[b], nld)
        w8(pg, "scalar", sRst, 16 * i)
        for c in range(8):
            bk = nb()
            pe_group(ctx, pg, bk, [
                (lambda eng, X=X, a=a, c=c, bk=bk: nc.tensor.matmul(
                    ctx.ps[bk][:, 128 * a:128 * (a + 1)],
                    X[:, a, 128 * c:128 * (c + 1)], cst.identF[:, :],
                    start=(a == 0), stop=(a == 3))) for a in range(4)])
            consume(ctx, pg, "scalar", bk, [
                lambda eng, c=c, bk=bk: eng.copy(rstage[:, c, :], ctx.ps[bk][:, :])])
        acR = tinc(ctx, pg, "scalar")
        w8(pg, "gpsimd", tAC, acR)
        dma(pg, "gpsimd", fm_slice(ctx.dr["R"], TT * i, TT), rstage[:, :, :], sRst)
        dv2 = tinc(ctx, pg, "vector")
        w8(pg, "tensor", tDV, dv2)
        for c in range(8):
            bk = nb()
            pe_group(ctx, pg, bk, [
                (lambda eng, XN=XN, a=a, c=c, bk=bk: nc.tensor.matmul(
                    ctx.ps[bk][:, 128 * a:128 * (a + 1)],
                    XN[:, a, 128 * c:128 * (c + 1)], cst.identB[:, :],
                    start=(a == 0), stop=(a == 3))) for a in range(4)])
            consume(ctx, pg, "scalar", bk, [
                lambda eng, c=c, bk=bk, b=b: eng.copy(
                    xnf[b][:, c, :], ctx.ps[bk][:, :])])
        acX = tinc(ctx, pg, "scalar")
        w8(pg, "vector", tAC, acX)
        pg.em("vector", lambda eng, b=b: eng.tensor_reduce(
            gsum[:, 0:8], xnf[b][:, :, :], AX.X, ALU.add))
        spacer(pg, ctx)
        pg.em("vector", lambda eng: eng.tensor_add(
            ctx.sxn[:, :], ctx.sxn[:, :], gsum[:, 0:8]))
        dv3 = tinc(ctx, pg, "vector")
        w8(pg, "tensor", tAC, acX)
        w8(pg, "tensor", tDV, dv3)
        first_ev = {"vector": True, "scalar": True}
        for t_i, (Wt, dst) in enumerate([(Wk, kf), (Wv, vf), (Wq, qf)]):
            for m in range(8):
                bk = nb()
                pe_group(ctx, pg, bk, [
                    (lambda eng, Wt=Wt, kc=kc, m=m, bk=bk, b=b: nc.tensor.matmul(
                        ctx.ps[bk][:, :], Wt[:, kc, 128 * m:128 * (m + 1)],
                        xnf[b][:, kc, :], start=(kc == 0), stop=(kc == 7)))
                    for kc in range(8)])
                e = "vector" if (m + t_i) % 2 == 0 else "scalar"
                if first_ev[e]:
                    w8(pg, e, sQKV, 16 * 3 * i)
                    first_ev[e] = False
                if e == "vector":
                    fn = (lambda eng, dst=dst, m=m, bk=bk: eng.tensor_copy(
                        dst[:, m, :], ctx.ps[bk][:, :]))
                else:
                    fn = (lambda eng, dst=dst, m=m, bk=bk: eng.copy(
                        dst[:, m, :], ctx.ps[bk][:, :]))
                consume(ctx, pg, e, bk, fn if isinstance(fn, list) else [fn])
        pe1 = tinc(ctx, pg, "tensor")
        dv4 = tinc(ctx, pg, "vector")
        ac3 = tinc(ctx, pg, "scalar")
        w8(pg, "gpsimd", tDV, dv4)
        w8(pg, "gpsimd", tAC, ac3)
        w8(pg, "gpsimd", ctx.tile["tensor"], pe1)

        dma(pg, "gpsimd", fm_slice(dr["kS"], TT * i, TT), kf[:, :, :], sQKV)
        dma(pg, "gpsimd", fm_slice(dr["vS"], TT * i, TT), vf[:, :, :], sQKV)
        dma(pg, "gpsimd", fm_slice(dr["qnS"], TT * i, TT), qf[:, :, :], sQKV)

    w8(pg, "gpsimd", sQKV, sQKV.n)
    w8(pg, "gpsimd", sRst, sRst.n)
    if os.environ.get("DBG_NT1"):
        sdbg = ctx.dsemf("a1dbg2")
        sdbg2 = ctx.dsemf("a1dbg3")
        w8(pg, "sync", ctx.tile["tensor"], ctx.tile["tensor"].n)
        w8(pg, "gpsimd", ctx.tile["tensor"], ctx.tile["tensor"].n)
        dma(pg, "gpsimd", dr["dbgA"].ap()[0:128, 0:4096],
            xnt[0].ap().rearrange("p a e -> p (a e)"), sdbg2)
        dma(pg, "gpsimd", dr["dbgA"].ap()[128:256, 0:4096],
            xnf[0].ap().rearrange("p c s -> p (c s)"), sdbg2)
        dma(pg, "sync", dr["dbgA"].ap()[256:384, 0:24],
            stat.ap().rearrange("p a k -> p (a k)"), sdbg)
        w8(pg, "sync", sdbg, sdbg.n)
        w8(pg, "gpsimd", sdbg2, sdbg2.n)



def build_program(upto=None, taps=()):
    nc = bass.Bass("TRN2", target_bir_lowering=False, debug=False)
    ctx = Ctx()
    ctx.nc = nc
    ctx.phase_idx = 0
    dr = {}

    def din(name, shape, dtype=F32):
        dr[name] = nc.dram_tensor(name, list(shape), dtype, kind="ExternalInput")

    def dint(name, shape, dtype=F32):
        dr[name] = nc.dram_tensor(name, list(shape), dtype, kind="Internal")

    din("x0", (S, D))
    for nm in ("WkT", "WvT", "WqT", "W0T", "W1T", "W1N", "WoutT"):
        din(nm, (D, D))
    din("Wg3", (D, 3))
    din("bg3", (3, 1))
    din("identF", (128, 128))
    din("onesCol", (128, 1))
    din("epsCol", (128, 1))
    din("ln8Col", (128, 1))
    din("onesRow", (1, 128))
    din("Hsel", (128, 8, HEADS))
    din("maskW", (128, 384))
    din("mask0a", (128, 256))
    din("mask0b", (128, 256))
    din("maskTA", (128, 512))
    din("maskTB", (128, 512))
    din("HselT", (HEADS, 8, 128))
    for nm in ("rotCQ", "rotSQ", "rotCK", "rotSK"):
        din(nm, (128, SEQ))
    for i in range(NL):
        din(f"Wqkv{i}", (D, 3 * D))
        din(f"Wo{i}", (D, D))
        din(f"W1a{i}", (D, DI))
        din(f"W1g{i}", (D, DI))
        din(f"W2p{i}", (DI, D))
        din(f"b1a{i}", (128, NJ))
        din(f"b1g{i}", (128, NJ))
        din(f"b2c{i}", (128, 8))
        din(f"lng{i}", (128, 8))
        din(f"lnb{i}", (128, 8))
        din(f"Mq{i}", (HEADS, D))
        din(f"Mk{i}", (HEADS, D))
    y = nc.dram_tensor("y", [S, D], F32, kind="ExternalOutput")
    dint("R", (8, 128, S))
    dint("R1", (8, 128, S))
    dint("kS", (8, 128, S), BF16)
    dint("vS", (8, 128, S), BF16)
    dint("qnS", (8, 128, S), BF16)
    dint("aqnS", (8, 128, S), BF16)
    dint("aknS", (8, 128, S), BF16)
    dint("avtS", (64, 128, HEADS, DH + 1), BF16)
    dint("dbgA", (384, 4096))
    dint("rows", (2, S), BF16)
    dint("rrow", (HEADS * WIN,), BF16)
    dint("W1ab", (D, DI), BF16)
    dint("W1gb", (D, DI), BF16)
    dint("W2pb", (DI, D), BF16)
    dint("Wqkvb", (D, 3 * D), BF16)
    dint("Wob", (D, D), BF16)
    dint("ccI", (2051, D))
    dint("ccO", (2051, D))
    ctx.dr = dr

    tap_outs = {}
    for t in taps:
        h = dr[t]
        tap_outs[t] = nc.dram_tensor("tap_" + t, list(h.shape), h.dtype,
                                     kind="ExternalOutput")

    with contextlib.ExitStack() as stk:
        ctx.stk = stk
        nsem = [0]

        def new_sem(tag):
            nsem[0] += 1
            return Sem(stk.enter_context(nc.semaphore(f"s{nsem[0]}_{tag}")))

        ctx.new_sem = new_sem
        ctx.ps = [stk.enter_context(nc.psum_tensor(f"ps{i}", [128, 512], F32))
                  for i in range(8)]
        ctx.mmS = [new_sem(f"mm{i}") for i in range(8)]
        ctx.evS = [new_sem(f"ev{i}") for i in range(8)]
        ctx.fills = [0] * 8
        ctx.frees = [0] * 8
        ctx.ph = {e: new_sem("ph_" + e) for e in ENGINES}
        ctx.dsem = {}

        def dsemf(tag):
            if tag not in ctx.dsem:
                ctx.dsem[tag] = new_sem("d_" + tag)
            return ctx.dsem[tag]

        ctx.dsemf = dsemf
        ctx.tile = {e: new_sem("t_" + e) for e in
                    ("vector", "scalar", "tensor", "gpsimd", "sync")}
        ctx.ccS = new_sem("cc")
        ctx.scope = [stk]

        nsb = [0]

        def sb(name, shape, dtype=F32):
            nsb[0] += 1
            return ctx.scope[-1].enter_context(
                nc.sbuf_tensor(f"{name}_{nsb[0]}", list(shape), dtype))

        ctx.sb = sb

        with nc.Block() as blk:
            ctx.blk = blk
            cst = Ctx()
            cst.identF = stk.enter_context(nc.sbuf_tensor("c_idF", [128, 128], F32))
            cst.identB = stk.enter_context(nc.sbuf_tensor("c_idB", [128, 128], BF16))
            cst.onesColB = stk.enter_context(nc.sbuf_tensor("c_o1", [128, 1], BF16))
            cst.onesRowB = stk.enter_context(nc.sbuf_tensor("c_or", [1, 128], BF16))
            cst.HselB = stk.enter_context(nc.sbuf_tensor("c_hs", [128, 8, HEADS], BF16))
            cst.maskWb = stk.enter_context(nc.sbuf_tensor("c_mw", [128, 384], BF16))
            cst.mask0ab = stk.enter_context(nc.sbuf_tensor("c_m0a", [128, 256], BF16))
            cst.mask0bb = stk.enter_context(nc.sbuf_tensor("c_m0b", [128, 256], BF16))
            cst.maskTAb = stk.enter_context(nc.sbuf_tensor("c_mta", [128, 512], BF16))
            cst.maskTBb = stk.enter_context(nc.sbuf_tensor("c_mtb", [128, 512], BF16))
            cst.HselT = stk.enter_context(
                nc.sbuf_tensor("c_hst", [HEADS, 8, 128], BF16))
            cst.bg3 = stk.enter_context(nc.sbuf_tensor("c_bg3", [3, 1], F32))
            cst.epsCol = stk.enter_context(nc.sbuf_tensor("c_eps", [128, 1], F32))
            cst.ln8Col = stk.enter_context(nc.sbuf_tensor("c_ln8", [128, 1], F32))
            ctx.cst = cst
            ctx.junk = stk.enter_context(nc.sbuf_tensor("junk", [128, D], BF16))
            ctx.junkV = stk.enter_context(
                nc.sbuf_tensor("junkV", [128, 256], BF16))
            ctx.sxn = stk.enter_context(nc.sbuf_tensor("sxn", [128, 8], F32))

            done = [False]

            def phase(fn, **kw):
                if done[0]:
                    return
                with contextlib.ExitStack() as pscope:
                    ctx.scope.append(pscope)
                    pg = Prog()
                    fn(ctx, pg, **kw)
                    run_phase(ctx, pg)
                    ctx.scope.pop()
                name = fn.__name__ + str(kw.get("layer", ""))
                if upto is not None and name == upto:
                    done[0] = True

            phase(ph_P0)
            phase(ph_A1)
            phase(ph_A2)
            with contextlib.ExitStack() as wscope:
                ctx.W0nT = wscope.enter_context(
                    nc.sbuf_tensor("W0nT", [128, 8, D], BF16))
                ctx.W1nT = wscope.enter_context(
                    nc.sbuf_tensor("W1nT", [128, 8, D], BF16))
                ctx.WoutTb = wscope.enter_context(
                    nc.sbuf_tensor("WoutTb", [128, 8, D], BF16))
                phase(ph_AR)
                phase(ph_E)
            phase(ph_E1)
            for i in range(NL):
                with contextlib.ExitStack() as lscope:
                    ctx.Wo_l = lscope.enter_context(
                        nc.sbuf_tensor(f"Wo_l{i}", [128, 8, D], BF16))
                    phase(ph_ATTN1, layer=i)
                    phase(ph_ATTN2, layer=i)
                with contextlib.ExitStack() as fscope:
                    ctx.W1a_l = fscope.enter_context(
                        nc.sbuf_tensor(f"W1a_l{i}", [128, 8, DI], BF16))
                    ctx.W1g_l = fscope.enter_context(
                        nc.sbuf_tensor(f"W1g_l{i}", [128, 8, DI], BF16))
                    ctx.W2p_l = fscope.enter_context(
                        nc.sbuf_tensor(f"W2p_l{i}", [128, NJ, D], BF16))
                    phase(ph_FFN0, layer=i)
                    phase(ph_FFN1, layer=i)
            phase(ph_FIN, y=y)

            if upto is not None:
                pg = Prog()
                sst = dsemf("dump")
                dma(pg, "sync",
                    y.ap().rearrange("(c p a) e -> c p (a e)", c=8, p=128),
                    dr["R"].ap(), sst)
                for t, o in tap_outs.items():
                    dma(pg, "sync", o.ap(), dr[t].ap(), sst)
                w8(pg, "sync", sst, sst.n)
                run_phase(ctx, pg)
    return nc


def ph_A2(ctx, pg):
    nc, dr, cst, sb = ctx.nc, ctx.dr, ctx.cst, ctx.sb
    TT2, NT2 = 256, 32
    W0T = sb("a2_W0T", (128, 8, D), BF16)
    W1T = sb("a2_W1T", (128, 8, D), BF16)
    W1N = sb("a2_W1N", (128, 8, D), BF16)
    ctx.G1T = sb("a2_G1T", (128, 8, D))
    ctx.G0T = sb("a2_G0T", (128, 8, D))
    kf = [sb(f"a2_kf{b}", (128, 8, TT2), BF16) for b in range(2)]
    vf = [sb(f"a2_vf{b}", (128, 8, TT2), BF16) for b in range(2)]
    sg = sb("a2_sg", (128, 8, TT2), BF16)
    a1f = sb("a2_a1", (128, 8, TT2), BF16)
    sdf = sb("a2_sd", (128, 8, TT2), BF16)
    erf = sb("a2_er", (128, 8, TT2), BF16)
    d1f = sb("a2_d1", (128, 8, TT2), BF16)
    e1t = sb("a2_e1t", (128, TT2))
    t1t = sb("a2_t1t", (128, TT2), BF16)
    t2t = sb("a2_t2t", (128, TT2))
    kT = sb("a2_kT", (128, 2, 8, 128), BF16)
    a1T = sb("a2_a1T", (128, 2, 8, 128), BF16)
    erT = sb("a2_erT", (128, 2, 8, 128), BF16)
    d1T = sb("a2_d1T", (128, 2, 8, 128), BF16)
    sW = ctx.dsemf("a2W")
    sK = [ctx.dsemf(f"a2k{b}") for b in range(2)]
    sG = ctx.dsemf("a2G")
    tPE, tDV, tAC = ctx.tile["tensor"], ctx.tile["vector"], ctx.tile["scalar"]

    dma(pg, "gpsimd", W0T[:, :, :], wchunks("W0T", ctx), sW)
    dma(pg, "gpsimd", W1T[:, :, :], wchunks("W1T", ctx), sW)
    dma(pg, "gpsimd", W1N[:, :, :], wchunks("W1N", ctx), sW)
    pg.em("vector", lambda eng: eng.memset(ctx.G1T[:, :, :], 0.0))
    pg.em("vector", lambda eng: eng.memset(ctx.G0T[:, :, :], 0.0))
    w8(pg, "tensor", sW, sW.n)
    nb = bank_cycler()
    pe_base = tPE.n

    def load(i):
        if i >= NT2:
            return
        b = i % 2
        w8(pg, "sync", tPE, pe_base + max(0, i - 1))
        dma(pg, "sync", kf[b][:, :, :], fm_slice(dr["kS"], TT2 * i, TT2), sK[b])
        dma(pg, "sync", vf[b][:, :, :], fm_slice(dr["vS"], TT2 * i, TT2), sK[b])

    load(0)
    for i in range(NT2):
        b = i % 2
        load(i + 1)
        nld = 32 * (i // 2 + 1)
        w8(pg, "tensor", sK[b], nld)
        for m in range(8):
            bk = nb()
            pe_group(ctx, pg, bk, [
                (lambda eng, kc=kc, m=m, bk=bk, b=b: nc.tensor.matmul(
                    ctx.ps[bk][:, 0:TT2], W0T[:, kc, 128 * m:128 * (m + 1)],
                    kf[b][:, kc, :], start=(kc == 0), stop=(kc == 7)))
                for kc in range(8)])
            consume(ctx, pg, "scalar", bk, [
                lambda eng, m=m, bk=bk: eng.activation(
                    sg[:, m, :], ctx.ps[bk][:, 0:TT2], AF.Sigmoid)])
            acm = tinc(ctx, pg, "scalar")
            w8(pg, "vector", tAC, acm)

            def dve_sd(eng, m=m, bk=bk):
                ps = ctx.ps[bk][:, 0:TT2]
                eng.tensor_tensor(a1f[:, m, :], ps, sg[:, m, :], ALU.mult)
                eng.tensor_scalar(t1t[:, :], sg[:, m, :], -1.0, 1.0,
                                  ALU.mult, ALU.add)
                eng.tensor_tensor(t2t[:, :], ps, t1t[:, :], ALU.mult)
                return eng.scalar_tensor_tensor(
                    sdf[:, m, :], t2t[:, :], 1.0, sg[:, m, :], ALU.add, ALU.mult)
            consume(ctx, pg, "vector", bk, [dve_sd])
        dv_a = tinc(ctx, pg, "vector")
        w8(pg, "tensor", tDV, dv_a)
        for m in range(8):
            bk = nb()
            pe_group(ctx, pg, bk, [
                (lambda eng, kc=kc, m=m, bk=bk: nc.tensor.matmul(
                    ctx.ps[bk][:, 0:TT2], W1T[:, kc, 128 * m:128 * (m + 1)],
                    a1f[:, kc, :], start=(kc == 0), stop=(kc == 7)))
                for kc in range(8)])

            def dve_err(eng, m=m, bk=bk, b=b):
                eng.tensor_sub(e1t[:, :], ctx.ps[bk][:, 0:TT2], vf[b][:, m, :])
                return eng.tensor_scalar(erf[:, m, :], e1t[:, :], ERR_CLIP,
                                         -ERR_CLIP, ALU.min, ALU.max)
            consume(ctx, pg, "vector", bk, [dve_err])
        dv_b = tinc(ctx, pg, "vector")
        w8(pg, "tensor", tDV, dv_b)
        for m in range(8):
            bk = nb()
            pe_group(ctx, pg, bk, [
                (lambda eng, kc=kc, m=m, bk=bk: nc.tensor.matmul(
                    ctx.ps[bk][:, 0:TT2], W1N[:, kc, 128 * m:128 * (m + 1)],
                    erf[:, kc, :], start=(kc == 0), stop=(kc == 7)))
                for kc in range(8)])
            consume(ctx, pg, "vector", bk, [
                lambda eng, m=m, bk=bk: eng.tensor_tensor(
                    d1f[:, m, :], ctx.ps[bk][:, 0:TT2], sdf[:, m, :], ALU.mult)])
        dv_c = tinc(ctx, pg, "vector")
        w8(pg, "tensor", tDV, dv_c)
        for (srcT, dstT) in ((kf[b], kT), (a1f, a1T), (erf, erT), (d1f, d1T)):
            for c in range(8):
                bk = nb()
                pe_group(ctx, pg, bk, [
                    (lambda eng, srcT=srcT, c=c, u=u, bk=bk: nc.tensor.matmul(
                        ctx.ps[bk][:, 128 * u:128 * (u + 1)],
                        srcT[:, c, 128 * u:128 * (u + 1)], cst.identB[:, :],
                        start=(u == 0), stop=(u == 1))) for u in range(2)])
                consume(ctx, pg, "vector", bk, [
                    lambda eng, dstT=dstT, c=c, bk=bk: eng.tensor_copy(
                        dstT[:, :, c, :], ctx.ps[bk][:, 0:TT2])])
        dv_d = tinc(ctx, pg, "vector")
        w8(pg, "tensor", tDV, dv_d)
        for (lT, rT, acc) in ((a1T, erT, ctx.G1T), (kT, d1T, ctx.G0T)):
            for m in range(8):
                for h in range(2):
                    bk = nb()
                    pe_group(ctx, pg, bk, [
                        (lambda eng, lT=lT, rT=rT, m=m, h=h, u=u, bk=bk:
                         nc.tensor.matmul(
                             ctx.ps[bk][:, :],
                             lT[:, u, m, :],
                             rT[:, u, 4 * h:4 * (h + 1), :], start=(u == 0),
                             stop=(u == 1))) for u in range(2)])
                    consume(ctx, pg, "vector", bk, [
                        lambda eng, acc=acc, m=m, h=h, bk=bk: eng.tensor_tensor(
                            acc[:, m, 512 * h:512 * (h + 1)],
                            acc[:, m, 512 * h:512 * (h + 1)],
                            ctx.ps[bk][:, :], ALU.add)])
        tinc(ctx, pg, "tensor")
    dv_e = tinc(ctx, pg, "vector")
    w8(pg, "sync", tDV, dv_e)
    dma(pg, "sync", dr["ccI"].ap()[0:1024, :].rearrange("(c p) d -> p c d", p=128),
        ctx.G1T[:, :, :], sG)
    dma(pg, "sync", dr["ccI"].ap()[1024:2048, :].rearrange("(c p) d -> p c d", p=128),
        ctx.G0T[:, :, :], sG)
    w8(pg, "sync", sG, sG.n)


def ph_AR(ctx, pg):
    nc, dr, cst, sb = ctx.nc, ctx.dr, ctx.cst, ctx.sb
    Wg3b = sb("ar_wg", (128, 8, 3), BF16)
    sxnb = sb("ar_sxnb", (128, 8), BF16)
    g3s = sb("ar_g3", (3, 1))
    a3r = sb("ar_a3r", (1, 3))
    a3rb = sb("ar_a3rb", (1, 3), BF16)
    gbf = sb("ar_gb", (128, 3))
    alc = sb("ar_alc", (128, 1))
    lrs = sb("ar_lrs", (128, 1))
    Gt = [sb(f"ar_gt{b}", (128, 1024)) for b in range(2)]
    Wf = [sb(f"ar_wf{b}", (128, 1024)) for b in range(2)]
    t2 = sb("ar_t2", (128, 1024))
    sL = ctx.dsemf("arL")
    sO = ctx.dsemf("arO")
    sW = ctx.dsemf("arW")
    tPE, tDV, tAC = ctx.tile["tensor"], ctx.tile["vector"], ctx.tile["scalar"]

    dma(pg, "gpsimd", Wg3b[:, :, :], dr["Wg3"].ap().rearrange(
        "(c p) n -> p c n", p=128), sL)
    dma(pg, "gpsimd", ctx.WoutTb[:, :, :], wchunks("WoutT", ctx), sL)
    pg.em("vector", lambda eng: eng.tensor_copy(sxnb[:, :], ctx.sxn[:, :]))
    dv1 = tinc(ctx, pg, "vector")
    w8(pg, "tensor", sL, 32)
    w8(pg, "tensor", tDV, dv1)
    bk = 0
    pe_group(ctx, pg, bk, [
        (lambda eng, kc=kc: nc.tensor.matmul(
            ctx.ps[0][0:3, 0:1], Wg3b[:, kc, :], sxnb[:, kc:kc + 1],
            start=(kc == 0), stop=(kc == 7))) for kc in range(8)])
    consume(ctx, pg, "vector", bk, [
        lambda eng: eng.tensor_copy(g3s[:, :], ctx.ps[0][0:3, 0:1])])
    dv2 = tinc(ctx, pg, "vector")
    w8(pg, "sync", tDV, dv2)
    dma(pg, "sync", dr["ccI"].ap()[2048:2051, 0:1], g3s[:, :],
        ctx.dsemf("arG"), allow_slow_non_contiguous=True)
    w8(pg, "sync", ctx.dsemf("arG"), ctx.dsemf("arG").n)
    sy1 = tinc(ctx, pg, "sync")
    w8(pg, "gpsimd", ctx.tile["sync"], sy1)
    w8(pg, "gpsimd", ctx.dsemf("a2G"), ctx.dsemf("a2G").n)

    def cc(eng):
        return eng.collective_compute(
            "AllReduce", ALU.add, replica_groups=[list(range(NCORES))],
            ins=[dr["ccI"].ap()], outs=[dr["ccO"].ap()])
    pg.em("gpsimd", lambda eng: cc(eng).then_inc(ctx.ccS.h))
    ctx.ccS.n += 1
    pg.anchor["gpsimd"] = (ctx.ccS.h, 1)
    pg.kind["gpsimd"] = "inc"
    w8(pg, "sync", ctx.ccS, 1)
    dma(pg, "sync", g3s[:, :], dr["ccO"].ap()[2048:2051, 0:1],
        ctx.dsemf("arG2"), allow_slow_non_contiguous=True)
    w8(pg, "scalar", ctx.dsemf("arG2"), 16)
    pg.em("scalar", lambda eng: eng.activation(
        g3s[:, :], g3s[:, :], AF.Sigmoid, bias=cst.bg3[:, :],
        scale=1.0 / (16 * SEQ)))
    ac1 = tinc(ctx, pg, "scalar")
    w8(pg, "tensor", tAC, ac1)
    pe_group(ctx, pg, 1, [lambda eng: nc.tensor.matmul(
        ctx.ps[1][0:1, 0:3], g3s[:, :], cst.identF[0:3, 0:3],
        start=True, stop=True)])
    consume(ctx, pg, "vector", 1, [
        lambda eng: eng.tensor_copy(a3r[:, :], ctx.ps[1][0:1, 0:3])])
    spacer(pg, ctx)
    pg.em("vector", lambda eng: eng.tensor_copy(a3rb[:, :], a3r[:, :]))
    dv3 = tinc(ctx, pg, "vector")
    w8(pg, "tensor", tDV, dv3)
    pe_group(ctx, pg, 2, [lambda eng: nc.tensor.matmul(
        ctx.ps[2][:, 0:3], cst.onesRowB[:, :], a3rb[:, :],
        start=True, stop=True)])
    consume(ctx, pg, "vector", 2, [
        lambda eng: eng.tensor_copy(gbf[:, :], ctx.ps[2][:, 0:3])])
    spacer(pg, ctx)
    pg.em("vector", lambda eng: eng.tensor_scalar(
        alc[:, :], gbf[:, 0:1], -1.0, 1.0, ALU.mult, ALU.add))
    spacer(pg, ctx)
    pg.em("vector", lambda eng: eng.tensor_copy(lrs[:, :], gbf[:, 1:2]))
    spacer(pg, ctx)
    dv4 = tinc(ctx, pg, "vector")
    ccO = dr["ccO"].ap()
    for wi, (rows, WTname, dst) in enumerate(
            ((1024, "W0T", ctx.W0nT), (0, "W1T", ctx.W1nT))):
        for c in range(8):
            b = (wi * 8 + c) % 2
            sGt = ctx.dsemf(f"arGt{b}")
            sWf = ctx.dsemf(f"arWf{b}")
            if wi * 8 + c >= 2:
                w8(pg, "sync", tDV, dv4 + (wi * 8 + c) - 1)
            dma(pg, "sync", Gt[b][:, :],
                ccO[rows:rows + 1024, :].rearrange("(c p) d -> p c d", p=128)[:, c, :],
                sGt)
            dma(pg, "sync", Wf[b][:, :],
                dr[WTname].ap().rearrange("(c p) d -> p c d", p=128)[:, c, :],
                sWf)
            w8(pg, "vector", sGt, sGt.n)
            w8(pg, "vector", sWf, sWf.n)

            def build(eng, b=b, c=c, dst=dst):
                eng.tensor_scalar(Gt[b][:, :], Gt[b][:, :], TTT_SCALE,
                                  GRAD_CLIP, ALU.mult, ALU.min)
                eng.tensor_scalar(t2[:, :], Gt[b][:, :], -GRAD_CLIP,
                                  lrs[:, :], ALU.max, ALU.mult)
                return eng.scalar_tensor_tensor(
                    dst[:, c, :], Wf[b][:, :], alc[:, :], t2[:, :],
                    ALU.mult, ALU.subtract)
            pg.em("vector", build)
            tinc(ctx, pg, "vector")
    w8(pg, "gpsimd", sL, sL.n)


def ph_E(ctx, pg):
    nc, dr, cst, sb = ctx.nc, ctx.dr, ctx.cst, ctx.sb
    TT, NT = 512, 16
    qf = [sb(f"e_qf{b}", (128, 8, TT), BF16) for b in range(2)]
    sg1 = sb("e_sg", (128, 8, TT), BF16)
    s1f = sb("e_s1", (128, 8, TT), BF16)
    h2f = sb("e_h2", (128, 8, TT), BF16)
    ost = sb("e_ost", (128, 8, TT))
    sQ = [ctx.dsemf(f"eq{b}") for b in range(2)]
    sE = ctx.dsemf("eR")
    tPE, tDV, tAC = ctx.tile["tensor"], ctx.tile["vector"], ctx.tile["scalar"]
    nb = bank_cycler()
    pe_base = tPE.n

    def load(i):
        if i >= NT:
            return
        b = i % 2
        w8(pg, "sync", tPE, pe_base + max(0, i - 1))
        dma(pg, "sync", qf[b][:, :, :], fm_slice(dr["qnS"], TT * i, TT), sQ[b])

    load(0)
    for i in range(NT):
        b = i % 2
        load(i + 1)
        w8(pg, "tensor", sQ[b], 16 * (i // 2 + 1))
        for m in range(8):
            bk = nb()
            pe_group(ctx, pg, bk, [
                (lambda eng, kc=kc, m=m, bk=bk, b=b: nc.tensor.matmul(
                    ctx.ps[bk][:, :], ctx.W0nT[:, kc, 128 * m:128 * (m + 1)],
                    qf[b][:, kc, :], start=(kc == 0), stop=(kc == 7)))
                for kc in range(8)])
            consume(ctx, pg, "scalar", bk, [
                lambda eng, m=m, bk=bk: eng.activation(
                    sg1[:, m, :], ctx.ps[bk][:, :], AF.Sigmoid)])
            acm = tinc(ctx, pg, "scalar")
            w8(pg, "vector", tAC, acm)
            consume(ctx, pg, "vector", bk, [
                lambda eng, m=m, bk=bk: eng.tensor_tensor(
                    s1f[:, m, :], ctx.ps[bk][:, :], sg1[:, m, :], ALU.mult)])
        dv1 = tinc(ctx, pg, "vector")
        w8(pg, "tensor", tDV, dv1)
        for m in range(8):
            bk = nb()
            pe_group(ctx, pg, bk, [
                (lambda eng, kc=kc, m=m, bk=bk: nc.tensor.matmul(
                    ctx.ps[bk][:, :], ctx.W1nT[:, kc, 128 * m:128 * (m + 1)],
                    s1f[:, kc, :], start=(kc == 0), stop=(kc == 7)))
                for kc in range(8)])
            consume(ctx, pg, "vector", bk, [
                lambda eng, m=m, bk=bk: eng.tensor_copy(
                    h2f[:, m, :], ctx.ps[bk][:, :])])
        dv2 = tinc(ctx, pg, "vector")
        w8(pg, "tensor", tDV, dv2)
        w8(pg, "scalar", sE, 16 * i)
        for m in range(8):
            bk = nb()
            pe_group(ctx, pg, bk, [
                (lambda eng, kc=kc, m=m, bk=bk: nc.tensor.matmul(
                    ctx.ps[bk][:, :], ctx.WoutTb[:, kc, 128 * m:128 * (m + 1)],
                    h2f[:, kc, :], start=(kc == 0), stop=(kc == 7)))
                for kc in range(8)])
            consume(ctx, pg, "scalar", bk, [
                lambda eng, m=m, bk=bk: eng.copy(ost[:, m, :], ctx.ps[bk][:, :])])
        ac2 = tinc(ctx, pg, "scalar")
        tinc(ctx, pg, "tensor")
        w8(pg, "gpsimd", tAC, ac2)
        dma(pg, "gpsimd", fm_slice(dr["R"], TT * i, TT), ost[:, :, :], sE,
            accum_op=ALU.add)
    w8(pg, "gpsimd", sE, sE.n)


def ph_E1(ctx, pg):
    nc, dr, cst, sb = ctx.nc, ctx.dr, ctx.cst, ctx.sb
    TT, NT = 512, 16
    for k in range(10):
        cnv_pieces(ctx, pg, 0, k)
    Rt = [sb(f"e1_r{b}", (128, 8, TT)) for b in range(2)]
    x1b = sb("e1_x1b", (128, 8, TT), BF16)
    sqb = sb("e1_sqb", (128, 8, TT), BF16)
    xLs = sb("e1_xls", (128, 8, TT))
    muR = sb("e1_mu", (1, TT))
    muR2 = sb("e1_mu2", (1, TT))
    veR = sb("e1_ve", (1, TT))
    veE = sb("e1_vee", (1, TT))
    yR = sb("e1_y", (1, TT))
    tR = sb("e1_t", (1, TT))
    rowYb = sb("e1_yb", (1, TT), BF16)
    nmRb = sb("e1_nmb", (1, TT), BF16)
    sR = [ctx.dsemf(f"e1r{b}") for b in range(2)]
    sS1 = ctx.dsemf("e1s1")
    sSL = ctx.dsemf("e1sl")
    tPE, tDV, tAC = ctx.tile["tensor"], ctx.tile["vector"], ctx.tile["scalar"]
    pe_base = tPE.n

    dv_base = tDV.n
    ac_base = tAC.n

    def load(i):
        if i >= NT:
            return
        b = i % 2
        w8(pg, "sync", tDV, dv_base + max(0, i - 1) * 3)
        w8(pg, "sync", tAC, ac_base + max(0, i - 1) * 2)
        dma(pg, "sync", Rt[b][:, :, :], fm_slice(dr["R"], TT * i, TT), sR[b])

    load(0)
    for i in range(NT):
        b = i % 2
        load(i + 1)
        nld = 16 * (i // 2 + 1)
        w8(pg, "scalar", sR[b], nld)
        w8(pg, "scalar", tPE, pe_base + max(0, i))
        pg.em("scalar", lambda eng, b=b: eng.activation(
            x1b.ap().rearrange("p c s -> p (c s)"),
            Rt[b].ap().rearrange("p c s -> p (c s)"), AF.Copy))
        pg.em("scalar", lambda eng, b=b: eng.activation(
            sqb.ap().rearrange("p c s -> p (c s)"),
            Rt[b].ap().rearrange("p c s -> p (c s)"), AF.Square))
        ac1 = tinc(ctx, pg, "scalar")
        dma(pg, "scalar", fm_slice(dr["R1"], TT * i, TT), Rt[b][:, :, :], sS1)
        w8(pg, "tensor", tAC, ac1)
        pe_group(ctx, pg, 0, [
            (lambda eng, c=c: nc.tensor.matmul(
                ctx.ps[0][0:1, :], cst.onesColB[:, :], x1b[:, c, :],
                start=(c == 0), stop=(c == 7))) for c in range(8)])
        pe_group(ctx, pg, 1, [
            (lambda eng, c=c: nc.tensor.matmul(
                ctx.ps[1][0:1, :], cst.onesColB[:, :], sqb[:, c, :],
                start=(c == 0), stop=(c == 7))) for c in range(8)])

        def rows0(eng):
            eng.tensor_scalar(muR[:, :], ctx.ps[0][0:1, :], 1.0 / D, None, ALU.mult)
            return eng.tensor_scalar(muR2[:, :], ctx.ps[0][0:1, :], 1.0 / D,
                                     None, ALU.mult)

        def rows1(eng):
            eng.tensor_scalar(veR[:, :], ctx.ps[1][0:1, :], 1.0 / D, None, ALU.mult)
            eng.tensor_mul(tR[:, :], muR[:, :], muR2[:, :])
            eng.tensor_sub(veR[:, :], veR[:, :], tR[:, :])
            return eng.tensor_scalar(veE[:, :], veR[:, :], LN_EPS, None, ALU.add)
        consume(ctx, pg, "vector", 0, [rows0])
        consume(ctx, pg, "vector", 1, [rows1])
        dv1 = tinc(ctx, pg, "vector")
        w8(pg, "scalar", tDV, dv1)
        pg.em("scalar", lambda eng: eng.activation(
            yR[:, :], veR[:, :], AF.Ln, bias=cst.epsCol[0:1, :]))
        pg.em("scalar", lambda eng: eng.activation(
            yR[:, :], yR[:, :], AF.Exp, scale=-0.5))
        ac2 = tinc(ctx, pg, "scalar")
        w8(pg, "vector", tAC, ac2)
        rsqrt_nr(pg, yR[:, :], veE[:, :], tR[:, :])

        def rows2(eng):
            eng.tensor_mul(tR[:, :], muR[:, :], yR[:, :])
            eng.tensor_scalar(tR[:, :], tR[:, :], -1.0, None, ALU.mult)
            eng.tensor_copy(rowYb[:, :], yR[:, :])
            return eng.tensor_copy(nmRb[:, :], tR[:, :])
        pg.em("vector", rows2)
        dv2 = tinc(ctx, pg, "vector")
        w8(pg, "tensor", tDV, dv2)
        pe_group(ctx, pg, 2, [lambda eng: nc.tensor.matmul(
            ctx.ps[2][:, :], cst.onesRowB[:, :], rowYb[:, :],
            start=True, stop=True)])
        pe_group(ctx, pg, 3, [lambda eng: nc.tensor.matmul(
            ctx.ps[3][:, :], cst.onesRowB[:, :], nmRb[:, :],
            start=True, stop=True)])
        w8(pg, "vector", sSL, 16 * i)
        fns = []
        for c in range(8):
            fns.append(lambda eng, c=c, b=b: eng.tensor_tensor(
                xLs[:, c, :], Rt[b][:, c, :], ctx.ps[2][:, :], ALU.mult))
        consume(ctx, pg, "vector", 2, fns)
        fns2 = []
        for c in range(8):
            fns2.append(lambda eng, c=c: eng.tensor_tensor(
                xLs[:, c, :], xLs[:, c, :], ctx.ps[3][:, :], ALU.add))
        consume(ctx, pg, "vector", 3, fns2)
        dv3 = tinc(ctx, pg, "vector")
        tinc(ctx, pg, "tensor")
        w8(pg, "sync", tDV, dv3)
        w8(pg, "sync", sS1, sS1.n)
        dma(pg, "sync", fm_slice(dr["R"], TT * i, TT), xLs[:, :, :], sSL)
    w8(pg, "sync", sSL, sSL.n)
    w8(pg, "scalar", sS1, sS1.n)


def ph_ATTN1(ctx, pg, layer=0):
    nc, dr, cst, sb = ctx.nc, ctx.dr, ctx.cst, ctx.sb
    TT, NT = 512, 16
    Wq3 = sb("t1_wq3", (128, 8, 3 * D), BF16)
    Mqb = sb("t1_mq", (16, D), BF16)
    Mkb = sb("t1_mk", (16, D), BF16)
    xsb = [sb(f"t1_xs{b}", (128, 8, TT + 1), BF16) for b in range(2)]
    tabs = [[sb(f"t1_tb{b}_{j}", (128, TT), BF16) for j in range(4)]
            for b in range(2)]
    qraw = sb("t1_qr", (128, 8, TT), BF16)
    kraw = sb("t1_kr", (128, 8, TT), BF16)
    sq2 = sb("t1_sq2", (128, 8, TT), BF16)
    ssm = sb("t1_ssm", (16, 2 * TT))
    yss = sb("t1_yss", (16, 2 * TT))
    tss = sb("t1_tss", (16, 2 * TT))
    rq8 = sb("t1_rq8", (16, TT), BF16)
    rk1 = sb("t1_rk1", (16, TT), BF16)
    tq = sb("t1_tq", (128, 8, TT), BF16)
    tk = sb("t1_tk", (128, 8, TT), BF16)
    tqp = sb("t1_tqp", (128, 8, TT), BF16)
    tkp = sb("t1_tkp", (128, 8, TT), BF16)
    avs = sb("t1_avs", (128, 4, HEADS, DH + 1), BF16)
    sW = ctx.dsemf("t1W")
    sX = [ctx.dsemf(f"t1x{b}") for b in range(2)]
    sTb = [ctx.dsemf(f"t1t{b}") for b in range(2)]
    sRot = ctx.dsemf("t1rot")
    sSt = ctx.dsemf("t1st")
    sX0 = [s.n for s in sX]
    sTb0 = [s.n for s in sTb]
    sSt0 = sSt.n
    tPE, tDV, tAC = ctx.tile["tensor"], ctx.tile["vector"], ctx.tile["scalar"]
    nb = bank_cycler()
    sWq = ctx.dsemf("t1Wq")
    w8(pg, "sync", ctx.dsemf("cnv"), ctx.dsemf("cnv").n)
    dma(pg, "sync", Wq3[:, :, :],
        dr["Wqkvb"].ap().rearrange("(c p) n -> p c n", p=128), sWq)
    dma(pg, "sync", ctx.Wo_l[:, :, :],
        dr["Wob"].ap().rearrange("(c p) n -> p c n", p=128), sWq)
    dma(pg, "gpsimd", Mqb[:, :], dr[f"Mq{layer}"].ap(), sW)
    dma(pg, "gpsimd", Mkb[:, :], dr[f"Mk{layer}"].ap(), sW)
    w8(pg, "tensor", sW, sW.n)
    w8(pg, "tensor", sWq, sWq.n)
    dvends = []
    pe_mark = {}

    def load(i):
        if i >= NT:
            return
        b = i % 2
        if i >= 2:
            w8(pg, "gpsimd", *pe_mark[i - 2])
            w8(pg, "gpsimd", tDV, dvends[i - 2])
        s0 = TT * i
        if i % 8 == 0:
            dma(pg, "gpsimd", xsb[b][:, :, 1:TT + 1],
                fm_slice(dr["R"], s0, TT), sX[b])
        else:
            dma(pg, "gpsimd", xsb[b][:, :, :],
                fm_slice(dr["R"], s0 - 1, TT + 1), sX[b])
        c0 = s0 % SEQ
        for j, tn in enumerate(("rotCQ", "rotSQ", "rotCK", "rotSK")):
            dma(pg, "gpsimd", tabs[b][j][:, :],
                dr[tn].ap()[:, c0:c0 + TT], sTb[b])

    pg.em("vector", lambda eng: eng.memset(avs[:, :, :, DH:DH + 1], 1.0))
    load(0)
    for i in range(NT):
        b = i % 2
        load(i + 1)
        if i % 8 == 0:
            pg.em("vector", lambda eng, b=b: eng.memset(xsb[b][:, :, 0:1], 0.0))
            dvm = tincL(ctx, pg, "vector")
            w8(pg, "tensor", tDV, dvm)
        w8(pg, "tensor", sX[b], sX0[b] + 16 * (i // 2 + 1))
        for m in range(16):
            bk = nb()
            pe_group(ctx, pg, bk, [
                (lambda eng, kc=kc, m=m, bk=bk, b=b: nc.tensor.matmul(
                    ctx.ps[bk][:, :],
                    Wq3[:, kc, 128 * m:128 * (m + 1)],
                    xsb[b][:, kc, 1:TT + 1] if kc < 4 else xsb[b][:, kc, 0:TT],
                    start=(kc == 0), stop=(kc == 7)))
                for kc in range(8)])
            tgt, mm = (qraw, m) if m < 8 else (kraw, m - 8)
            e = "vector" if m % 2 == 0 else "scalar"
            fn = ((lambda eng, tgt=tgt, mm=mm, bk=bk: eng.tensor_copy(
                tgt[:, mm, :], ctx.ps[bk][:, :])) if e == "vector" else
                (lambda eng, tgt=tgt, mm=mm, bk=bk: eng.copy(
                    tgt[:, mm, :], ctx.ps[bk][:, :])))
            consume(ctx, pg, e, bk, [fn])
            if e == "vector":
                qkv_dv = (ctx.evS[bk], ctx.evS[bk].n)
        w8(pg, "scalar", *qkv_dv)
        pg.em("scalar", lambda eng: eng.activation(
            sq2.ap().rearrange("p c s -> p (c s)"),
            qraw.ap().rearrange("p c s -> p (c s)"), AF.Square))
        ac1 = tinc(ctx, pg, "scalar")
        w8(pg, "tensor", tAC, ac1)
        bq = nb()
        pe_group(ctx, pg, bq, [
            (lambda eng, c=c, bq=bq: nc.tensor.matmul(
                ctx.ps[bq][0:16, :], cst.HselB[:, c, :], sq2[:, c, :],
                start=(c == 0), stop=(c == 7))) for c in range(8)])
        w8(pg, "scalar", ctx.mmS[bq], ctx.fills[bq])
        pg.em("scalar", lambda eng: eng.activation(
            sq2.ap().rearrange("p c s -> p (c s)"),
            kraw.ap().rearrange("p c s -> p (c s)"), AF.Square))
        ac2 = tinc(ctx, pg, "scalar")
        w8(pg, "tensor", tAC, ac2)
        bK = nb()
        pe_group(ctx, pg, bK, [
            (lambda eng, c=c, bK=bK: nc.tensor.matmul(
                ctx.ps[bK][0:16, :], cst.HselB[:, c, :], sq2[:, c, :],
                start=(c == 0), stop=(c == 7))) for c in range(8)])
        consume(ctx, pg, "vector", bq, [
            lambda eng, bq=bq: eng.tensor_scalar(
                ssm[:, 0:TT], ctx.ps[bq][0:16, :], 1e-24, None, ALU.max)])
        consume(ctx, pg, "vector", bK, [
            lambda eng, bK=bK: eng.tensor_scalar(
                ssm[:, TT:], ctx.ps[bK][0:16, :], 1e-24, None, ALU.max)])
        ssm_done = (ctx.evS[bK], ctx.evS[bK].n)
        w8(pg, "scalar", *ssm_done)
        pg.em("scalar", lambda eng: eng.activation(yss[:, :], ssm[:, :], AF.Ln))
        pg.em("scalar", lambda eng: eng.activation(
            yss[:, :], yss[:, :], AF.Exp, scale=-0.5))
        ac3 = tinc(ctx, pg, "scalar")
        w8(pg, "vector", tAC, ac3)
        rsqrt_nr(pg, yss[:, :], ssm[:, :], tss[:, :])

        def facs(eng):
            eng.tensor_scalar(rq8[:, :], yss[:, 0:TT], 8.0, None, ALU.mult)
            return eng.tensor_copy(rk1[:, :], yss[:, TT:])
        pg.em("vector", facs)
        dv3 = tincL(ctx, pg, "vector")
        w8(pg, "vector", sSt, sSt0 + 48 * i)
        w8(pg, "tensor", tDV, dv3)
        for (Mb, rb, raw, tdst) in ((Mqb, rq8, qraw, tq), (Mkb, rk1, kraw, tk)):
            for c in range(8):
                bk = nb()
                pe_group(ctx, pg, bk, [
                    lambda eng, Mb=Mb, rb=rb, c=c, bk=bk: nc.tensor.matmul(
                        ctx.ps[bk][:, :], Mb[:, 128 * c:128 * (c + 1)], rb[:, :],
                        start=True, stop=True)])
                consume(ctx, pg, "vector", bk, [
                    lambda eng, raw=raw, tdst=tdst, c=c, bk=bk: eng.tensor_tensor(
                        tdst[:, c, :], raw[:, c, :], ctx.ps[bk][:, :], ALU.mult)])
                bc_done = (ctx.evS[bk], ctx.evS[bk].n)
        w8(pg, "gpsimd", *bc_done)
        for (src_t, dst_t) in ((tq, tqp), (tk, tkp)):
            for (d0, s0_) in ((0, 32), (32, 0), (64, 96), (96, 64)):
                dma(pg, "gpsimd", dst_t[d0:d0 + 32, :, :],
                    src_t[s0_:s0_ + 32, :, :], sRot)
        w8(pg, "vector", sRot, sRot.n)
        w8(pg, "vector", sTb[b], sTb0[b] + 64 * (i // 2 + 1))

        def rot(eng, b=b):
            CQ = tabs[b][0].ap().unsqueeze(1).broadcast_to([128, 8, TT])
            SQ = tabs[b][1].ap().unsqueeze(1).broadcast_to([128, 8, TT])
            CK = tabs[b][2].ap().unsqueeze(1).broadcast_to([128, 8, TT])
            SK = tabs[b][3].ap().unsqueeze(1).broadcast_to([128, 8, TT])
            eng.tensor_tensor(tq[:, :, :], tq[:, :, :], CQ, ALU.mult)
            eng.tensor_tensor(tqp[:, :, :], tqp[:, :, :], SQ, ALU.mult)
            eng.tensor_tensor(tq[:, :, :], tq[:, :, :], tqp[:, :, :], ALU.add)
            eng.tensor_tensor(tk[:, :, :], tk[:, :, :], CK, ALU.mult)
            eng.tensor_tensor(tkp[:, :, :], tkp[:, :, :], SK, ALU.mult)
            return eng.tensor_tensor(tk[:, :, :], tk[:, :, :], tkp[:, :, :],
                                     ALU.add)
        pg.em("vector", rot)
        dv5 = tincL(ctx, pg, "vector")
        for u in range(4):
            for fh in range(2):
                bk = nb()
                pe_group(ctx, pg, bk, [
                    (lambda eng, u=u, fh=fh, kc=kc, bk=bk, b=b:
                     nc.tensor.matmul(
                        ctx.ps[bk][:, :],
                        xsb[b][:, kc, 1 + 128 * u:1 + 128 * (u + 1)]
                        if kc < 4 else xsb[b][:, kc, 128 * u:128 * (u + 1)],
                        Wq3[:, kc, 2048 + 512 * fh:2048 + 512 * (fh + 1)],
                        start=(kc == 0), stop=(kc == 7)))
                    for kc in range(8)])
                consume(ctx, pg, "vector", bk, [
                    lambda eng, u=u, fh=fh, bk=bk: eng.tensor_copy(
                        avs[:, u, 8 * fh:8 * (fh + 1), 0:DH],
                        ctx.ps[bk].ap().rearrange("p (h e) -> p h e", h=8))])
                avs_done = (ctx.evS[bk], ctx.evS[bk].n)
                pe_mark[i] = (ctx.mmS[bk], ctx.mmS[bk].n)
        w8(pg, "sync", *avs_done)
        w8(pg, "sync", tDV, dv5)
        s0 = TT * i
        dma(pg, "sync", fm_slice(dr["aqnS"], s0, TT), tq[:, :, :], sSt)
        dma(pg, "sync", fm_slice(dr["aknS"], s0, TT), tk[:, :, :], sSt)
        dma(pg, "sync", dr["avtS"].ap()[4 * i:4 * (i + 1), :, :, :].transpose(
            [1, 0, 2, 3]), avs[:, :, :, :], sSt)
        dvends.append(tDV.n)
    w8(pg, "sync", sSt, sSt.n)
    w8(pg, "gpsimd", sRot, sRot.n)
    tinc(ctx, pg, "vector")
    tinc(ctx, pg, "scalar")
    tinc(ctx, pg, "tensor")


def ph_ATTN2(ctx, pg, layer=0):
    nc, dr, cst, sb = ctx.nc, ctx.dr, ctx.cst, ctx.sb
    aqn = [sb(f"t2_q{b}", (128, 8, WIN), BF16) for b in range(2)]
    akn = [sb(f"t2_k{b}", (128, 8, 2 * WIN), BF16) for b in range(2)]
    avt = [sb(f"t2_v{b}", (128, 4, HEADS, DH + 1), BF16) for b in range(2)]
    Pb2 = [sb(f"t2_p{b}", (128, 2, 512), BF16) for b in range(2)]
    AOr = sb("t2_aor", (128, 8, WIN), BF16)
    AOn = sb("t2_aon", (128, 8, WIN), BF16)
    LmatR = sb("t2_l", (1, HEADS * WIN))
    rcpR = sb("t2_rcf", (1, HEADS * WIN))
    rcpRb = sb("t2_rcb", (1, HEADS * WIN), BF16)
    rcp16 = sb("t2_rcp", (HEADS, WIN), BF16)
    ost = sb("t2_ost", (128, 8, WIN))
    sQ = [ctx.dsemf(f"t2q{b}") for b in range(2)]
    sK = [ctx.dsemf(f"t2k{b}") for b in range(2)]
    sV = [ctx.dsemf(f"t2v{b}") for b in range(2)]
    sO = ctx.dsemf("t2o")
    sLr = ctx.dsemf("t2lr")
    sW1 = ctx.dsemf("f1W")
    sQ0 = [s.n for s in sQ]
    sK0 = [s.n for s in sK]
    sV0 = [s.n for s in sV]
    sO0 = sO.n
    tPE, tDV, tAC = ctx.tile["tensor"], ctx.tile["vector"], ctx.tile["scalar"]
    nb = bank_cycler()
    NWT = 2 * NWIN
    pe_mark = {}

    def load(i):
        if i >= NWT:
            return
        b = i % 2
        seq, w = divmod(i, NWIN)
        s0 = seq * SEQ + w * WIN
        if i >= 2:
            w8(pg, "sync", *pe_mark[i - 2])
        dma(pg, "sync", aqn[b][:, :, :], fm_slice(dr["aqnS"], s0, WIN), sQ[b])
        if w == 0:
            dma(pg, "sync", akn[b][:, :, 0:WIN],
                fm_slice(dr["aknS"], s0, WIN), sK[b])
            dma(pg, "sync", avt[b][:, 0:2, :, :],
                dr["avtS"].ap()[2 * i:2 * i + 2].transpose([1, 0, 2, 3]), sV[b])
        else:
            dma(pg, "sync", akn[b][:, :, :],
                fm_slice(dr["aknS"], s0 - WIN, 2 * WIN), sK[b])
            dma(pg, "sync", avt[b][:, :, :, :],
                dr["avtS"].ap()[2 * i - 2:2 * i + 2].transpose([1, 0, 2, 3]),
                sV[b])

    load(0)
    for i in range(NWT):
        b = i % 2
        seq, w = divmod(i, NWIN)
        w0 = (w == 0)
        load(i + 1)
        nq = 16 * (i // 2 + 1)
        w8(pg, "tensor", sQ[b], sQ0[b] + nq)
        w8(pg, "tensor", sK[b], sK0[b] + nq)
        w8(pg, "tensor", sV[b], sV0[b] + nq)
        scs = (0, 1) if w0 else (0, 1, 2, 3)
        gs = (0,) if w0 else (0, 1)
        w8(pg, "vector", sLr, sLr.n)
        pmin = {}
        for h in range(HEADS + 1):
            if h < HEADS:
                hp = h % 2
                pb = 64 * (h % 2)
                hc = h // 2
                exp_mark = None
                for g in gs:
                    bk = nb()
                    pe_group(ctx, pg, bk, [
                        (lambda eng, sc=2 * g + d, d=d, bk=bk, hc=hc, pb=pb,
                                b=b: nc.tensor.matmul(
                            ctx.ps[bk][:, 256 * d:256 * (d + 1)],
                            akn[b][pb:pb + 64, hc, 128 * sc:128 * (sc + 1)],
                            aqn[b][pb:pb + 64, hc, :],
                            start=(d == 0), stop=(d == 1)))
                        for d in range(2)])
                    consume(ctx, pg, "scalar", bk, [
                        lambda eng, g=g, bk=bk, hp=hp: eng.activation(
                            Pb2[hp][:, g, :], ctx.ps[bk][:, :], AF.Exp)])
                    exp_mark = (ctx.evS[bk], ctx.evS[bk].n)
                w8(pg, "vector", *exp_mark)
                for g in gs:
                    mask = cst.maskTBb if (w0 or g == 1) else cst.maskTAb
                    pg.em("vector", lambda eng, g=g, hp=hp, mask=mask:
                          eng.tensor_tensor(Pb2[hp][:, g, :], Pb2[hp][:, g, :],
                                            mask[:, :], ALU.min))
                pmin[h] = tincL(ctx, pg, "vector")
            if h >= 1:
                hq = h - 1
                hqp = hq % 2
                pbq = 64 * (hq % 2)
                hcq = hq // 2
                w8(pg, "tensor", tDV, pmin[hq])
                bkv = nb()
                pe_group(ctx, pg, bkv, [
                    (lambda eng, sc=sc, z=z, bkv=bkv, b=b, hq=hq, hqp=hqp:
                     nc.tensor.matmul(
                        ctx.ps[bkv][0:DH + 1, 0:WIN],
                        avt[b][:, sc, hq, :],
                        Pb2[hqp][:, sc // 2, 256 * (sc % 2):256 * (sc % 2) + 256],
                        start=(z == 0), stop=(z == len(scs) - 1)))
                    for z, sc in enumerate(scs)])
                consume(ctx, pg, "scalar", bkv, [
                    (lambda eng, pbq=pbq, hcq=hcq, bkv=bkv: eng.copy(
                        AOr[pbq:pbq + 64, hcq, :], ctx.ps[bkv][0:DH, 0:WIN])),
                    (lambda eng, hq=hq, bkv=bkv: eng.copy(
                        LmatR[0:1, WIN * hq:WIN * (hq + 1)],
                        ctx.ps[bkv][DH:DH + 1, 0:WIN])),
                ])
                w8(pg, "vector", ctx.evS[bkv], ctx.evS[bkv].n)
                pg.em("vector", lambda eng, hq=hq: eng.reciprocal(
                    rcpR[0:1, WIN * hq:WIN * (hq + 1)],
                    LmatR[0:1, WIN * hq:WIN * (hq + 1)]))
                pg.em("vector", lambda eng, hq=hq: eng.tensor_copy(
                    rcpRb[0:1, WIN * hq:WIN * (hq + 1)],
                    rcpR[0:1, WIN * hq:WIN * (hq + 1)]))
                rcv = tincL(ctx, pg, "vector")
                w8(pg, "gpsimd", tDV, rcv)
                dma(pg, "gpsimd", rcp16[hq:hq + 1, :],
                    rcpRb[0:1, WIN * hq:WIN * (hq + 1)], sLr)
        w8(pg, "tensor", sLr, sLr.n)
        aon_mark = None
        for c in range(8):
            bk = nb()
            pe_group(ctx, pg, bk, [
                lambda eng, c=c, bk=bk: nc.tensor.matmul(
                    ctx.ps[bk][:, 0:WIN], cst.HselT[:, c, :], rcp16[:, :],
                    start=True, stop=True)])
            consume(ctx, pg, "vector", bk, [
                lambda eng, c=c, bk=bk: eng.tensor_tensor(
                    AOn[:, c, :], AOr[:, c, :], ctx.ps[bk][:, 0:WIN],
                    ALU.mult)])
            aon_mark = (ctx.evS[bk], ctx.evS[bk].n)
        w8(pg, "tensor", *aon_mark)
        w8(pg, "vector", sO, sO0 + 16 * i)
        ost_done = None
        for m in range(8):
            bk = nb()
            pe_group(ctx, pg, bk, [
                (lambda eng, kc=kc, m=m, bk=bk: nc.tensor.matmul(
                    ctx.ps[bk][:, 0:WIN], ctx.Wo_l[:, kc, 128 * m:128 * (m + 1)],
                    AOn[:, kc, :], start=(kc == 0), stop=(kc == 7)))
                for kc in range(8)])
            consume(ctx, pg, "vector", bk, [
                lambda eng, m=m, bk=bk: eng.tensor_copy(ost[:, m, :],
                                                        ctx.ps[bk][:, 0:WIN])])
            ost_done = (ctx.evS[bk], ctx.evS[bk].n)
            if m == 7:
                pe_mark[i] = (ctx.mmS[bk], ctx.mmS[bk].n)
        w8(pg, "gpsimd", *ost_done)
        s0 = seq * SEQ + w * WIN
        dma(pg, "gpsimd", fm_slice(dr["R"], s0, WIN), ost[:, :, :], sO,
            accum_op=ALU.add)
        if i < 12:
            Q = D // 4
            if i < 4:
                dma(pg, "gpsimd", dr["W1ab"].ap()[Q * i:Q * (i + 1), :],
                    dr[f"W1a{layer}"].ap()[Q * i:Q * (i + 1), :], sW1)
            elif i < 8:
                k = i - 4
                dma(pg, "gpsimd", dr["W1gb"].ap()[Q * k:Q * (k + 1), :],
                    dr[f"W1g{layer}"].ap()[Q * k:Q * (k + 1), :], sW1)
            else:
                k = i - 8
                Q2 = DI // 4
                dma(pg, "gpsimd", dr["W2pb"].ap()[Q2 * k:Q2 * (k + 1), :],
                    dr[f"W2p{layer}"].ap()[Q2 * k:Q2 * (k + 1), :], sW1)
    w8(pg, "gpsimd", sO, sO.n)
    tinc(ctx, pg, "vector")
    tinc(ctx, pg, "scalar")
    tinc(ctx, pg, "tensor")


def ph_FFN0(ctx, pg, layer=0):
    nc, dr, cst, sb = ctx.nc, ctx.dr, ctx.cst, ctx.sb
    TT, NT = 512, 16
    sWL = ctx.dsemf("f1WL")
    w8(pg, "sync", ctx.dsemf("f1W"), ctx.dsemf("f1W").n)
    dma(pg, "sync", ctx.W1a_l[:, :, :],
        dr["W1ab"].ap().rearrange("(c p) n -> p c n", p=128), sWL)
    dma(pg, "sync", ctx.W1g_l[:, :, :],
        dr["W1gb"].ap().rearrange("(c p) n -> p c n", p=128), sWL)
    dma(pg, "sync", ctx.W2p_l[:, :, :],
        dr["W2pb"].ap().rearrange("(j p) n -> p j n", p=128), sWL)

    xsb = [sb(f"f0_xs{b}", (128, 8, TT + 1), BF16) for b in range(2)]
    sqb = sb("f0_sq", (128, 8, TT + 1), BF16)
    muR = sb("f0_mu", (1, TT))
    muR2 = sb("f0_mu2", (1, TT))
    veR = sb("f0_ve", (1, TT))
    veE = sb("f0_vee", (1, TT))
    yR = sb("f0_y", (1, TT))
    tR = sb("f0_t", (1, TT))
    arL = sb("f0_ar", (1, TT), BF16)
    amL = sb("f0_am", (1, TT), BF16)
    sX = [ctx.dsemf(f"f0x{b}") for b in range(2)]
    sRw = ctx.dsemf("f0rw")
    sX0 = [s.n for s in sX]
    sRw0 = sRw.n
    tPE, tDV, tAC = ctx.tile["tensor"], ctx.tile["vector"], ctx.tile["scalar"]
    pe_base = tPE.n

    def load(i):
        if i >= NT:
            return
        b = i % 2
        w8(pg, "gpsimd", tPE, pe_base + max(0, i - 1))
        s0 = TT * i
        if i % 8 == 0:
            dma(pg, "gpsimd", xsb[b][:, :, 1:TT + 1],
                fm_slice(dr["R"], s0, TT), sX[b])
        else:
            dma(pg, "gpsimd", xsb[b][:, :, :],
                fm_slice(dr["R"], s0 - 1, TT + 1), sX[b])

    load(0)
    for i in range(NT):
        b = i % 2
        load(i + 1)
        if i % 8 == 0:
            pg.em("vector", lambda eng, b=b: eng.memset(xsb[b][:, :, 0:1], 0.0))
            dvm = tinc(ctx, pg, "vector")
            w8(pg, "scalar", tDV, dvm)
        w8(pg, "scalar", sX[b], sX0[b] + 16 * (i // 2 + 1))
        w8(pg, "scalar", tPE, pe_base + max(0, i))
        pg.em("scalar", lambda eng, b=b: eng.activation(
            sqb.ap().rearrange("p c s -> p (c s)"),
            xsb[b].ap().rearrange("p c s -> p (c s)"), AF.Square))
        ac1 = tinc(ctx, pg, "scalar")
        w8(pg, "tensor", tAC, ac1)
        pe_group(ctx, pg, 0, [
            (lambda eng, c=c, b=b: nc.tensor.matmul(
                ctx.ps[0][0:1, :], cst.onesColB[:, :],
                xsb[b][:, c, 1:TT + 1] if c < 4 else xsb[b][:, c, 0:TT],
                start=(c == 0), stop=(c == 7))) for c in range(8)])
        pe_group(ctx, pg, 1, [
            (lambda eng, c=c: nc.tensor.matmul(
                ctx.ps[1][0:1, :], cst.onesColB[:, :],
                sqb[:, c, 1:TT + 1] if c < 4 else sqb[:, c, 0:TT],
                start=(c == 0), stop=(c == 7))) for c in range(8)])

        def rows0(eng):
            eng.tensor_scalar(muR[:, :], ctx.ps[0][0:1, :], 1.0 / D, None,
                              ALU.mult)
            return eng.tensor_scalar(muR2[:, :], ctx.ps[0][0:1, :], 1.0 / D,
                                     None, ALU.mult)

        def rows1(eng):
            eng.tensor_scalar(veR[:, :], ctx.ps[1][0:1, :], 1.0 / D, None,
                              ALU.mult)
            eng.tensor_mul(tR[:, :], muR[:, :], muR2[:, :])
            eng.tensor_sub(veR[:, :], veR[:, :], tR[:, :])
            return eng.tensor_scalar(veE[:, :], veR[:, :], LN_EPS, None, ALU.add)
        consume(ctx, pg, "vector", 0, [rows0])
        consume(ctx, pg, "vector", 1, [rows1])
        dv1 = tinc(ctx, pg, "vector")
        w8(pg, "scalar", tDV, dv1)
        pg.em("scalar", lambda eng: eng.activation(
            yR[:, :], veR[:, :], AF.Ln, bias=cst.epsCol[0:1, :]))
        pg.em("scalar", lambda eng: eng.activation(
            yR[:, :], yR[:, :], AF.Exp, scale=-0.5))
        ac2 = tinc(ctx, pg, "scalar")
        w8(pg, "vector", tAC, ac2)
        rsqrt_nr(pg, yR[:, :], veE[:, :], tR[:, :])
        w8(pg, "vector", sRw, sRw0 + 32 * i)

        def rows2(eng):
            eng.tensor_mul(tR[:, :], muR[:, :], yR[:, :])
            eng.tensor_scalar(amL[:, :], tR[:, :], -1.0, None, ALU.mult)
            return eng.tensor_copy(arL[:, :], yR[:, :])
        pg.em("vector", rows2)
        dv2 = tinc(ctx, pg, "vector")
        tinc(ctx, pg, "tensor")
        w8(pg, "sync", tDV, dv2)
        s0 = TT * i
        dma(pg, "sync", dr["rows"].ap()[0:1, s0:s0 + TT], arL[:, :], sRw)
        dma(pg, "sync", dr["rows"].ap()[1:2, s0:s0 + TT], amL[:, :], sRw)
    w8(pg, "sync", sRw, sRw.n)


def ph_FFN1(ctx, pg, layer=0):
    nc, dr, cst, sb = ctx.nc, ctx.dr, ctx.cst, ctx.sb
    TT, NT = 512, 16
    W1a, W1g, W2p = ctx.W1a_l, ctx.W1g_l, ctx.W2p_l
    b1a = sb("f1_b1a", (128, NJ))
    b1g = sb("f1_b1g", (128, NJ))
    b2c = sb("f1_b2", (128, 8))
    lng = sb("f1_g", (128, 8))
    lnb = sb("f1_b", (128, 8))
    rwr = [sb(f"f1_rr{b}", (1, TT), BF16) for b in range(2)]
    rwm = [sb(f"f1_rm{b}", (1, TT), BF16) for b in range(2)]
    xsb = sb("f1_xs", (128, 8, TT + 1), BF16)
    xn = sb("f1_xn", (128, 8, TT), BF16)
    gelG = sb("f1_gg", (128, TT), BF16)
    u = sb("f1_u", (128, NJ, TT), BF16)
    ost = sb("f1_ost", (128, 8, TT // 2))
    sWs = ctx.dsemf("f1Ws")
    sX = ctx.dsemf("f1x")
    sRw = [ctx.dsemf(f"f1rw{b}") for b in range(2)]
    sO = ctx.dsemf("f1o")
    sX0 = sX.n
    sRw0 = [s.n for s in sRw]
    sO0 = sO.n
    tPE, tDV, tAC = ctx.tile["tensor"], ctx.tile["vector"], ctx.tile["scalar"]
    nb = bank_cycler()
    dma(pg, "sync", b1a[:, :], dr[f"b1a{layer}"].ap(), sWs)
    dma(pg, "sync", b1g[:, :], dr[f"b1g{layer}"].ap(), sWs)
    dma(pg, "sync", b2c[:, :], dr[f"b2c{layer}"].ap(), sWs)
    dma(pg, "sync", lng[:, :], dr[f"lng{layer}"].ap(), sWs)
    dma(pg, "sync", lnb[:, :], dr[f"lnb{layer}"].ap(), sWs)
    w8(pg, "tensor", ctx.dsemf("f1WL"), ctx.dsemf("f1WL").n)
    w8(pg, "tensor", sWs, sWs.n)
    pe_base = tPE.n
    dv2s = []

    def loadx(i):
        if i >= NT:
            return
        s0 = TT * i
        if i >= 1:
            w8(pg, "gpsimd", tDV, dv2s[i - 1])
        if i % 8 == 0:
            dma(pg, "gpsimd", xsb[:, :, 1:TT + 1],
                fm_slice(dr["R"], s0, TT), sX)
        else:
            dma(pg, "gpsimd", xsb[:, :, :],
                fm_slice(dr["R"], s0 - 1, TT + 1), sX)

    bc_mark = {}

    def loadrows(i):
        if i >= NT:
            return
        b = i % 2
        s0 = TT * i
        if i >= 2:
            w8(pg, "sync", *bc_mark[i - 2])
        dma(pg, "sync", rwr[b][:, :], dr["rows"].ap()[0:1, s0:s0 + TT], sRw[b])
        dma(pg, "sync", rwm[b][:, :], dr["rows"].ap()[1:2, s0:s0 + TT], sRw[b])

    loadx(0)
    loadrows(0)
    for i in range(NT):
        b = i % 2
        s0 = TT * i
        loadrows(i + 1)
        if i % 8 == 0:
            pg.em("vector", lambda eng: eng.memset(xsb[:, :, 0:1], 0.0))
        w8(pg, "tensor", sRw[b], sRw0[b] + 32 * (i // 2 + 1))
        bR, bM = nb(), nb()
        pe_group(ctx, pg, bR, [lambda eng, bR=bR, b=b: nc.tensor.matmul(
            ctx.ps[bR][:, :], cst.onesRowB[:, :], rwr[b][:, :],
            start=True, stop=True)])
        pe_group(ctx, pg, bM, [lambda eng, bM=bM, b=b: nc.tensor.matmul(
            ctx.ps[bM][:, :], cst.onesRowB[:, :], rwm[b][:, :],
            start=True, stop=True)])
        bc_mark[i] = (ctx.mmS[bM], ctx.mmS[bM].n)
        w8(pg, "vector", sX, sX0 + 16 * (i + 1))
        fnsR = []
        for c in range(8):
            off = 1 if c < 4 else 0
            fnsR.append(lambda eng, c=c, off=off, bR=bR: eng.tensor_tensor(
                xn[:, c, :], xsb[:, c, off:off + TT], ctx.ps[bR][:, :],
                ALU.mult))
        consume(ctx, pg, "vector", bR, fnsR)
        fnsM = []
        for c in range(8):
            fnsM.append(lambda eng, c=c, bM=bM: eng.tensor_tensor(
                xn[:, c, :], xn[:, c, :], ctx.ps[bM][:, :], ALU.add))
        consume(ctx, pg, "vector", bM, fnsM)
        for c in range(8):
            pg.em("vector", lambda eng, c=c: eng.tensor_scalar(
                xn[:, c, :], xn[:, c, :], lng[:, c:c + 1], lnb[:, c:c + 1],
                ALU.mult, ALU.add))
        dv2 = tincL(ctx, pg, "vector")
        dv2s.append(dv2)
        loadx(i + 1)
        w8(pg, "tensor", tDV, dv2)
        gel_free = None
        for j in range(NJ):
            bA, bG = nb(), nb()
            pe_group(ctx, pg, bA, [
                (lambda eng, kc=kc, j=j, bA=bA: nc.tensor.matmul(
                    ctx.ps[bA][:, :], W1a[:, kc, 128 * j:128 * (j + 1)],
                    xn[:, kc, :], start=(kc == 0), stop=(kc == 7)))
                for kc in range(8)])
            pe_group(ctx, pg, bG, [
                (lambda eng, kc=kc, j=j, bG=bG: nc.tensor.matmul(
                    ctx.ps[bG][:, :], W1g[:, kc, 128 * j:128 * (j + 1)],
                    xn[:, kc, :], start=(kc == 0), stop=(kc == 7)))
                for kc in range(8)])
            if gel_free is not None:
                w8(pg, "scalar", gel_free[0], gel_free[1])
            consume(ctx, pg, "scalar", bG, [
                lambda eng, j=j, bG=bG: eng.activation(
                    gelG[:, :], ctx.ps[bG][:, :], AF.Gelu,
                    bias=b1g[:, j:j + 1])])
            w8(pg, "vector", ctx.evS[bG], ctx.evS[bG].n)
            consume(ctx, pg, "vector", bA, [
                lambda eng, j=j, bA=bA: eng.scalar_tensor_tensor(
                    u[:, j, :], ctx.ps[bA][:, :], b1a[:, j:j + 1], gelG[:, :],
                    ALU.add, ALU.mult)])
            gel_free = (ctx.evS[bA], ctx.evS[bA].n)
        u_done = gel_free
        w8(pg, "tensor", *u_done)
        bks = [nb() for _ in range(8)]
        for m in range(8):
            pe_group(ctx, pg, bks[m], [
                (lambda eng, j=j, m=m, bk=bks[m]: nc.tensor.matmul(
                    ctx.ps[bk][:, :], W2p[:, j, 128 * m:128 * (m + 1)],
                    u[:, j, :], start=(j == 0), stop=(j == NJ - 1)))
                for j in range(NJ)])
        w8(pg, "vector", sO, sO0 + 32 * i)
        for m in range(8):
            w8(pg, "vector", ctx.mmS[bks[m]], ctx.fills[bks[m]])
            pg.em("vector", lambda eng, m=m, bk=bks[m]: eng.tensor_scalar(
                ost[:, m, :], ctx.ps[bk][:, 0:TT // 2], b2c[:, m:m + 1],
                None, ALU.add))
        dvA = tincL(ctx, pg, "vector")
        w8(pg, "gpsimd", tDV, dvA)
        if i + 1 < NT:
            w8(pg, "gpsimd", sX, sX0 + 16 * (i + 2))
        dma(pg, "gpsimd", fm_slice(dr["R"], s0, TT // 2), ost[:, :, :], sO,
            accum_op=ALU.add)
        w8(pg, "vector", sO, sO0 + 32 * i + 16)
        ostB_done = None
        for m in range(8):
            consume(ctx, pg, "vector", bks[m], [
                lambda eng, m=m, bk=bks[m]: eng.tensor_scalar(
                    ost[:, m, :], ctx.ps[bk][:, TT // 2:TT], b2c[:, m:m + 1],
                    None, ALU.add)])
            ostB_done = (ctx.evS[bks[m]], ctx.evS[bks[m]].n)
        w8(pg, "gpsimd", *ostB_done)
        dma(pg, "gpsimd", fm_slice(dr["R"], s0 + TT // 2, TT // 2),
            ost[:, :, :], sO, accum_op=ALU.add)
        if layer + 1 < NL and i < 10:
            cnv_pieces(ctx, pg, layer + 1, i)
    w8(pg, "gpsimd", sO, sO.n)
    tinc(ctx, pg, "vector")
    tinc(ctx, pg, "scalar")
    tinc(ctx, pg, "tensor")


def cnv_pieces(ctx, pg, layer, k):
    dr = ctx.dr
    sC = ctx.dsemf("cnv")
    if k < 8:
        Q = 3 * D // 8
        dma(pg, "gpsimd", dr["Wqkvb"].ap()[:, Q * k:Q * (k + 1)],
            dr[f"Wqkv{layer}"].ap()[:, Q * k:Q * (k + 1)], sC)
    else:
        H2 = D // 2
        j = k - 8
        dma(pg, "gpsimd", dr["Wob"].ap()[H2 * j:H2 * (j + 1), :],
            dr[f"Wo{layer}"].ap()[H2 * j:H2 * (j + 1), :], sC)


def ph_FIN(ctx, pg, y=None):
    nc, dr, cst, sb = ctx.nc, ctx.dr, ctx.cst, ctx.sb
    TT, NT = 512, 16
    rt = [sb(f"fn_r{b}", (128, 8, TT)) for b in range(2)]
    r1t = [sb(f"fn_r1{b}", (128, 8, TT)) for b in range(2)]
    sm = sb("fn_sm", (128, 8, TT))
    yt = sb("fn_yt", (128, 4, D))
    sR = [ctx.dsemf(f"fnr{b}") for b in range(2)]
    sY = ctx.dsemf("fny")
    tPE, tDV, tAC = ctx.tile["tensor"], ctx.tile["vector"], ctx.tile["scalar"]
    pe_base, dv_base = tPE.n, tDV.n
    nb = bank_cycler()

    def load(i):
        if i >= NT:
            return
        b = i % 2
        w8(pg, "sync", tDV, dv_base + max(0, i - 1))
        dma(pg, "sync", rt[b][:, :, :], fm_slice(dr["R"], TT * i, TT), sR[b])
        dma(pg, "sync", r1t[b][:, :, :], fm_slice(dr["R1"], TT * i, TT), sR[b])

    load(0)
    for i in range(NT):
        b = i % 2
        load(i + 1)
        w8(pg, "vector", sR[b], 32 * (i // 2 + 1))
        w8(pg, "vector", tPE, pe_base + max(0, i))
        pg.em("vector", lambda eng, b=b: eng.tensor_tensor(
            sm.ap().rearrange("p c s -> p (c s)"),
            rt[b].ap().rearrange("p c s -> p (c s)"),
            r1t[b].ap().rearrange("p c s -> p (c s)"), ALU.add))
        dv1 = tinc(ctx, pg, "vector")
        w8(pg, "tensor", tDV, dv1)
        w8(pg, "scalar", sY, 16 * i)
        for a in range(4):
            for half in range(2):
                bk = nb()
                pe_group(ctx, pg, bk, [
                    (lambda eng, a=a, c=4 * half + cc, cc=cc, bk=bk:
                     nc.tensor.matmul(
                        ctx.ps[bk][:, 128 * cc:128 * (cc + 1)],
                        sm[:, c, 128 * a:128 * (a + 1)],
                        cst.identF[:, :], start=(cc == 0), stop=(cc == 3)))
                    for cc in range(4)])
                consume(ctx, pg, "scalar", bk, [
                    lambda eng, a=a, half=half, bk=bk: eng.copy(
                        yt[:, a, 512 * half:512 * (half + 1)],
                        ctx.ps[bk][:, :])])
        tinc(ctx, pg, "tensor")
        ac1 = tinc(ctx, pg, "scalar")
        w8(pg, "sync", tAC, ac1)
        dma(pg, "sync",
            y.ap()[TT * i:TT * (i + 1), :].rearrange("(a p) e -> p a e", p=128),
            yt[:, :, :], sY)
    w8(pg, "sync", sY, sY.n)


def kernel(**inputs):
    shared, shards = _host_prep(inputs)
    nc = build_program()
    in_maps = []
    for c in range(NCORES):
        m = dict(shared)
        m["x0"] = shards[c]
        in_maps.append(m)
    res = run_bass_kernel_spmd(nc, in_maps, core_ids=list(range(NCORES)))
    out = np.concatenate([np.asarray(res.results[c]["y"]).reshape(2, SEQ, D)
                          for c in range(NCORES)], axis=0)
    return out.reshape(16, SEQ, D)
